# version 08
# baseline (speedup 1.0000x reference)
import math
from contextlib import ExitStack

import numpy as np
import ml_dtypes

import concourse.bass as bass
import concourse.mybir as mybir
from concourse.bass_utils import run_bass_kernel_spmd

F32 = mybir.dt.float32
BF16 = mybir.dt.bfloat16
I32 = mybir.dt.int32
AF = mybir.ActivationFunctionType
ALU = mybir.AluOpType
BF = ml_dtypes.bfloat16
AX = mybir.AxisListType.X

B, S, D = 4, 2048, 4096
NH, NKV, HD = 32, 8, 128
EPS = 1e-5
SM_SHIFT = 24.0
NCORES = 8
SCALE = 1.0 / math.sqrt(HD)

PAIR_SWAP = [1, 0, 3, 2, 5, 4, 7, 6, 9, 8, 11, 10, 13, 12, 15, 14,
             17, 16, 19, 18, 21, 20, 23, 22, 25, 24, 27, 26, 29, 28, 31, 30]


def _make_lut2acc():
    import concourse.dve_ops as _dve_ops
    from concourse.dve_op import DveOp as _DveOp
    from concourse.dve_spec import Spec as _Spec, lower as _lower
    from concourse.dve_spec import Src0, Src1, C0, C1, C2, Zero, One, eq as _eq

    a = _eq(Src0, C2) * C0
    b = _eq(Src0, C2 + One) * C1
    body = (Src1 + a) + b
    spec = _Spec(body=body)
    perf = {}
    shas = {}
    for ver in ("v3", "v4"):
        try:
            shas[ver] = _lower(spec, ver=ver, perf_en=True, name="LUT2ACC").sha(ver)
            perf[ver] = True
        except Exception:
            shas[ver] = _lower(spec, ver=ver, perf_en=False, name="LUT2ACC").sha(ver)
            perf[ver] = False
    op = _DveOp(
        "LUT2ACC", spec, subdim=False, uops_sha=shas, perf_en=perf,
        reference=lambda in0, in1, s0, s1, imm2:
            in1 + (in0 == imm2) * s0 + (in0 == (imm2 + 1.0)) * s1)
    if not any(o.name == "LUT2ACC" for o in _dve_ops.OPS):
        _dve_ops.OPS.append(op)
        _dve_ops._SUB_OPCODE_FOR_NAME[op.name] = (
            _dve_ops._CUSTOM_DVE_ROW_BASE + len(_dve_ops.OPS) - 1)
    return op


LUT2ACC = _make_lut2acc()


def _rope_l0():
    inv = 1.0 / 10000.0 ** (np.arange(0, HD, 2, dtype=np.float32) / HD)
    ang = np.arange(S, dtype=np.float32)[None, :] * inv[:, None]
    cos = np.zeros((HD, S), dtype=np.float32)
    sin = np.zeros((HD, S), dtype=np.float32)
    cos[0::2] = np.cos(ang)
    cos[1::2] = np.cos(ang)
    sin[0::2] = -np.sin(ang)
    sin[1::2] = np.sin(ang)
    return cos, sin


def _rope_l1():
    inv = 1.0 / 10000.0 ** (np.arange(0, HD, 2, dtype=np.float32) / HD)
    f = np.arange(S, dtype=np.float32)[None, :] * inv[:, None]
    cos = np.zeros((HD, S), dtype=np.float32)
    sin = np.zeros((HD, S), dtype=np.float32)
    cos[0::2] = np.cos(f)
    cos[1::2] = np.cos(f)
    sin[0::2] = np.sin(f)
    sin[1::2] = np.sin(f)
    return cos, sin


def _rotm_l1():
    P = np.zeros((128, 128), dtype=np.float32)
    for d_ in range(64):
        P[d_, d_ + 64] = -1.0
    for d_ in range(64, 128):
        P[d_, d_ - 64] = 1.0
    return np.ascontiguousarray(P.T.astype(BF))


def _tril_masks():
    kp = np.arange(128)[:, None]
    qf = np.arange(512)[None, :]
    m = np.stack([(qf >= j * 128 + kp) for j in range(4)]).astype(np.float32)
    return m.astype(BF).reshape(4 * 128, 512)


def _alpha_tables():
    a = np.linspace(0.0, 1.0, 256, dtype=np.float32)
    am = np.broadcast_to(a, (128, 256)).copy()
    am1 = np.broadcast_to(1.0 - a, (128, 256)).copy()
    return am, am1


class Sem:
    def __init__(self, h):
        self.h = h
        self.n = 0

    def inc(self, instr, k=1):
        instr.then_inc(self.h, k)
        self.n += k
        return self.n


class SerEng:

    _PASS = {"wait_ge", "nop", "dma_start", "dma_start_transpose", "wait_op"}

    def __init__(self, eng, sem):
        self._e = eng
        self._s = sem

    def __getattr__(self, name):
        f = getattr(self._e, name)
        if name in SerEng._PASS or name.startswith("_"):
            return f

        def wrapped(*a, **k):
            self._e.wait_ge(self._s.h, self._s.n)
            ins = f(*a, **k)
            ins.then_inc(self._s.h, 1)
            self._s.n += 1
            return ins

        return wrapped


class PB:
    def __init__(self):
        self.nc = bass.Bass("TRN2", target_bir_lowering=False)
        self.ctx = ExitStack()
        self.fvals = []
        self.ivals = []
        self._scopes = []

    def push_scope(self, pfx):
        self._scopes.append((ExitStack(), pfx))

    def pop_scope(self):
        st_, _ = self._scopes.pop()
        st_.close()

    def _alloc(self, factory, name):
        if self._scopes:
            st_, pfx = self._scopes[-1]
            return st_.enter_context(factory(pfx + name))
        return self.ctx.enter_context(factory(name))

    def sbuf(self, name, shape, dt):
        return self._alloc(lambda n: self.nc.sbuf_tensor(n, shape, dt), name)

    def psum(self, name, shape, dt=F32):
        return self._alloc(lambda n: self.nc.psum_tensor(n, shape, dt), name)

    def sem(self, name):
        return Sem(self.ctx.enter_context(self.nc.semaphore(name)))

    def make_sems(self):
        self.dse = self.sem("dse")
        self.gse2 = [self.sem("gse0"), self.sem("gse1")]
        self.gsev = self.sem("gsev")
        self.sv = self.sem("sv")
        self.ss = self.sem("ss")
        self.st = self.sem("st")
        self.sg = self.sem("sg")
        self.cc = self.sem("cc")
        self.dso = self.sem("dso")

    def all_sems(self):
        return [self.dse, self.gse2[0], self.gse2[1], self.gsev,
                self.sv, self.ss, self.st, self.sg, self.cc, self.dso]

    def barrier(self, skip_cc=False):
        sems = self.all_sems()
        if skip_cc:
            sems = [sm for sm in sems if sm is not self.cc]
        vals = [(sm.h, sm.n) for sm in sems]
        with self.nc.Block() as blk:
            for engname in ("sync", "gpsimd", "tensor", "vector", "scalar"):
                def _mk(eng, vals=vals):
                    for h, n in vals:
                        if n:
                            eng.wait_ge(h, n)
                getattr(blk, engname)(_mk)

    def alloc_consts(self, nf=24, ni=8):
        self.csb_f = self.sbuf("consts_f", [128, nf], F32)
        self.csb_i = self.sbuf("consts_i", [128, ni], I32)
        self.cin_f = self.nc.dram_tensor("consts_fin", [nf], F32, kind="ExternalInput")
        self.cin_i = self.nc.dram_tensor("consts_iin", [ni], I32, kind="ExternalInput")

    def cf(self, v):
        v = float(v)
        if v not in self.fvals:
            i = len(self.fvals)
            self.fvals.append(v)
            self.nc.const_aps.register(F32, v, self.csb_f[:, i : i + 1])
        return self.csb_f[:, self.fvals.index(v) : self.fvals.index(v) + 1]

    def ci(self, v):
        v = int(v)
        if v not in self.ivals:
            i = len(self.ivals)
            self.ivals.append(v)
            self.nc.const_aps.register(I32, v, self.csb_i[:, i : i + 1])
        return self.csb_i[:, self.ivals.index(v) : self.ivals.index(v) + 1]

    def load_consts(self, sync, dse):
        nf, ni = self.csb_f.shape[1], self.csb_i.shape[1]
        dse.inc(sync.dma_start(out=self.csb_f[:, :], in_=bass.AP(self.cin_f, 0, [[0, 128], [1, nf]])), 16)
        dse.inc(sync.dma_start(out=self.csb_i[:, :], in_=bass.AP(self.cin_i, 0, [[0, 128], [1, ni]])), 16)

    def const_arrays(self):
        nf, ni = self.csb_f.shape[1], self.csb_i.shape[1]
        cf = np.zeros(nf, dtype=np.float32)
        cf[: len(self.fvals)] = self.fvals
        ci_ = np.zeros(ni, dtype=np.int32)
        ci_[: len(self.ivals)] = self.ivals
        return {"consts_fin": cf, "consts_iin": ci_}

    def close(self):
        self.ctx.close()


def view(t, off, dims):
    X = t.shape[1]
    return bass.AP(t, off, [[X, 128]] + dims)


def wstrip_in(w, col0, ncols, nchunk=32):
    W = w.shape[1]
    return bass.AP(w, col0, [[W, 128], [128 * W, nchunk], [1, ncols]])


def emit_l0attn(p, x_in, wq, wk, wv, wo, cos_in, sin_in, tril_in, yp_out,
                xh_dram, q_dram, ao_dram):
    nc = p.nc
    p.push_scope("l0_")
    dse, dso, gse2, gsev = p.dse, p.dso, p.gse2, p.gsev
    sv, ss, st, sg = p.sv, p.ss, p.st, p.sg
    cosT = p.sbuf("cosT", [128, S], F32)
    sinT = p.sbuf("sinT", [128, S], F32)
    tril = p.sbuf("tril", [128, 4 * 512], BF16)
    ones_k = p.sbuf("ones_k", [128, 1], BF16)
    ones_1 = p.sbuf("ones_1", [1, 512], BF16)
    ks = p.sbuf("ks", [128, 4 * S], BF16)
    vs = p.sbuf("vs", [128, (S // 128) * 4 * HD], BF16)
    arA = p.sbuf("arA", [128, 16384], BF16)
    arB = p.sbuf("arB", [128, 32768], BF16)
    _xt0 = p.sbuf("xt0", [128, D], F32)
    xt = [_xt0, _xt0]
    sq = p.sbuf("sq", [128, D], BF16)
    _xh0 = p.sbuf("xhat0", [128, D], BF16)
    xhat = [_xh0, _xh0]
    stat = p.sbuf("stat", [128, 4], F32)
    qf = p.sbuf("qf", [128, 512], F32)
    qt0 = p.sbuf("qt0", [128, 512], F32)
    qrt = [p.sbuf(f"qrt{i}", [128, 512], BF16) for i in range(2)]
    qtile = p.sbuf("qtile", [128, 4 * 512], BF16)
    probs = [p.sbuf(f"probs{i}", [128, 512], BF16) for i in range(2)]
    srec = p.sbuf("srec", [1, 512], F32)
    srecb = [p.sbuf(f"srecb{i}", [1, 512], BF16) for i in range(2)]
    bc_sb = p.sbuf("bc_sb", [128, 512], F32)
    _ypt0 = p.sbuf("ypt0", [128, 512], F32)
    ypt = [_ypt0, _ypt0]

    ps_a = p.psum("ps_a", [128, 512], F32)
    ps_b = p.psum("ps_b", [128, 512], F32)
    ps_sc = [p.psum(f"ps_sc{i}", [128, 512], F32) for i in range(2)]
    ps_pv = p.psum("ps_pv", [128, 512], F32)
    ps_sum = p.psum("ps_sum", [1, 512], F32)
    ps_sum2 = p.psum("ps_sum2", [1, 512], F32)
    ps_bc = p.psum("ps_bc", [128, 512], F32)

    with nc.Block() as blk:
        @blk.sync
        def _(sync):
            dse.inc(sync.dma_start(out=cosT[:, :], in_=cos_in[:, :]), 16)
            dse.inc(sync.dma_start(out=sinT[:, :], in_=sin_in[:, :]), 16)
            dse.inc(sync.dma_start(
                out=tril[:, :],
                in_=bass.AP(tril_in, 0, [[512, 128], [128 * 512, 4], [1, 512]])), 16)
        boot_d = dse.n

        @blk.gpsimd
        def _(gpsimd):
            G = SerEng(gpsimd, sg)
            G.memset(ones_k[:, :], 1.0)
            G.memset(ones_1[:, :], 1.0)
        boot_g = sg.n

        c_eps = p.cf(EPS)
        c_im = p.cf(1.0 / D)
        c_shift = p.cf(-SM_SHIFT)
        c_scale = p.cf(SCALE)

        A_ld, A_sq, A_var, A_sd, A_xh, A_st = [], [], [], [], [], []
        for i in range(16):
            bi = i % 2

            @blk.sync
            def _(sync, i=i, bi=bi):
                if i == 0:
                    sync.wait_ge(dse.h, dse.n)
                    sync.wait_ge(dso.h, dso.n)
                if i >= 1:
                    sync.wait_ge(sv.h, A_xh[i - 1])
                dse.inc(sync.dma_start(out=xt[bi][:, :], in_=x_in[i * 128 : (i + 1) * 128, :]), 16)
            A_ld.append(dse.n)

            @blk.scalar
            def _(scalar, i=i, bi=bi):
                SS = SerEng(scalar, ss)
                scalar.wait_ge(dse.h, dse.n)
                scalar.wait_ge(dso.h, dso.n)
                if i >= 1:
                    scalar.wait_ge(sv.h, A_var[i - 1])
                SS.activation(out=sq[:, :], in_=xt[bi][:, :], func=AF.Identity,
                              accum_out=stat[:, 0:1])
                SS.activation(out=sq[:, :], in_=xt[bi][:, :], func=AF.Square,
                              accum_out=stat[:, 1:2])
            A_sq.append(ss.n)

            @blk.vector
            def _(vector, i=i, bi=bi):
                V = SerEng(vector, sv)
                vector.wait_ge(ss.h, A_sq[i])
                V.tensor_scalar(stat[:, 0:1], stat[:, 0:1], c_im, None, ALU.mult)
                V.tensor_scalar(stat[:, 1:2], stat[:, 1:2], c_im, None, ALU.mult)
                V.tensor_tensor(stat[:, 2:3], stat[:, 0:1], stat[:, 0:1], ALU.mult)
                V.tensor_tensor(stat[:, 2:3], stat[:, 1:2], stat[:, 2:3], ALU.subtract)
            A_var.append(sv.n)

            @blk.scalar
            def _(scalar, i=i):
                SS = SerEng(scalar, ss)
                scalar.wait_ge(sv.h, A_var[i])
                SS.activation(out=stat[:, 3:4], in_=stat[:, 2:3], func=AF.Sqrt, bias=c_eps)
            A_sd.append(ss.n)

            @blk.vector
            def _(vector, i=i, bi=bi):
                V = SerEng(vector, sv)
                vector.wait_ge(ss.h, A_sd[i])
                V.reciprocal(stat[:, 2:3], stat[:, 3:4])
                if i >= 2:
                    vector.wait_ge(dse.h, dse.n)
                    vector.wait_ge(dso.h, dso.n)
                V.tensor_scalar(
                    xhat[bi][:, :], xt[bi][:, :], stat[:, 0:1], stat[:, 2:3], ALU.subtract, ALU.mult)
            A_xh.append(sv.n)

            @blk.sync
            def _(sync, i=i, bi=bi):
                sync.wait_ge(sv.h, A_xh[i])
                dso.inc(sync.dma_start(out=xh_dram[i * 128 : (i + 1) * 128, :], in_=xhat[bi][:, :]), 16)
            A_st.append(dso.n)
        stageA_st = dso.n

        pe_tg = []
        q_ready = []
        kv_ready = []
        ev_marks = {}
        for tg in range(4):
            tg0 = tg * 512

            @blk.sync
            def _(sync, tg=tg):
                sync.wait_ge(dso.h, A_st[4 * tg + 3])
                if tg > 0:
                    sync.wait_ge(st.h, pe_tg[tg - 1])
                for c in range(32):
                    dse.inc(sync.dma_start(
                        out=view(arA, c * 512, [[1, 512]]),
                        in_=xh_dram[tg0 : tg0 + 512, c * 128 : (c + 1) * 128],
                        transpose=True), 16)
            xhT_ld = dse.n

            tg_ev0 = sv.n
            fgroups = [("k", 0)] + [("q", g) for g in range(4)]
            ev_idx = 0
            for gi, (kind, g) in enumerate(fgroups):
                sb = gi % 2
                win = wk if kind == "k" else wq

                @blk.gpsimd
                def _(gpsimd, tg=tg, g=g, sb=sb, gi=gi, win=win):
                    gpsimd.wait_ge(sg.h, boot_g)
                    key = ("peg", tg, gi - 2)
                    if key in ev_marks:
                        gpsimd.wait_ge(st.h, ev_marks[key])
                    elif tg > 0:
                        gpsimd.wait_ge(st.h, pe_tg[tg - 1])
                    gse2[sb].inc(gpsimd.dma_start(
                        out=view(arB, sb * 16384, [[512, 32], [1, 512]]),
                        in_=wstrip_in(win, g * 512, 512)), 16)
                ev_marks[("ld", tg, gi)] = gse2[sb].n

                for sub in range(4):
                    f = g * 4 + sub
                    idx = ev_idx
                    ev_idx += 1
                    ps = ps_a if idx % 2 == 0 else ps_b

                    @blk.tensor
                    def _(tensor, tg=tg, gi=gi, idx=idx, ps=ps, sb=sb, sub=sub):
                        tensor.wait_ge(gse2[sb].h, ev_marks[("ld", tg, gi)])
                        tensor.wait_ge(dse.h, dse.n)
                        tensor.wait_ge(dso.h, dso.n)
                        key = ("ev", tg, idx - 2)
                        tensor.wait_ge(sv.h, ev_marks.get(key, tg_ev0))
                        for c in range(32):
                            mm = tensor.matmul(
                                ps[:, :],
                                view(arB, sb * 16384 + c * 512 + sub * 128, [[1, 128]]),
                                view(arA, c * 512, [[1, 512]]),
                                start=(c == 0), stop=(c == 31))
                        st.inc(mm)
                    ev_marks[("peg", tg, gi)] = st.n
                    ev_marks[("pemm", tg, idx)] = st.n

                    @blk.vector
                    def _(vector, tg=tg, f=f, kind=kind, idx=idx, ps=ps, tg0=tg0):
                        V = SerEng(vector, sv)
                        vector.wait_ge(st.h, ev_marks[("pemm", tg, idx)])
                        V.tensor_copy(qf[:, :], ps[:, :])
                        V.stream_shuffle(qt0[:, :], qf[:, :], PAIR_SWAP)
                        V.tensor_tensor(qt0[:, :], qt0[:, :], sinT[:, tg0 : tg0 + 512], ALU.mult)
                        V.tensor_tensor(qf[:, :], qf[:, :], cosT[:, tg0 : tg0 + 512], ALU.mult)
                        if kind == "k":
                            dst = ks[:, f * S + tg0 : f * S + tg0 + 512]
                            V.tensor_tensor(dst, qf[:, :], qt0[:, :], ALU.add)
                        else:
                            par = idx % 2
                            vector.wait_ge(dse.h, dse.n)
                            vector.wait_ge(dso.h, dso.n)
                            V.tensor_tensor(qrt[par][:, :], qf[:, :], qt0[:, :], ALU.add)
                    ev_marks[("ev", tg, idx)] = sv.n

                    if kind == "q":
                        @blk.sync
                        def _(sync, tg=tg, f=f, idx=idx, tg0=tg0):
                            sync.wait_ge(sv.h, ev_marks[("ev", tg, idx)])
                            dso.inc(sync.dma_start(
                                out=q_dram[f * 128 : (f + 1) * 128, tg0 : tg0 + 512],
                                in_=qrt[idx % 2][:, :]), 16)

            @blk.gpsimd
            def _(gpsimd, tg=tg):
                gpsimd.wait_ge(st.h, st.n)
                gsev.inc(gpsimd.dma_start(
                    out=view(arB, 0, [[512, 32], [1, 512]]),
                    in_=wstrip_in(wv, 0, 512)), 16)
            vld = gsev.n

            for sub in range(4):
                @blk.tensor
                def _(tensor, tg=tg, sub=sub, vld=vld):
                    tensor.wait_ge(gsev.h, vld)
                    tensor.wait_ge(sv.h, sv.n)
                    for c in range(32):
                        mm = tensor.matmul(
                            ps_a[:, :],
                            view(arA, c * 512 + sub * 128, [[1, 128]]),
                            view(arB, c * 512, [[1, 512]]),
                            start=(c == 0), stop=(c == 31))
                    st.inc(mm)
                vpe = st.n

                @blk.vector
                def _(vector, tg=tg, sub=sub, vpe=vpe):
                    V = SerEng(vector, sv)
                    vector.wait_ge(st.h, vpe)
                    tt = tg * 4 + sub
                    V.tensor_copy(vs[:, tt * 512 : (tt + 1) * 512], ps_a[:, :])
            pe_tg.append(st.n)
            q_ready.append(dso.n)
            kv_ready.append(sv.n)
        stageB_v = sv.n

        expc = {0: 0, 1: 0}
        pvc = {0: 0, 1: 0}
        gp = 0
        last_ev_par = [sv.n, sv.n]
        pv_b = [ps_pv, ps_a]
        bc_b = [ps_bc, ps_b]
        for qb in range(4):
            nkt = (qb + 1) * 4
            for kvh in range(4):
                @blk.sync
                def _(sync, qb=qb, kvh=kvh):
                    sync.wait_ge(sv.h, kv_ready[qb])
                    sync.wait_ge(dso.h, q_ready[qb])
                    sync.wait_ge(st.h, st.n)
                    for j in range(4):
                        fr = (4 * kvh + j) * 128
                        dse.inc(sync.dma_start(
                            out=qtile[:, j * 512 : (j + 1) * 512],
                            in_=q_dram[fr : fr + 128, qb * 512 : (qb + 1) * 512]), 16)
                qld = dse.n

                for j in range(4):
                    par = gp % 2
                    pvx = pv_b[par]
                    bcx = bc_b[par]
                    sumx = ps_sum[0:1, :] if par == 0 else ps_sum2[0:1, :]
                    for kt in range(nkt):
                        pp = kt % 2

                        @blk.tensor
                        def _(tensor, kvh=kvh, j=j, kt=kt, pp=pp, qld=qld):
                            tensor.wait_ge(dse.h, qld)
                            tensor.wait_ge(ss.h, expc[pp])
                            st.inc(tensor.matmul(
                                ps_sc[pp][:, :],
                                ks[:, kvh * S + kt * 128 : kvh * S + kt * 128 + 128],
                                qtile[:, j * 512 : (j + 1) * 512],
                                start=True, stop=True))
                        scd = st.n

                        @blk.scalar
                        def _(scalar, kt=kt, pp=pp, scd=scd):
                            SS = SerEng(scalar, ss)
                            scalar.wait_ge(st.h, scd)
                            scalar.wait_ge(st.h, pvc[pp])
                            SS.activation(
                                out=probs[pp][:, :], in_=ps_sc[pp][:, :],
                                func=AF.Exp, bias=c_shift, scale=c_scale)
                        expc[pp] = ss.n
                        expd = ss.n

                        maskd = None
                        if kt >= qb * 4:
                            @blk.vector
                            def _(vector, kt=kt, qb=qb, pp=pp, expd=expd):
                                V = SerEng(vector, sv)
                                vector.wait_ge(ss.h, expd)
                                off = kt - qb * 4
                                V.tensor_tensor(
                                    probs[pp][:, :], probs[pp][:, :],
                                    tril[:, off * 512 : (off + 1) * 512], ALU.mult)
                            maskd = sv.n

                        @blk.tensor
                        def _(tensor, kvh=kvh, kt=kt, pp=pp, expd=expd, maskd=maskd,
                              nkt=nkt, pvx=pvx, sumx=sumx, par=par):
                            if maskd is not None:
                                tensor.wait_ge(sv.h, maskd)
                            else:
                                tensor.wait_ge(ss.h, expd)
                            if kt == 0:
                                tensor.wait_ge(sv.h, last_ev_par[par])
                            tensor.matmul(
                                pvx[:, :],
                                vs[:, kt * 512 + kvh * 128 : kt * 512 + kvh * 128 + 128],
                                probs[pp][:, :],
                                start=(kt == 0), stop=(kt == nkt - 1))
                            st.inc(tensor.matmul(
                                sumx, ones_k[:, :], probs[pp][:, :],
                                start=(kt == 0), stop=(kt == nkt - 1)))
                        pvc[pp] = st.n
                    pvdone = st.n

                    @blk.vector
                    def _(vector, pvdone=pvdone, sumx=sumx, par=par):
                        V = SerEng(vector, sv)
                        vector.wait_ge(st.h, pvdone)
                        V.reciprocal(srec[:, :], sumx)
                        V.tensor_copy(srecb[par][:, :], srec[:, :])
                    recd = sv.n

                    @blk.tensor
                    def _(tensor, recd=recd, bcx=bcx, par=par):
                        tensor.wait_ge(sv.h, recd)
                        tensor.wait_ge(sv.h, last_ev_par[par])
                        st.inc(tensor.matmul(
                            bcx[:, :], ones_1[0:1, 0:128], srecb[par][0:1, :],
                            start=True, stop=True))
                    bcd = st.n

                    @blk.vector
                    def _(vector, kvh=kvh, j=j, bcd=bcd, pvx=pvx, bcx=bcx):
                        V = SerEng(vector, sv)
                        vector.wait_ge(st.h, bcd)
                        V.tensor_copy(bc_sb[:, :], bcx[:, :])
                        c = 4 * kvh + j
                        vector.wait_ge(dso.h, dso.n)
                        V.tensor_tensor(
                            view(arA, c * 512, [[1, 512]]), pvx[:, :], bc_sb[:, :], ALU.mult)
                    last_ev_par[par] = sv.n
                    aoev = sv.n

                    @blk.sync
                    def _(sync, qb=qb, kvh=kvh, j=j, aoev=aoev):
                        sync.wait_ge(sv.h, aoev)
                        c = 4 * kvh + j
                        dso.inc(sync.dma_start(
                            out=ao_dram[c * 128 : (c + 1) * 128, qb * 512 : (qb + 1) * 512],
                            in_=view(arA, c * 512, [[1, 512]])), 16)
                    gp += 1
        stageC_ao = dso.n

        _aot0 = p.sbuf("aot0", [128, 16 * 128], BF16)
        aot = [_aot0, _aot0]
        aold = {}
        ypst = {}
        for fg in range(8):
            @blk.gpsimd
            def _(gpsimd, fg=fg):
                gpsimd.wait_ge(st.h, st.n)
                gsev.inc(gpsimd.dma_start(
                    out=view(arB, 0, [[512, 16], [1, 512]]),
                    in_=bass.AP(wo, fg * 512, [[D, 128], [128 * D, 16], [1, 512]])), 16)
            wold = gsev.n

            for tt in range(16):
                par = tt % 2

                @blk.sync
                def _(sync, fg=fg, tt=tt, par=par):
                    if fg == 0 and tt == 0:
                        sync.wait_ge(dse.h, dse.n)
                        sync.wait_ge(dso.h, dso.n)
                    key = ("pe", fg, tt - 1) if tt >= 1 else ("pe", fg - 1, tt + 15)
                    if key in aold:
                        sync.wait_ge(st.h, aold[key])
                    dse.inc(sync.dma_start(
                        out=aot[par][:, :],
                        in_=bass.AP(ao_dram, tt * 128, [[S, 128], [128 * S, 16], [1, 128]])), 16)
                aold[("ld", fg, tt)] = dse.n

                @blk.tensor
                def _(tensor, fg=fg, tt=tt, par=par, wold=wold):
                    tensor.wait_ge(gsev.h, wold)
                    tensor.wait_ge(dse.h, dse.n)
                    tensor.wait_ge(dso.h, dso.n)
                    tensor.wait_ge(sv.h, sv.n)
                    for c in range(16):
                        mm = tensor.matmul(
                            ps_a[:, :],
                            view(aot[par], c * 128, [[1, 128]]),
                            view(arB, c * 512, [[1, 512]]),
                            start=(c == 0), stop=(c == 15))
                    st.inc(mm)
                aold[("pe", fg, tt)] = st.n

                @blk.vector
                def _(vector, fg=fg, tt=tt, par=par):
                    V = SerEng(vector, sv)
                    vector.wait_ge(st.h, aold[("pe", fg, tt)])
                    key = ("st", fg, tt - 2) if tt >= 2 else ("st", fg - 1, tt + 14)
                    vector.wait_ge(dse.h, dse.n)
                    vector.wait_ge(dso.h, dso.n)
                    V.tensor_copy(ypt[par][:, :], ps_a[:, :])
                ypev = sv.n

                @blk.sync
                def _(sync, fg=fg, tt=tt, par=par, ypev=ypev):
                    sync.wait_ge(sv.h, ypev)
                    dso.inc(sync.dma_start(
                        out=yp_out[tt * 128 : (tt + 1) * 128, fg * 512 : (fg + 1) * 512],
                        in_=ypt[par][:, :]), 16)
                ypst[("st", fg, tt)] = dse.n

        @blk.sync
        def _(sync):
            sync.wait_ge(dse.h, dse.n)
            sync.wait_ge(dso.h, dso.n)

    p.pop_scope()


def l0attn_inmaps(x_full, w_qkv, w_o, consts):
    cos, sin = _rope_l0()
    tril = _tril_masks()
    maps = []
    for c in range(8):
        b, hh = c // 2, c % 2
        m = dict(consts)
        m["xb"] = np.ascontiguousarray(x_full[b])
        m["wq_s"] = np.ascontiguousarray(w_qkv[:, hh * 2048 : (hh + 1) * 2048])
        m["wk_s"] = np.ascontiguousarray(w_qkv[:, 4096 + hh * 512 : 4096 + (hh + 1) * 512])
        m["wv_s"] = np.ascontiguousarray(w_qkv[:, 5120 + hh * 512 : 5120 + (hh + 1) * 512])
        m["wo_s"] = np.ascontiguousarray(w_o[hh * 2048 : (hh + 1) * 2048, :])
        m["cos_t"] = cos
        m["sin_t"] = sin
        m["tril_t"] = tril
        maps.append(m)
    return maps


def emit_dequant(p, qw, lut4, lut8, w1q, sc_in, zr_in, mk_in, s2_in,
                 shf_in, kramp_in, wl_out, wh_out, w1_out):
    nc = p.nc
    p.push_scope("dq_")
    dse, dso, sv = p.dse, p.dso, p.sv
    shf_sb = p.sbuf("shf_sb", [128, 32], I32)
    kramp_sb = p.sbuf("kramp_sb", [128, 256], F32)
    qws = p.sbuf("qws", [128, 8 * 128], I32)
    c8 = p.sbuf("c8", [128, D], I32)
    tbit = p.sbuf("tbit", [128, D], I32)
    c8b = p.sbuf("c8b", [128, D], BF16)
    c4b = p.sbuf("c4b", [128, D], BF16)
    eq = p.sbuf("eq", [128, D], BF16)
    accl = p.sbuf("accl", [128, D], BF16)
    acch = p.sbuf("acch", [128, D], BF16)
    lut4_sb = p.sbuf("lut4_sb", [128, 16], F32)
    lut8_sb = p.sbuf("lut8_sb", [128, 256], F32)
    w1qs = p.sbuf("w1qs", [128, D], I32)
    mks = p.sbuf("mks", [128, D], I32)
    qbf = p.sbuf("qbf", [128, D], BF16)
    mbf = p.sbuf("mbf", [128, D], BF16)
    w1t = p.sbuf("w1t", [128, D], BF16)
    sc_sb = p.sbuf("sc_sb", [128, 32], F32)
    zr_sb = p.sbuf("zr_sb", [128, 32], F32)
    s2_sb = p.sbuf("s2_sb", [128, 1], F32)
    A_sb = p.sbuf("A_sb", [128, 32], F32)
    Bz_sb = p.sbuf("Bz_sb", [128, 32], F32)

    with nc.Block() as blk:
        @blk.sync
        def _(sync):
            dse.inc(sync.dma_start(out=shf_sb[:, :], in_=shf_in[:, :]), 16)
            dse.inc(sync.dma_start(out=kramp_sb[:, :], in_=kramp_in[:, :]), 16)
        c1i = p.ci(1)

        for ot in range(4):
            o0 = ot * 128

            @blk.sync
            def _(sync, ot=ot, o0=o0):
                sync.wait_ge(sv.h, sv.n)
                dse.inc(sync.dma_start(
                    out=qws[:, :],
                    in_=bass.AP(qw, o0 * 128, [[128, 128], [512 * 128, 8], [1, 128]])), 16)
                dse.inc(sync.dma_start(out=lut4_sb[:, :], in_=lut4[o0 : o0 + 128, :]), 16)
                dse.inc(sync.dma_start(out=lut8_sb[:, :], in_=lut8[o0 : o0 + 128, :]), 16)
                dse.inc(sync.dma_start(out=w1qs[:, :], in_=w1q[o0 : o0 + 128, :]), 16)
                dse.inc(sync.dma_start(out=mks[:, :], in_=mk_in[o0 : o0 + 128, :]), 16)
                dse.inc(sync.dma_start(out=sc_sb[:, :], in_=sc_in[o0 : o0 + 128, :]), 16)
                dse.inc(sync.dma_start(out=zr_sb[:, :], in_=zr_in[o0 : o0 + 128, :]), 16)
                dse.inc(sync.dma_start(out=s2_sb[:, :], in_=s2_in[o0 : o0 + 128, :]), 16)

            @blk.vector
            def _(vector, ot=ot):
                V = SerEng(vector, sv)
                vector.wait_ge(dse.h, dse.n)
                vector.wait_ge(dso.h, dso.n)
                for pl in range(8):
                    a_w = view(qws, pl * 128, [[1, 128], [0, 32]])
                    a_s = bass.AP(shf_sb, 0, [[32, 128], [0, 128], [1, 32]])
                    V.tensor_tensor(tbit[:, :], a_w, a_s, ALU.logical_shift_right)
                    V.tensor_scalar(tbit[:, :], tbit[:, :], c1i, None, ALU.bitwise_and)
                    if pl == 0:
                        V.tensor_copy(c8[:, :], tbit[:, :])
                    else:
                        V.tensor_scalar(c8[:, :], tbit[:, :], p.ci(pl), c8[:, :], ALU.logical_shift_left, ALU.bitwise_or) if False else \
                            V.scalar_tensor_tensor(c8[:, :], tbit[:, :], p.ci(pl), c8[:, :], ALU.logical_shift_left, ALU.bitwise_or)
                V.tensor_copy(c8b[:, :], c8[:, :])
                V.tensor_scalar(c8[:, :], c8[:, :], p.ci(15), None, ALU.bitwise_and)
                V.tensor_copy(c4b[:, :], c8[:, :])
                for k in range(16):
                    if k == 0:
                        V.tensor_scalar(accl[:, :], c4b[:, :], kramp_sb[:, 0:1], lut4_sb[:, 0:1], ALU.is_equal, ALU.mult)
                    else:
                        V.tensor_scalar(eq[:, :], c4b[:, :], kramp_sb[:, k : k + 1], lut4_sb[:, k : k + 1], ALU.is_equal, ALU.mult)
                        V.tensor_tensor(accl[:, :], accl[:, :], eq[:, :], ALU.add)
                for k in range(256):
                    if k == 0:
                        V.tensor_scalar(acch[:, :], c8b[:, :], kramp_sb[:, 0:1], lut8_sb[:, 0:1], ALU.is_equal, ALU.mult)
                    else:
                        V.tensor_scalar(eq[:, :], c8b[:, :], kramp_sb[:, k : k + 1], lut8_sb[:, k : k + 1], ALU.is_equal, ALU.mult)
                        V.tensor_tensor(acch[:, :], acch[:, :], eq[:, :], ALU.add)
                V.tensor_scalar(A_sb[:, :], sc_sb[:, :], s2_sb[:, 0:1], None, ALU.mult)
                V.tensor_tensor(Bz_sb[:, :], zr_sb[:, :], A_sb[:, :], ALU.mult)
                V.tensor_scalar(Bz_sb[:, :], Bz_sb[:, :], -1.0, None, ALU.mult)
                V.tensor_copy(qbf[:, :], w1qs[:, :])
                V.tensor_copy(mbf[:, :], mks[:, :])
                for g in range(32):
                    V.tensor_scalar(
                        w1t[:, g * 128 : (g + 1) * 128], qbf[:, g * 128 : (g + 1) * 128],
                        A_sb[:, g : g + 1], Bz_sb[:, g : g + 1], ALU.mult, ALU.add)
                V.tensor_tensor(w1t[:, :], w1t[:, :], mbf[:, :], ALU.mult)

            @blk.sync
            def _(sync, ot=ot, o0=o0):
                sync.wait_ge(sv.h, sv.n)
                dso.inc(sync.dma_start(out=wl_out[o0 : o0 + 128, :], in_=accl[:, :]), 16)
                dso.inc(sync.dma_start(out=wh_out[o0 : o0 + 128, :], in_=acch[:, :]), 16)
                dso.inc(sync.dma_start(out=w1_out[o0 : o0 + 128, :], in_=w1t[:, :]), 16)

        @blk.sync
        def _(sync):
            sync.wait_ge(dse.h, dse.n)
            sync.wait_ge(dso.h, dso.n)

    p.pop_scope()


def dequant_inmaps(z, consts):
    shf = np.broadcast_to(np.arange(32, dtype=np.int32), (128, 32)).copy()
    kramp = np.broadcast_to(np.arange(256, dtype=np.float32), (128, 256)).copy()
    maps = []
    for c in range(8):
        r = slice(c * 512, (c + 1) * 512)
        m = dict(consts)
        m["qw_s"] = np.ascontiguousarray(z["L0_ffn_qweight"][:, r, :])
        m["lut4_s"] = np.ascontiguousarray(z["L0_ffn_lut4"][r])
        m["lut8_s"] = np.ascontiguousarray(z["L0_ffn_lut8"][r])
        m["w1q_s"] = np.ascontiguousarray(z["L1_ffn_W_q"][r])
        m["sc_s"] = np.ascontiguousarray(z["L1_ffn_scales"][r, :, 0])
        m["zr_s"] = np.ascontiguousarray(z["L1_ffn_zeros"][r, :, 0])
        m["mk_s"] = np.ascontiguousarray(z["L1_ffn_mask"][r])
        m["s2_s"] = np.ascontiguousarray(z["L1_ffn_scale2"][r])
        m["shf"] = shf
        m["kramp"] = kramp
        maps.append(m)
    return maps


def emit_ffn0(p, x_in, ps_in, wl_in, wh_in, bias_in, x2_out,
              x1_dram, xh2_dram, ydl_dram, ydh_dram):
    nc = p.nc
    p.push_scope("f0_")
    dse, dso, sv, ss, st, cc = p.dse, p.dso, p.sv, p.ss, p.st, p.cc
    ln2T = p.sbuf("ln2T", [128, 32 * 1024], BF16)
    arW = p.sbuf("arW", [128, 2 * 32 * 512], BF16)
    ta = p.sbuf("ta", [128, D], F32)
    tb = p.sbuf("tb", [128, D], F32)
    tc = p.sbuf("tc", [128, D], F32)
    sq = p.sbuf("sq", [128, D], BF16)
    stat = p.sbuf("stat", [128, 8], F32)
    masks = p.sbuf("masks", [128, 8], F32)
    biasbc = p.sbuf("biasbc", [128, 1024], F32)
    ypt = [p.sbuf(f"ypt{i}", [128, 512], F32) for i in range(2)]
    ps_a = p.psum("ps_a", [128, 512], F32)
    ps_b = p.psum("ps_b", [128, 512], F32)

    with nc.Block() as blk:
        c_eps = p.cf(EPS)
        c_im = p.cf(1.0 / D)
        c_64 = p.cf(64.0)

        for i in range(8):
            r0 = i * 128

            @blk.sync
            def _(sync, i=i, r0=r0):
                sync.wait_ge(sv.h, sv.n)
                sync.wait_ge(dse.h, dse.n)
                sync.wait_ge(dso.h, dso.n)
                dse.inc(sync.dma_start(out=ta[:, :], in_=x_in[r0 : r0 + 128, :]), 16)
                dse.inc(sync.dma_start(out=tb[:, :], in_=ps_in[r0 : r0 + 128, :]), 16)

            @blk.scalar
            def _(scalar, i=i):
                SS = SerEng(scalar, ss)
                scalar.wait_ge(sv.h, sv.n)
                scalar.wait_ge(dse.h, dse.n)
                scalar.wait_ge(dso.h, dso.n)
                SS.activation(out=sq[:, :], in_=tb[:, :], func=AF.Identity,
                              accum_out=stat[:, 0:1])
                SS.activation(out=sq[:, :], in_=tb[:, :], func=AF.Square,
                              accum_out=stat[:, 1:2])

            @blk.vector
            def _(vector, i=i):
                V = SerEng(vector, sv)
                vector.wait_ge(ss.h, ss.n)
                V.tensor_scalar(stat[:, 0:1], stat[:, 0:1], c_im, None, ALU.mult)
                V.tensor_scalar(stat[:, 1:2], stat[:, 1:2], c_im, None, ALU.mult)
                V.tensor_tensor(stat[:, 2:3], stat[:, 0:1], stat[:, 0:1], ALU.mult)
                V.tensor_tensor(stat[:, 2:3], stat[:, 1:2], stat[:, 2:3], ALU.subtract)

            @blk.scalar
            def _(scalar, i=i):
                SS = SerEng(scalar, ss)
                scalar.wait_ge(sv.h, sv.n)
                SS.activation(out=stat[:, 3:4], in_=stat[:, 2:3], func=AF.Sqrt, bias=c_eps)

            @blk.vector
            def _(vector, i=i, r0=r0):
                V = SerEng(vector, sv)
                vector.wait_ge(ss.h, ss.n)
                V.reciprocal(stat[:, 3:4], stat[:, 3:4])
                V.tensor_scalar(tc[:, :], tb[:, :], stat[:, 0:1], stat[:, 3:4], ALU.subtract, ALU.mult)
                V.tensor_tensor(ta[:, :], ta[:, :], tc[:, :], ALU.add)

            @blk.sync
            def _(sync, i=i, r0=r0):
                sync.wait_ge(sv.h, sv.n)
                dso.inc(sync.dma_start(out=x1_dram[r0 : r0 + 128, :], in_=ta[:, :]), 16)

            @blk.scalar
            def _(scalar, i=i):
                SS = SerEng(scalar, ss)
                scalar.wait_ge(sv.h, sv.n)
                scalar.wait_ge(dse.h, dse.n)
                scalar.wait_ge(dso.h, dso.n)
                SS.activation(out=sq[:, :], in_=ta[:, :], func=AF.Identity,
                              accum_out=stat[:, 0:1])
                SS.activation(out=sq[:, :], in_=ta[:, :], func=AF.Square,
                              accum_out=stat[:, 1:2])

            @blk.vector
            def _(vector, i=i):
                V = SerEng(vector, sv)
                vector.wait_ge(ss.h, ss.n)
                V.tensor_scalar(stat[:, 0:1], stat[:, 0:1], c_im, None, ALU.mult)
                V.tensor_scalar(stat[:, 1:2], stat[:, 1:2], c_im, None, ALU.mult)
                V.tensor_tensor(stat[:, 2:3], stat[:, 0:1], stat[:, 0:1], ALU.mult)
                V.tensor_tensor(stat[:, 2:3], stat[:, 1:2], stat[:, 2:3], ALU.subtract)
                V.tensor_scalar(stat[:, 4:5], stat[:, 2:3], c_eps, None, ALU.add)
                V.reciprocal(stat[:, 4:5], stat[:, 4:5])
                V.tensor_tensor(stat[:, 4:5], stat[:, 2:3], stat[:, 4:5], ALU.mult)

            @blk.scalar
            def _(scalar, i=i):
                SS = SerEng(scalar, ss)
                scalar.wait_ge(sv.h, sv.n)
                SS.activation(out=stat[:, 5:6], in_=stat[:, 4:5], func=AF.Sqrt)
                SS.activation(out=stat[:, 3:4], in_=stat[:, 2:3], func=AF.Sqrt, bias=c_eps)

            @blk.vector
            def _(vector, i=i):
                V = SerEng(vector, sv)
                vector.wait_ge(ss.h, ss.n)
                V.tensor_scalar(masks[:, i : i + 1], stat[:, 5:6], c_64, c_64, ALU.mult, ALU.is_gt)
                V.reciprocal(stat[:, 3:4], stat[:, 3:4])
                V.tensor_scalar(sq[:, :], ta[:, :], stat[:, 0:1], stat[:, 3:4], ALU.subtract, ALU.mult)

            @blk.sync
            def _(sync, i=i, r0=r0):
                sync.wait_ge(sv.h, sv.n)
                dso.inc(sync.dma_start(out=xh2_dram[r0 : r0 + 128, :], in_=sq[:, :]), 16)

        @blk.sync
        def _(sync):
            sync.wait_ge(dse.h, dse.n)
            sync.wait_ge(dso.h, dso.n)
            for c in range(32):
                dse.inc(sync.dma_start(
                    out=view(ln2T, c * 1024, [[1, 1024]]),
                    in_=xh2_dram[0:1024, c * 128 : (c + 1) * 128], transpose=True), 16)

        wmk = {}
        for wi, (win, ydram) in enumerate(((wl_in, ydl_dram), (wh_in, ydh_dram))):
            for og in range(8):
                gidx = wi * 8 + og
                wh_ = gidx % 2

                @blk.sync
                def _(sync, og=og, win=win, wh_=wh_, gidx=gidx):
                    if gidx == 0:
                        sync.wait_ge(dso.h, dso.n)
                        sync.wait_ge(cc.h, 2)
                    key = ("pe", gidx - 2)
                    if key in wmk:
                        sync.wait_ge(st.h, wmk[key])
                    for c in range(32):
                        dse.inc(sync.dma_start(
                            out=view(arW, wh_ * 16384 + c * 512, [[1, 512]]),
                            in_=win[og * 512 : (og + 1) * 512, c * 128 : (c + 1) * 128],
                            transpose=True), 16)
                    dse.inc(sync.dma_start(
                        out=biasbc[:, wh_ * 512 : wh_ * 512 + 512],
                        in_=bass.AP(bias_in, og * 512, [[0, 128], [1, 512]])), 16)
                wld = dse.n

                for tsub in range(8):
                    pp = tsub % 2
                    ps = ps_a if pp == 0 else ps_b

                    @blk.tensor
                    def _(tensor, og=og, tsub=tsub, ps=ps, wh_=wh_, wld=wld):
                        tensor.wait_ge(dse.h, wld)
                        tensor.wait_ge(sv.h, sv.n)
                        for c in range(32):
                            mm = tensor.matmul(
                                ps[:, :],
                                view(ln2T, c * 1024 + tsub * 128, [[1, 128]]),
                                view(arW, wh_ * 16384 + c * 512, [[1, 512]]),
                                start=(c == 0), stop=(c == 31))
                        st.inc(mm)
                    ped = st.n

                    @blk.vector
                    def _(vector, tsub=tsub, ped=ped, pp=pp, ps=ps, wh_=wh_):
                        V = SerEng(vector, sv)
                        vector.wait_ge(st.h, ped)
                        vector.wait_ge(dso.h, dso.n)
                        V.tensor_tensor(ypt[pp][:, :], ps[:, :],
                                        biasbc[:, wh_ * 512 : wh_ * 512 + 512], ALU.add)
                    yev = sv.n

                    @blk.sync
                    def _(sync, og=og, tsub=tsub, yev=yev, pp=pp, ydram=ydram):
                        sync.wait_ge(sv.h, yev)
                        dso.inc(sync.dma_start(
                            out=ydram[tsub * 128 : (tsub + 1) * 128, og * 512 : (og + 1) * 512],
                            in_=ypt[pp][:, :]), 16)
                wmk[("pe", gidx)] = st.n

        for i in range(8):
            r0 = i * 128

            @blk.sync
            def _(sync, i=i, r0=r0):
                sync.wait_ge(dse.h, dse.n)
                sync.wait_ge(dso.h, dso.n)
                sync.wait_ge(sv.h, sv.n)
                dse.inc(sync.dma_start(out=ta[:, :], in_=x1_dram[r0 : r0 + 128, :]), 16)
                dse.inc(sync.dma_start(out=tb[:, :], in_=ydh_dram[r0 : r0 + 128, :]), 16)
                dse.inc(sync.dma_start(out=tc[:, :], in_=ydl_dram[r0 : r0 + 128, :]), 16)

            @blk.vector
            def _(vector, i=i):
                V = SerEng(vector, sv)
                vector.wait_ge(dse.h, dse.n)
                vector.wait_ge(dso.h, dso.n)
                V.tensor_tensor(tb[:, :], tb[:, :], tc[:, :], ALU.subtract)
                V.scalar_tensor_tensor(tb[:, :], tb[:, :], masks[:, i : i + 1], tc[:, :], ALU.mult, ALU.add)

            @blk.scalar
            def _(scalar, i=i):
                SS = SerEng(scalar, ss)
                scalar.wait_ge(sv.h, sv.n)
                scalar.wait_ge(dse.h, dse.n)
                scalar.wait_ge(dso.h, dso.n)
                SS.activation(out=sq[:, :], in_=tb[:, :], func=AF.Identity,
                              accum_out=stat[:, 0:1])
                SS.activation(out=sq[:, :], in_=tb[:, :], func=AF.Square,
                              accum_out=stat[:, 1:2])

            @blk.vector
            def _(vector, i=i):
                V = SerEng(vector, sv)
                vector.wait_ge(ss.h, ss.n)
                V.tensor_scalar(stat[:, 0:1], stat[:, 0:1], c_im, None, ALU.mult)
                V.tensor_scalar(stat[:, 1:2], stat[:, 1:2], c_im, None, ALU.mult)
                V.tensor_tensor(stat[:, 2:3], stat[:, 0:1], stat[:, 0:1], ALU.mult)
                V.tensor_tensor(stat[:, 2:3], stat[:, 1:2], stat[:, 2:3], ALU.subtract)

            @blk.scalar
            def _(scalar, i=i):
                SS = SerEng(scalar, ss)
                scalar.wait_ge(sv.h, sv.n)
                SS.activation(out=stat[:, 3:4], in_=stat[:, 2:3], func=AF.Sqrt, bias=c_eps)

            @blk.vector
            def _(vector, i=i):
                V = SerEng(vector, sv)
                vector.wait_ge(ss.h, ss.n)
                V.reciprocal(stat[:, 3:4], stat[:, 3:4])
                V.tensor_scalar(tc[:, :], tb[:, :], stat[:, 0:1], stat[:, 3:4], ALU.subtract, ALU.mult)
                V.tensor_tensor(ta[:, :], ta[:, :], tc[:, :], ALU.add)

            @blk.sync
            def _(sync, i=i, r0=r0):
                sync.wait_ge(sv.h, sv.n)
                dso.inc(sync.dma_start(out=x2_out[r0 : r0 + 128, :], in_=ta[:, :]), 16)

        @blk.sync
        def _(sync):
            sync.wait_ge(dse.h, dse.n)
            sync.wait_ge(dso.h, dso.n)

    p.pop_scope()


def ffn0_inmaps(x_full, yp0, yp1, wl_full, wh_full, bias, consts):
    maps = []
    for c in range(8):
        b, half = c // 2, c % 2
        rows = slice(half * 1024, (half + 1) * 1024)
        m = dict(consts)
        m["x_r"] = np.ascontiguousarray(x_full[b][rows])
        m["p0_r"] = np.ascontiguousarray(yp0[b][rows])
        m["p1_r"] = np.ascontiguousarray(yp1[b][rows])
        m["wl_f"] = wl_full
        m["wh_f"] = wh_full
        m["bias"] = bias
        maps.append(m)
    return maps


def emit_l1attn(p, x_in, wq, wk, wv, wo, cos_in, sin_in, am_in, am1_in,
                rotm_in, yp_out, xh_dram, q_dram, ao_dram):
    nc = p.nc
    p.push_scope("l1_")
    dse, dso, gse2, gsev = p.dse, p.dso, p.gse2, p.gsev
    sv, ss, st, sg = p.sv, p.ss, p.st, p.sg
    cosT = p.sbuf("cosT", [128, S], F32)
    sinT = p.sbuf("sinT", [128, S], F32)
    am_sb = p.sbuf("am_sb", [128, 256], F32)
    am1_sb = p.sbuf("am1_sb", [128, 256], F32)
    rotm_sb = p.sbuf("rotm_sb", [128, 128], BF16)
    qb16 = p.sbuf("qb16", [128, 512], BF16)
    ones_k = p.sbuf("ones_k", [128, 1], BF16)
    ones_1 = p.sbuf("ones_1", [1, 512], BF16)
    ks1 = p.sbuf("ks1", [128, S], BF16)
    vs1 = p.sbuf("vs1", [128, (S // 128) * HD], BF16)
    arA = p.sbuf("arA", [128, 16384], BF16)
    arB = p.sbuf("arB", [128, 32768], BF16)
    _xt0 = p.sbuf("xt0", [128, D], F32)
    xt = [_xt0, _xt0]
    sq = p.sbuf("sq", [128, D], BF16)
    _xh0 = p.sbuf("xhat0", [128, D], BF16)
    xhat = [_xh0, _xh0]
    stat = p.sbuf("stat", [128, 4], F32)
    qf = p.sbuf("qf", [128, 512], F32)
    qt0 = p.sbuf("qt0", [128, 512], F32)
    qrt = [p.sbuf(f"qrt{i}", [128, 512], BF16) for i in range(2)]
    qtile = p.sbuf("qtile", [128, 16 * 256], BF16)
    probs = [p.sbuf(f"probs{i}", [128, 512], BF16) for i in range(2)]
    srec = p.sbuf("srec", [1, 512], F32)
    srecb = [p.sbuf(f"srecb{i}", [1, 512], BF16) for i in range(2)]
    bc_sb = p.sbuf("bc_sb", [128, 512], F32)
    ow = [p.sbuf(f"ow{i}", [128, 512], F32) for i in range(2)]
    blt = [p.sbuf(f"blt{i}", [128, 512], F32) for i in range(2)]
    aoC = p.sbuf("aoC", [128, 16 * 256], BF16)
    _aot0 = p.sbuf("aot0", [128, 16 * 128], BF16)
    aot = [_aot0, _aot0]
    ypt = [p.sbuf(f"ypt{i}", [128, 512], F32) for i in range(2)]

    ps_a = p.psum("ps_a", [128, 512], F32)
    ps_b = p.psum("ps_b", [128, 512], F32)
    ps_sc = [p.psum(f"ps_sc{i}", [128, 512], F32) for i in range(2)]
    ps_rot = p.psum("ps_rot", [128, 512], F32)
    ps_pv = p.psum("ps_pv", [128, 512], F32)
    ps_sum = p.psum("ps_sum", [1, 512], F32)
    ps_bc = p.psum("ps_bc", [128, 512], F32)

    with nc.Block() as blk:
        @blk.sync
        def _(sync):
            dse.inc(sync.dma_start(out=cosT[:, :], in_=cos_in[:, :]), 16)
            dse.inc(sync.dma_start(out=sinT[:, :], in_=sin_in[:, :]), 16)
            dse.inc(sync.dma_start(out=am_sb[:, :], in_=am_in[:, :]), 16)
            dse.inc(sync.dma_start(out=am1_sb[:, :], in_=am1_in[:, :]), 16)
            dse.inc(sync.dma_start(out=rotm_sb[:, :], in_=rotm_in[:, :]), 16)

        @blk.gpsimd
        def _(gpsimd):
            G = SerEng(gpsimd, sg)
            G.memset(ones_k[:, :], 1.0)
            G.memset(ones_1[:, :], 1.0)
        boot_g = sg.n

        c_eps = p.cf(EPS)
        c_im = p.cf(1.0 / D)
        c_scale = p.cf(SCALE)

        A_ld, A_sq, A_var, A_sd, A_xh, A_st = [], [], [], [], [], []
        for i in range(16):
            bi = i % 2

            @blk.sync
            def _(sync, i=i, bi=bi):
                sync.wait_ge(dse.h, dse.n)
                sync.wait_ge(dso.h, dso.n)
                if i >= 1:
                    sync.wait_ge(sv.h, A_xh[i - 1])
                dse.inc(sync.dma_start(out=xt[bi][:, :], in_=x_in[i * 128 : (i + 1) * 128, :]), 16)
            A_ld.append(dse.n)

            @blk.scalar
            def _(scalar, i=i, bi=bi):
                SS = SerEng(scalar, ss)
                scalar.wait_ge(dse.h, dse.n)
                scalar.wait_ge(dso.h, dso.n)
                if i >= 1:
                    scalar.wait_ge(sv.h, A_var[i - 1])
                SS.activation(out=sq[:, :], in_=xt[bi][:, :], func=AF.Identity,
                              accum_out=stat[:, 0:1])
                SS.activation(out=sq[:, :], in_=xt[bi][:, :], func=AF.Square,
                              accum_out=stat[:, 1:2])
            A_sq.append(ss.n)

            @blk.vector
            def _(vector, i=i, bi=bi):
                V = SerEng(vector, sv)
                vector.wait_ge(ss.h, A_sq[i])
                V.tensor_scalar(stat[:, 0:1], stat[:, 0:1], c_im, None, ALU.mult)
                V.tensor_scalar(stat[:, 1:2], stat[:, 1:2], c_im, None, ALU.mult)
                V.tensor_tensor(stat[:, 2:3], stat[:, 0:1], stat[:, 0:1], ALU.mult)
                V.tensor_tensor(stat[:, 2:3], stat[:, 1:2], stat[:, 2:3], ALU.subtract)
            A_var.append(sv.n)

            @blk.scalar
            def _(scalar, i=i):
                SS = SerEng(scalar, ss)
                scalar.wait_ge(sv.h, A_var[i])
                SS.activation(out=stat[:, 3:4], in_=stat[:, 2:3], func=AF.Sqrt, bias=c_eps)
            A_sd.append(ss.n)

            @blk.vector
            def _(vector, i=i, bi=bi):
                V = SerEng(vector, sv)
                vector.wait_ge(ss.h, A_sd[i])
                V.reciprocal(stat[:, 2:3], stat[:, 3:4])
                vector.wait_ge(dse.h, dse.n)
                vector.wait_ge(dso.h, dso.n)
                V.tensor_scalar(
                    xhat[bi][:, :], xt[bi][:, :], stat[:, 0:1], stat[:, 2:3], ALU.subtract, ALU.mult)
            A_xh.append(sv.n)

            @blk.sync
            def _(sync, i=i, bi=bi):
                sync.wait_ge(sv.h, A_xh[i])
                dso.inc(sync.dma_start(out=xh_dram[i * 128 : (i + 1) * 128, :], in_=xhat[bi][:, :]), 16)
            A_st.append(dso.n)
        stageA_st = dso.n

        pe_tg = []
        q_ready = []
        kv_ready = []
        ev_marks = {}
        for tg in range(4):
            tg0 = tg * 512

            @blk.sync
            def _(sync, tg=tg):
                sync.wait_ge(dso.h, A_st[4 * tg + 3])
                if tg > 0:
                    sync.wait_ge(st.h, pe_tg[tg - 1])
                for c in range(32):
                    dse.inc(sync.dma_start(
                        out=view(arA, c * 512, [[1, 512]]),
                        in_=xh_dram[tg0 : tg0 + 512, c * 128 : (c + 1) * 128],
                        transpose=True), 16)
            xhT_ld = dse.n

            tg_ev0 = sv.n
            fgroups = [("k", 0)] + [("q", g) for g in range(4)]
            ev_idx = 0
            for gi, (kind, g) in enumerate(fgroups):
                sb = gi % 2
                win = wk if kind == "k" else wq
                ncols = 128 if kind == "k" else 512

                @blk.gpsimd
                def _(gpsimd, tg=tg, g=g, sb=sb, gi=gi, win=win, ncols=ncols):
                    gpsimd.wait_ge(sg.h, boot_g)
                    key = ("peg", tg, gi - 2)
                    if key in ev_marks:
                        gpsimd.wait_ge(st.h, ev_marks[key])
                    elif tg > 0:
                        gpsimd.wait_ge(st.h, pe_tg[tg - 1])
                    gse2[sb].inc(gpsimd.dma_start(
                        out=view(arB, sb * 16384, [[ncols, 32], [1, ncols]]),
                        in_=wstrip_in(win, g * 512 if kind == "q" else 0, ncols)), 16)
                ev_marks[("ld", tg, gi)] = gse2[sb].n

                nsub = 1 if kind == "k" else 4
                for sub in range(nsub):
                    f = g * 4 + sub
                    idx = ev_idx
                    ev_idx += 1
                    ps = ps_a if idx % 2 == 0 else ps_b

                    @blk.tensor
                    def _(tensor, tg=tg, gi=gi, idx=idx, ps=ps, sb=sb, sub=sub, ncols=ncols):
                        tensor.wait_ge(gse2[sb].h, ev_marks[("ld", tg, gi)])
                        tensor.wait_ge(dse.h, dse.n)
                        tensor.wait_ge(dso.h, dso.n)
                        key = ("ev", tg, idx - 2)
                        tensor.wait_ge(sv.h, ev_marks.get(key, tg_ev0))
                        for c in range(32):
                            mm = tensor.matmul(
                                ps[:, :],
                                view(arB, sb * 16384 + c * ncols + sub * 128, [[1, 128]]),
                                view(arA, c * 512, [[1, 512]]),
                                start=(c == 0), stop=(c == 31))
                        st.inc(mm)
                    ev_marks[("peg", tg, gi)] = st.n
                    ev_marks[("pemm", tg, idx)] = st.n

                    @blk.vector
                    def _(vector, tg=tg, idx=idx, ps=ps):
                        V = SerEng(vector, sv)
                        vector.wait_ge(st.h, ev_marks[("pemm", tg, idx)])
                        V.tensor_copy(qb16[:, :], ps[:, :])
                    v1 = sv.n

                    @blk.tensor
                    def _(tensor, v1=v1):
                        tensor.wait_ge(sv.h, v1)
                        st.inc(tensor.matmul(ps_rot[:, :], rotm_sb[:, :], qb16[:, :],
                                             start=True, stop=True))
                    t1 = st.n

                    @blk.vector
                    def _(vector, tg=tg, f=f, kind=kind, idx=idx, ps=ps, tg0=tg0, t1=t1):
                        V = SerEng(vector, sv)
                        vector.wait_ge(st.h, t1)
                        V.tensor_tensor(qt0[:, :], ps_rot[:, :], sinT[:, tg0 : tg0 + 512], ALU.mult)
                        V.tensor_tensor(qf[:, :], ps[:, :], cosT[:, tg0 : tg0 + 512], ALU.mult)
                        if kind == "k":
                            dst = ks1[:, tg0 : tg0 + 512]
                            V.tensor_tensor(dst, qf[:, :], qt0[:, :], ALU.add)
                        else:
                            par = idx % 2
                            vector.wait_ge(dse.h, dse.n)
                            vector.wait_ge(dso.h, dso.n)
                            V.tensor_tensor(qrt[par][:, :], qf[:, :], qt0[:, :], ALU.add)
                    ev_marks[("ev", tg, idx)] = sv.n

                    if kind == "q":
                        @blk.sync
                        def _(sync, tg=tg, f=f, idx=idx, tg0=tg0):
                            sync.wait_ge(sv.h, ev_marks[("ev", tg, idx)])
                            dso.inc(sync.dma_start(
                                out=q_dram[f * 128 : (f + 1) * 128, tg0 : tg0 + 512],
                                in_=qrt[idx % 2][:, :]), 16)

            @blk.gpsimd
            def _(gpsimd, tg=tg):
                gpsimd.wait_ge(st.h, st.n)
                gsev.inc(gpsimd.dma_start(
                    out=view(arB, 0, [[128, 32], [1, 128]]),
                    in_=wstrip_in(wv, 0, 128)), 16)
            vld = gsev.n

            for sub in range(4):
                @blk.tensor
                def _(tensor, tg=tg, sub=sub, vld=vld):
                    tensor.wait_ge(gsev.h, vld)
                    tensor.wait_ge(sv.h, sv.n)
                    for c in range(32):
                        mm = tensor.matmul(
                            ps_a[0:128, 0:128],
                            view(arA, c * 512 + sub * 128, [[1, 128]]),
                            view(arB, c * 128, [[1, 128]]),
                            start=(c == 0), stop=(c == 31))
                    st.inc(mm)
                vpe = st.n

                @blk.vector
                def _(vector, tg=tg, sub=sub, vpe=vpe):
                    V = SerEng(vector, sv)
                    vector.wait_ge(st.h, vpe)
                    tt = tg * 4 + sub
                    V.tensor_copy(vs1[:, tt * HD : (tt + 1) * HD], ps_a[0:128, 0:128])
            pe_tg.append(st.n)
            q_ready.append(dso.n)
            kv_ready.append(sv.n)
        stageB_v = sv.n

        expc = {0: 0, 1: 0}
        pvc = {0: 0, 1: 0}
        last_ao_ev = 0
        wcount = 0
        last_ev_par = [sv.n, sv.n]
        pv_b = [ps_pv, ps_a]
        bc_b = [ps_bc, ps_b]
        for ch in range(8):
            if ch == 0:
                wins = [(0, 4, 1)]
            elif ch < 7:
                wins = [(2 * (ch - 1), 4, 0), (2 * ch, 4, 1)]
            else:
                wins = [(12, 4, 0), (14, 2, 1)]

            @blk.sync
            def _(sync, ch=ch):
                tg_need = min((2 * ch + 3) // 4, 3)
                sync.wait_ge(sv.h, kv_ready[max(tg_need, ch // 2)])
                sync.wait_ge(dso.h, q_ready[ch // 2])
                sync.wait_ge(st.h, st.n)
                for j in range(16):
                    dse.inc(sync.dma_start(
                        out=qtile[:, j * 256 : (j + 1) * 256],
                        in_=q_dram[j * 128 : (j + 1) * 128, ch * 256 : (ch + 1) * 256]), 16)
            qld = dse.n

            for j in range(0, 16, 2):
                for (kt0, nkts, wslot) in wins:
                    par = wcount % 2
                    pvx = pv_b[par]
                    bcx = bc_b[par]
                    sumx = ps_sum[0:1, :] if par == 0 else ps_rot[0:1, :]
                    for ki in range(nkts):
                        kt = kt0 + ki
                        pp = kt % 2

                        @blk.tensor
                        def _(tensor, j=j, kt=kt, pp=pp, qld=qld):
                            tensor.wait_ge(dse.h, dse.n)
                            tensor.wait_ge(dso.h, dso.n)
                            tensor.wait_ge(ss.h, expc[pp])
                            st.inc(tensor.matmul(
                                ps_sc[pp][:, :],
                                ks1[:, kt * 128 : (kt + 1) * 128],
                                qtile[:, j * 256 : (j + 2) * 256],
                                start=True, stop=True))
                        scd = st.n

                        @blk.scalar
                        def _(scalar, pp=pp, scd=scd):
                            SS = SerEng(scalar, ss)
                            scalar.wait_ge(st.h, scd)
                            scalar.wait_ge(st.h, pvc[pp])
                            SS.activation(
                                out=probs[pp][:, :], in_=ps_sc[pp][:, :],
                                func=AF.Sigmoid, scale=c_scale)
                        expc[pp] = ss.n
                        expd = ss.n

                        @blk.tensor
                        def _(tensor, kt=kt, pp=pp, expd=expd, ki=ki, nkts=nkts,
                              pvx=pvx, sumx=sumx, par=par):
                            tensor.wait_ge(ss.h, expd)
                            if ki == 0:
                                tensor.wait_ge(sv.h, last_ev_par[par])
                            tensor.matmul(
                                pvx[:, :],
                                vs1[:, kt * HD : (kt + 1) * HD],
                                probs[pp][:, :],
                                start=(ki == 0), stop=(ki == nkts - 1))
                            st.inc(tensor.matmul(
                                sumx, ones_k[:, :], probs[pp][:, :],
                                start=(ki == 0), stop=(ki == nkts - 1)))
                        pvc[pp] = st.n
                    pvdone = st.n

                    @blk.vector
                    def _(vector, pvdone=pvdone, sumx=sumx, par=par):
                        V = SerEng(vector, sv)
                        vector.wait_ge(st.h, pvdone)
                        V.reciprocal(srec[:, :], sumx)
                        V.tensor_copy(srecb[par][:, :], srec[:, :])
                    recd = sv.n

                    @blk.tensor
                    def _(tensor, recd=recd, bcx=bcx, par=par):
                        tensor.wait_ge(sv.h, recd)
                        tensor.wait_ge(sv.h, last_ev_par[par])
                        st.inc(tensor.matmul(
                            bcx[:, :], ones_1[0:1, 0:128], srecb[par][0:1, :],
                            start=True, stop=True))
                    bcd = st.n

                    @blk.vector
                    def _(vector, wslot=wslot, bcd=bcd, pvx=pvx, bcx=bcx):
                        V = SerEng(vector, sv)
                        vector.wait_ge(st.h, bcd)
                        V.tensor_copy(bc_sb[:, :], bcx[:, :])
                        V.tensor_tensor(ow[wslot][:, :], pvx[:, :], bc_sb[:, :], ALU.mult)
                    last_ev_par[par] = sv.n
                    last_ao_ev = sv.n
                    wcount += 1

                am2 = bass.AP(am_sb, 0, [[256, 128], [0, 2], [1, 256]])
                am12 = bass.AP(am1_sb, 0, [[256, 128], [0, 2], [1, 256]])

                @blk.vector
                def _(vector, ch=ch, j=j, nw=len(wins), am2=am2, am12=am12):
                    V = SerEng(vector, sv)
                    vector.wait_ge(dse.h, dse.n)
                    vector.wait_ge(dso.h, dso.n)
                    dst = aoC[:, j * 256 : (j + 2) * 256]
                    if nw == 1:
                        V.tensor_copy(dst, ow[1][:, :])
                    else:
                        V.tensor_tensor(blt[0][:, :], ow[0][:, :], am12, ALU.mult)
                        V.tensor_tensor(blt[1][:, :], ow[1][:, :], am2, ALU.mult)
                        V.tensor_tensor(dst, blt[0][:, :], blt[1][:, :], ALU.add)
                last_ao_ev = sv.n

                @blk.sync
                def _(sync, ch=ch, j=j, aoev=last_ao_ev):
                    sync.wait_ge(sv.h, aoev)
                    dso.inc(sync.dma_start(
                        out=bass.AP(ao_dram, (j * 128) * S + ch * 256,
                                    [[S, 128], [128 * S, 2], [1, 256]]),
                        in_=bass.AP(aoC, j * 256, [[4096, 128], [256, 2], [1, 256]])), 16)
        stageC_ao = dse.n

        aold = {}
        ypst = {}
        for fg in range(8):
            @blk.gpsimd
            def _(gpsimd, fg=fg):
                gpsimd.wait_ge(st.h, st.n)
                gsev.inc(gpsimd.dma_start(
                    out=view(arB, 0, [[512, 16], [1, 512]]),
                    in_=bass.AP(wo, fg * 512, [[D, 128], [128 * D, 16], [1, 512]])), 16)
            wold = gsev.n

            for tt in range(16):
                par = tt % 2

                @blk.sync
                def _(sync, fg=fg, tt=tt, par=par):
                    if fg == 0 and tt == 0:
                        sync.wait_ge(dse.h, dse.n)
                        sync.wait_ge(dso.h, dso.n)
                    key = ("pe", fg, tt - 1) if tt >= 1 else ("pe", fg - 1, tt + 15)
                    if key in aold:
                        sync.wait_ge(st.h, aold[key])
                    dse.inc(sync.dma_start(
                        out=aot[par][:, :],
                        in_=bass.AP(ao_dram, tt * 128, [[S, 128], [128 * S, 16], [1, 128]])), 16)
                aold[("ld", fg, tt)] = dse.n

                @blk.tensor
                def _(tensor, fg=fg, tt=tt, par=par, wold=wold):
                    tensor.wait_ge(gsev.h, wold)
                    tensor.wait_ge(dse.h, dse.n)
                    tensor.wait_ge(dso.h, dso.n)
                    tensor.wait_ge(sv.h, sv.n)
                    for c in range(16):
                        mm = tensor.matmul(
                            ps_a[:, :],
                            view(aot[par], c * 128, [[1, 128]]),
                            view(arB, c * 512, [[1, 512]]),
                            start=(c == 0), stop=(c == 15))
                    st.inc(mm)
                aold[("pe", fg, tt)] = st.n

                @blk.vector
                def _(vector, fg=fg, tt=tt, par=par):
                    V = SerEng(vector, sv)
                    vector.wait_ge(st.h, aold[("pe", fg, tt)])
                    vector.wait_ge(dse.h, dse.n)
                    vector.wait_ge(dso.h, dso.n)
                    V.tensor_copy(ypt[par][:, :], ps_a[:, :])
                ypev = sv.n

                @blk.sync
                def _(sync, fg=fg, tt=tt, par=par, ypev=ypev):
                    sync.wait_ge(sv.h, ypev)
                    dso.inc(sync.dma_start(
                        out=yp_out[tt * 128 : (tt + 1) * 128, fg * 512 : (fg + 1) * 512],
                        in_=ypt[par][:, :]), 16)
                ypst[("st", fg, tt)] = dse.n

    p.pop_scope()


def l1attn_inmaps(x2_full, Wq, Wk, Wv, Wo, consts):
    cos, sin = _rope_l1()
    am, am1 = _alpha_tables()
    WqT = np.ascontiguousarray(Wq.T)
    WkT = np.ascontiguousarray(Wk.T)
    WvT = np.ascontiguousarray(Wv.T)
    WoT = np.ascontiguousarray(Wo.T)
    maps = []
    for c in range(8):
        b, hh = c // 2, c % 2
        m = dict(consts)
        m["xb2"] = np.ascontiguousarray(x2_full[b])
        m["wqT_s"] = np.ascontiguousarray(WqT[:, hh * 2048 : (hh + 1) * 2048])
        m["wkT"] = WkT
        m["wvT"] = WvT
        m["woT_s"] = np.ascontiguousarray(WoT[hh * 2048 : (hh + 1) * 2048, :])
        m["cos1_t"] = cos
        m["sin1_t"] = sin
        m["am_t"] = am
        m["am1_t"] = am1
        m["rotm_t"] = _rotm_l1()
        maps.append(m)
    return maps


def emit_ffn1(p, x_in, ps_in, w1_in, bias_in, x4_out,
              x3_dram, xh3_dram, yd_dram):
    nc = p.nc
    p.push_scope("f1_")
    dse, dso, sv, ss, st, cc = p.dse, p.dso, p.sv, p.ss, p.st, p.cc
    ln2T = p.sbuf("ln2T", [128, 32 * 1024], BF16)
    arW = p.sbuf("arW", [128, 2 * 32 * 512], BF16)
    ta = p.sbuf("ta", [128, D], F32)
    tb = p.sbuf("tb", [128, D], F32)
    tc = p.sbuf("tc", [128, D], F32)
    sq = p.sbuf("sq", [128, D], BF16)
    stat = p.sbuf("stat", [128, 8], F32)
    biasbc = p.sbuf("biasbc", [128, 1024], F32)
    ypt = [p.sbuf(f"ypt{i}", [128, 512], F32) for i in range(2)]
    ps_a = p.psum("ps_a", [128, 512], F32)
    ps_b = p.psum("ps_b", [128, 512], F32)

    with nc.Block() as blk:
        c_eps = p.cf(EPS)
        c_im = p.cf(1.0 / D)

        for i in range(8):
            r0 = i * 128

            @blk.sync
            def _(sync, i=i, r0=r0):
                sync.wait_ge(sv.h, sv.n)
                sync.wait_ge(dse.h, dse.n)
                sync.wait_ge(dso.h, dso.n)
                dse.inc(sync.dma_start(out=ta[:, :], in_=x_in[r0 : r0 + 128, :]), 16)
                dse.inc(sync.dma_start(out=tb[:, :], in_=ps_in[r0 : r0 + 128, :]), 16)

            @blk.scalar
            def _(scalar, i=i):
                SS = SerEng(scalar, ss)
                scalar.wait_ge(sv.h, sv.n)
                scalar.wait_ge(dse.h, dse.n)
                scalar.wait_ge(dso.h, dso.n)
                SS.activation(out=sq[:, :], in_=tb[:, :], func=AF.Identity,
                              accum_out=stat[:, 0:1])
                SS.activation(out=sq[:, :], in_=tb[:, :], func=AF.Square,
                              accum_out=stat[:, 1:2])

            @blk.vector
            def _(vector, i=i):
                V = SerEng(vector, sv)
                vector.wait_ge(ss.h, ss.n)
                V.tensor_scalar(stat[:, 0:1], stat[:, 0:1], c_im, None, ALU.mult)
                V.tensor_scalar(stat[:, 1:2], stat[:, 1:2], c_im, None, ALU.mult)
                V.tensor_tensor(stat[:, 2:3], stat[:, 0:1], stat[:, 0:1], ALU.mult)
                V.tensor_tensor(stat[:, 2:3], stat[:, 1:2], stat[:, 2:3], ALU.subtract)

            @blk.scalar
            def _(scalar, i=i):
                SS = SerEng(scalar, ss)
                scalar.wait_ge(sv.h, sv.n)
                SS.activation(out=stat[:, 3:4], in_=stat[:, 2:3], func=AF.Sqrt, bias=c_eps)

            @blk.vector
            def _(vector, i=i, r0=r0):
                V = SerEng(vector, sv)
                vector.wait_ge(ss.h, ss.n)
                V.reciprocal(stat[:, 3:4], stat[:, 3:4])
                V.tensor_scalar(tc[:, :], tb[:, :], stat[:, 0:1], stat[:, 3:4], ALU.subtract, ALU.mult)
                V.tensor_tensor(ta[:, :], ta[:, :], tc[:, :], ALU.add)

            @blk.sync
            def _(sync, i=i, r0=r0):
                sync.wait_ge(sv.h, sv.n)
                dso.inc(sync.dma_start(out=x3_dram[r0 : r0 + 128, :], in_=ta[:, :]), 16)

            @blk.scalar
            def _(scalar, i=i):
                SS = SerEng(scalar, ss)
                scalar.wait_ge(sv.h, sv.n)
                scalar.wait_ge(dse.h, dse.n)
                scalar.wait_ge(dso.h, dso.n)
                SS.activation(out=sq[:, :], in_=ta[:, :], func=AF.Identity,
                              accum_out=stat[:, 0:1])
                SS.activation(out=sq[:, :], in_=ta[:, :], func=AF.Square,
                              accum_out=stat[:, 1:2])

            @blk.vector
            def _(vector, i=i):
                V = SerEng(vector, sv)
                vector.wait_ge(ss.h, ss.n)
                V.tensor_scalar(stat[:, 0:1], stat[:, 0:1], c_im, None, ALU.mult)
                V.tensor_scalar(stat[:, 1:2], stat[:, 1:2], c_im, None, ALU.mult)
                V.tensor_tensor(stat[:, 2:3], stat[:, 0:1], stat[:, 0:1], ALU.mult)
                V.tensor_tensor(stat[:, 2:3], stat[:, 1:2], stat[:, 2:3], ALU.subtract)

            @blk.scalar
            def _(scalar, i=i):
                SS = SerEng(scalar, ss)
                scalar.wait_ge(sv.h, sv.n)
                SS.activation(out=stat[:, 3:4], in_=stat[:, 2:3], func=AF.Sqrt, bias=c_eps)

            @blk.vector
            def _(vector, i=i):
                V = SerEng(vector, sv)
                vector.wait_ge(ss.h, ss.n)
                V.reciprocal(stat[:, 3:4], stat[:, 3:4])
                V.tensor_scalar(sq[:, :], ta[:, :], stat[:, 0:1], stat[:, 3:4], ALU.subtract, ALU.mult)

            @blk.sync
            def _(sync, i=i, r0=r0):
                sync.wait_ge(sv.h, sv.n)
                dso.inc(sync.dma_start(out=xh3_dram[r0 : r0 + 128, :], in_=sq[:, :]), 16)

        @blk.sync
        def _(sync):
            sync.wait_ge(dse.h, dse.n)
            sync.wait_ge(dso.h, dso.n)
            for c in range(32):
                dse.inc(sync.dma_start(
                    out=view(ln2T, c * 1024, [[1, 1024]]),
                    in_=xh3_dram[0:1024, c * 128 : (c + 1) * 128], transpose=True), 16)

        wmk = {}
        for og in range(8):
            wh_ = og % 2

            @blk.sync
            def _(sync, og=og, wh_=wh_):
                if og == 0:
                    sync.wait_ge(dso.h, dso.n)
                    sync.wait_ge(cc.h, cc.n)
                key = ("pe", og - 2)
                if key in wmk:
                    sync.wait_ge(st.h, wmk[key])
                for c in range(32):
                    dse.inc(sync.dma_start(
                        out=view(arW, wh_ * 16384 + c * 512, [[1, 512]]),
                        in_=w1_in[og * 512 : (og + 1) * 512, c * 128 : (c + 1) * 128],
                        transpose=True), 16)
                dse.inc(sync.dma_start(
                    out=biasbc[:, wh_ * 512 : wh_ * 512 + 512],
                    in_=bass.AP(bias_in, og * 512, [[0, 128], [1, 512]])), 16)
            wld = dse.n

            for tsub in range(8):
                pp = tsub % 2
                ps = ps_a if pp == 0 else ps_b

                @blk.tensor
                def _(tensor, og=og, tsub=tsub, ps=ps, wh_=wh_, wld=wld):
                    tensor.wait_ge(dse.h, wld)
                    tensor.wait_ge(sv.h, sv.n)
                    for c in range(32):
                        mm = tensor.matmul(
                            ps[:, :],
                            view(ln2T, c * 1024 + tsub * 128, [[1, 128]]),
                            view(arW, wh_ * 16384 + c * 512, [[1, 512]]),
                            start=(c == 0), stop=(c == 31))
                    st.inc(mm)
                ped = st.n

                @blk.vector
                def _(vector, tsub=tsub, ped=ped, pp=pp, ps=ps, wh_=wh_):
                    V = SerEng(vector, sv)
                    vector.wait_ge(st.h, ped)
                    vector.wait_ge(dso.h, dso.n)
                    V.tensor_tensor(ypt[pp][:, :], ps[:, :],
                                    biasbc[:, wh_ * 512 : wh_ * 512 + 512], ALU.add)
                yev = sv.n

                @blk.sync
                def _(sync, og=og, tsub=tsub, yev=yev, pp=pp):
                    sync.wait_ge(sv.h, yev)
                    dso.inc(sync.dma_start(
                        out=yd_dram[tsub * 128 : (tsub + 1) * 128, og * 512 : (og + 1) * 512],
                        in_=ypt[pp][:, :]), 16)
            wmk[("pe", og)] = st.n

        for i in range(8):
            r0 = i * 128

            @blk.sync
            def _(sync, i=i, r0=r0):
                sync.wait_ge(dse.h, dse.n)
                sync.wait_ge(dso.h, dso.n)
                sync.wait_ge(sv.h, sv.n)
                dse.inc(sync.dma_start(out=ta[:, :], in_=x3_dram[r0 : r0 + 128, :]), 16)
                dse.inc(sync.dma_start(out=tb[:, :], in_=yd_dram[r0 : r0 + 128, :]), 16)

            @blk.scalar
            def _(scalar, i=i):
                SS = SerEng(scalar, ss)
                scalar.wait_ge(dse.h, dse.n)
                scalar.wait_ge(dso.h, dso.n)
                scalar.wait_ge(sv.h, sv.n)
                SS.activation(out=sq[:, :], in_=tb[:, :], func=AF.Identity,
                              accum_out=stat[:, 0:1])
                SS.activation(out=sq[:, :], in_=tb[:, :], func=AF.Square,
                              accum_out=stat[:, 1:2])

            @blk.vector
            def _(vector, i=i):
                V = SerEng(vector, sv)
                vector.wait_ge(dse.h, dse.n)
                vector.wait_ge(dso.h, dso.n)
                vector.wait_ge(ss.h, ss.n)
                V.tensor_scalar(stat[:, 0:1], stat[:, 0:1], c_im, None, ALU.mult)
                V.tensor_scalar(stat[:, 1:2], stat[:, 1:2], c_im, None, ALU.mult)
                V.tensor_tensor(stat[:, 2:3], stat[:, 0:1], stat[:, 0:1], ALU.mult)
                V.tensor_tensor(stat[:, 2:3], stat[:, 1:2], stat[:, 2:3], ALU.subtract)

            @blk.scalar
            def _(scalar, i=i):
                SS = SerEng(scalar, ss)
                scalar.wait_ge(sv.h, sv.n)
                SS.activation(out=stat[:, 3:4], in_=stat[:, 2:3], func=AF.Sqrt, bias=c_eps)

            @blk.vector
            def _(vector, i=i):
                V = SerEng(vector, sv)
                vector.wait_ge(ss.h, ss.n)
                V.reciprocal(stat[:, 3:4], stat[:, 3:4])
                V.tensor_scalar(tc[:, :], tb[:, :], stat[:, 0:1], stat[:, 3:4], ALU.subtract, ALU.mult)
                V.tensor_tensor(ta[:, :], ta[:, :], tc[:, :], ALU.add)

            @blk.sync
            def _(sync, i=i, r0=r0):
                sync.wait_ge(sv.h, sv.n)
                dso.inc(sync.dma_start(out=x4_out[r0 : r0 + 128, :], in_=ta[:, :]), 16)

        @blk.sync
        def _(sync):
            sync.wait_ge(dse.h, dse.n)
            sync.wait_ge(dso.h, dso.n)

    p.pop_scope()


def ffn1_inmaps(x2_full, yp0, yp1, w1_full, bias, consts):
    maps = []
    for c in range(8):
        b, half = c // 2, c % 2
        rows = slice(half * 1024, (half + 1) * 1024)
        m = dict(consts)
        m["x_r"] = np.ascontiguousarray(x2_full[b][rows])
        m["p0_r"] = np.ascontiguousarray(yp0[b][rows])
        m["p1_r"] = np.ascontiguousarray(yp1[b][rows])
        m["w1_f"] = w1_full
        m["bias"] = bias
        maps.append(m)
    return maps


def build_fused():
    p = PB()
    nc = p.nc
    EI, IN = "ExternalInput", "Internal"
    xb = nc.dram_tensor("xb", [S, D], F32, kind=EI)
    x_half = nc.dram_tensor("x_half", [1024, D], F32, kind=EI)
    wq_s = nc.dram_tensor("wq_s", [D, 2048], F32, kind=EI)
    wk_s = nc.dram_tensor("wk_s", [D, 512], F32, kind=EI)
    wv_s = nc.dram_tensor("wv_s", [D, 512], F32, kind=EI)
    wo_s = nc.dram_tensor("wo_s", [2048, D], F32, kind=EI)
    cos_t = nc.dram_tensor("cos_t", [HD, S], F32, kind=EI)
    sin_t = nc.dram_tensor("sin_t", [HD, S], F32, kind=EI)
    tril_t = nc.dram_tensor("tril_t", [4 * 128, 512], BF16, kind=EI)
    qw_s = nc.dram_tensor("qw_s", [8, 512, 128], I32, kind=EI)
    lut4_s = nc.dram_tensor("lut4_s", [512, 16], F32, kind=EI)
    lut8_s = nc.dram_tensor("lut8_s", [512, 256], F32, kind=EI)
    w1q_s = nc.dram_tensor("w1q_s", [512, D], I32, kind=EI)
    sc_s = nc.dram_tensor("sc_s", [512, 32], F32, kind=EI)
    zr_s = nc.dram_tensor("zr_s", [512, 32], F32, kind=EI)
    mk_s = nc.dram_tensor("mk_s", [512, D], I32, kind=EI)
    s2_s = nc.dram_tensor("s2_s", [512, 1], F32, kind=EI)
    shf = nc.dram_tensor("shf", [128, 32], I32, kind=EI)
    kramp = nc.dram_tensor("kramp", [128, 256], F32, kind=EI)
    bias0 = nc.dram_tensor("bias0", [D], F32, kind=EI)
    bias1 = nc.dram_tensor("bias1", [D], F32, kind=EI)
    wqT_s = nc.dram_tensor("wqT_s", [D, 2048], F32, kind=EI)
    wkT = nc.dram_tensor("wkT", [D, HD], F32, kind=EI)
    wvT = nc.dram_tensor("wvT", [D, HD], F32, kind=EI)
    woT_s = nc.dram_tensor("woT_s", [2048, D], F32, kind=EI)
    cos1_t = nc.dram_tensor("cos1_t", [HD, S], F32, kind=EI)
    sin1_t = nc.dram_tensor("sin1_t", [HD, S], F32, kind=EI)
    am_t = nc.dram_tensor("am_t", [128, 256], F32, kind=EI)
    am1_t = nc.dram_tensor("am1_t", [128, 256], F32, kind=EI)
    rotm_t = nc.dram_tensor("rotm_t", [128, 128], BF16, kind=EI)
    x4_r = nc.dram_tensor("x4_r", [1024, D], F32, kind="ExternalOutput")
    wl_sl = nc.dram_tensor("wl_sl", [512, D], BF16, kind=IN)
    wh_sl = nc.dram_tensor("wh_sl", [512, D], BF16, kind=IN)
    w1_sl = nc.dram_tensor("w1_sl", [512, D], BF16, kind=IN)
    wl_full = nc.dram_tensor("wl_full", [D, D], BF16, kind=IN)
    wh_full = nc.dram_tensor("wh_full", [D, D], BF16, kind=IN)
    w1_full = nc.dram_tensor("w1_full", [D, D], BF16, kind=IN)
    xh0_d = nc.dram_tensor("xh0_d", [S, D], BF16, kind=IN)
    q0_d = nc.dram_tensor("q0_d", [2048, S], BF16, kind=IN)
    ao0_d = nc.dram_tensor("ao0_d", [2048, S], BF16, kind=IN)
    yp0_d = nc.dram_tensor("yp0_d", [S, D], F32, kind=IN)
    psum0_d = nc.dram_tensor("psum0_d", [1024, D], F32, kind=IN)
    x1_d = nc.dram_tensor("x1_d", [1024, D], F32, kind=IN)
    xh2_d = nc.dram_tensor("xh2_d", [1024, D], BF16, kind=IN)
    ydl_d = nc.dram_tensor("ydl_d", [1024, D], F32, kind=IN)
    ydh_d = nc.dram_tensor("ydh_d", [1024, D], F32, kind=IN)
    x2h_d = nc.dram_tensor("x2h_d", [1024, D], F32, kind=IN)
    x2f_d = nc.dram_tensor("x2f_d", [S, D], F32, kind=IN)
    xh1_d = nc.dram_tensor("xh1_d", [S, D], BF16, kind=IN)
    q1_d = nc.dram_tensor("q1_d", [2048, S], BF16, kind=IN)
    ao1_d = nc.dram_tensor("ao1_d", [2048, S], BF16, kind=IN)
    yp1_d = nc.dram_tensor("yp1_d", [S, D], F32, kind=IN)
    psum1_d = nc.dram_tensor("psum1_d", [1024, D], F32, kind=IN)
    x3_d = nc.dram_tensor("x3_d", [1024, D], F32, kind=IN)
    xh3_d = nc.dram_tensor("xh3_d", [1024, D], BF16, kind=IN)
    yd_d = nc.dram_tensor("yd_d", [1024, D], F32, kind=IN)

    p.alloc_consts()
    p.make_sems()
    G8 = [list(range(8))]
    GP = [[0, 1], [2, 3], [4, 5], [6, 7]]

    with nc.Block() as blk:
        @blk.sync
        def _(sync):
            p.load_consts(sync, p.dse)

    emit_dequant(p, qw_s, lut4_s, lut8_s, w1q_s, sc_s, zr_s, mk_s, s2_s,
                 shf, kramp, wl_sl, wh_sl, w1_sl)

    with nc.Block() as blk:
        @blk.gpsimd
        def _(g):
            g.wait_ge(p.dse.h, p.dse.n)
            g.wait_ge(p.dso.h, p.dso.n)
            p.cc.inc(g.collective_compute(
                "AllGather", ALU.bypass, replica_groups=G8,
                ins=[wl_sl.ap().opt()], outs=[wl_full.ap().opt()]))
            p.cc.inc(g.collective_compute(
                "AllGather", ALU.bypass, replica_groups=G8,
                ins=[wh_sl.ap().opt()], outs=[wh_full.ap().opt()]))
            p.cc.inc(g.collective_compute(
                "AllGather", ALU.bypass, replica_groups=G8,
                ins=[w1_sl.ap().opt()], outs=[w1_full.ap().opt()]))
    p.barrier(skip_cc=True)

    emit_l0attn(p, xb, wq_s, wk_s, wv_s, wo_s, cos_t, sin_t, tril_t, yp0_d,
                xh0_d, q0_d, ao0_d)

    with nc.Block() as blk:
        @blk.gpsimd
        def _(g):
            g.wait_ge(p.dse.h, p.dse.n)
            g.wait_ge(p.dso.h, p.dso.n)
            p.cc.inc(g.collective_compute(
                "ReduceScatter", ALU.add, replica_groups=GP,
                ins=[yp0_d.ap().opt()], outs=[psum0_d.ap().opt()]))
    p.barrier()

    emit_ffn0(p, x_half, psum0_d, wl_full, wh_full, bias0, x2h_d,
              x1_d, xh2_d, ydl_d, ydh_d)

    with nc.Block() as blk:
        @blk.gpsimd
        def _(g):
            g.wait_ge(p.dse.h, p.dse.n)
            g.wait_ge(p.dso.h, p.dso.n)
            p.cc.inc(g.collective_compute(
                "AllGather", ALU.bypass, replica_groups=GP,
                ins=[x2h_d.ap().opt()], outs=[x2f_d.ap().opt()]))
    p.barrier()

    emit_l1attn(p, x2f_d, wqT_s, wkT, wvT, woT_s, cos1_t, sin1_t, am_t, am1_t,
                rotm_t, yp1_d, xh1_d, q1_d, ao1_d)

    with nc.Block() as blk:
        @blk.gpsimd
        def _(g):
            g.wait_ge(p.dse.h, p.dse.n)
            g.wait_ge(p.dso.h, p.dso.n)
            p.cc.inc(g.collective_compute(
                "ReduceScatter", ALU.add, replica_groups=GP,
                ins=[yp1_d.ap().opt()], outs=[psum1_d.ap().opt()]))
    p.barrier()

    emit_ffn1(p, x2h_d, psum1_d, w1_full, bias1, x4_r,
              x3_d, xh3_d, yd_d)

    p.barrier()
    p.close()
    return p


def fused_inmaps(z, consts):
    x = np.asarray(z["L0_attn_x"], dtype=np.float32)
    w_qkv = np.asarray(z["L0_attn_w_qkv"], dtype=np.float32)
    w_o = np.asarray(z["L0_attn_w_o"], dtype=np.float32)
    cos0, sin0 = _rope_l0()
    tril = _tril_masks()
    cos1, sin1 = _rope_l1()
    am, am1 = _alpha_tables()
    rotm = _rotm_l1()
    WqT = np.ascontiguousarray(np.asarray(z["L1_attn_Wq"], dtype=np.float32).T)
    WkT = np.ascontiguousarray(np.asarray(z["L1_attn_Wk"], dtype=np.float32).T)
    WvT = np.ascontiguousarray(np.asarray(z["L1_attn_Wv"], dtype=np.float32).T)
    WoT = np.ascontiguousarray(np.asarray(z["L1_attn_Wo"], dtype=np.float32).T)
    shf_a = np.broadcast_to(np.arange(32, dtype=np.int32), (128, 32)).copy()
    kramp_a = np.broadcast_to(np.arange(256, dtype=np.float32), (128, 256)).copy()
    maps = []
    for c in range(8):
        b, hh = c // 2, c % 2
        r = slice(c * 512, (c + 1) * 512)
        m = dict(consts)
        m["xb"] = np.ascontiguousarray(x[b])
        m["x_half"] = np.ascontiguousarray(x[b][hh * 1024 : (hh + 1) * 1024])
        m["wq_s"] = np.ascontiguousarray(w_qkv[:, hh * 2048 : (hh + 1) * 2048])
        m["wk_s"] = np.ascontiguousarray(w_qkv[:, 4096 + hh * 512 : 4096 + (hh + 1) * 512])
        m["wv_s"] = np.ascontiguousarray(w_qkv[:, 5120 + hh * 512 : 5120 + (hh + 1) * 512])
        m["wo_s"] = np.ascontiguousarray(w_o[hh * 2048 : (hh + 1) * 2048, :])
        m["cos_t"] = cos0
        m["sin_t"] = sin0
        m["tril_t"] = tril
        m["qw_s"] = np.ascontiguousarray(np.asarray(z["L0_ffn_qweight"])[:, r, :])
        m["lut4_s"] = np.ascontiguousarray(np.asarray(z["L0_ffn_lut4"], dtype=np.float32)[r])
        m["lut8_s"] = np.ascontiguousarray(np.asarray(z["L0_ffn_lut8"], dtype=np.float32)[r])
        m["w1q_s"] = np.ascontiguousarray(np.asarray(z["L1_ffn_W_q"])[r])
        m["sc_s"] = np.ascontiguousarray(np.asarray(z["L1_ffn_scales"], dtype=np.float32)[r, :, 0])
        m["zr_s"] = np.ascontiguousarray(np.asarray(z["L1_ffn_zeros"], dtype=np.float32)[r, :, 0])
        m["mk_s"] = np.ascontiguousarray(np.asarray(z["L1_ffn_mask"])[r])
        m["s2_s"] = np.ascontiguousarray(np.asarray(z["L1_ffn_scale2"], dtype=np.float32)[r])
        m["shf"] = shf_a
        m["kramp"] = kramp_a
        m["bias0"] = np.asarray(z["L0_ffn_bias"], dtype=np.float32)
        m["bias1"] = np.asarray(z["L1_ffn_bias"], dtype=np.float32)
        m["wqT_s"] = np.ascontiguousarray(WqT[:, hh * 2048 : (hh + 1) * 2048])
        m["wkT"] = WkT
        m["wvT"] = WvT
        m["woT_s"] = np.ascontiguousarray(WoT[hh * 2048 : (hh + 1) * 2048, :])
        m["cos1_t"] = cos1
        m["sin1_t"] = sin1
        m["am_t"] = am
        m["am1_t"] = am1
        m["rotm_t"] = rotm
        maps.append(m)
    return maps


def kernel(**inputs):
    p = build_fused()
    maps = fused_inmaps(inputs, p.const_arrays())
    res = run_bass_kernel_spmd(p.nc, maps, core_ids=list(range(8)))
    out = np.stack([
        np.concatenate(
            [np.asarray(res.results[2 * b]["x4_r"]), np.asarray(res.results[2 * b + 1]["x4_r"])],
            axis=0)
        for b in range(4)])
    return np.ascontiguousarray(out.astype(np.float32))


# version 10
# speedup vs baseline: 1.0404x; 1.0404x over previous
import math
from contextlib import ExitStack

import numpy as np
import ml_dtypes

import concourse.bass as bass
import concourse.mybir as mybir
from concourse.bass_utils import run_bass_kernel_spmd

F32 = mybir.dt.float32
BF16 = mybir.dt.bfloat16
I32 = mybir.dt.int32
AF = mybir.ActivationFunctionType
ALU = mybir.AluOpType
BF = ml_dtypes.bfloat16
AX = mybir.AxisListType.X

B, S, D = 4, 2048, 4096
NH, NKV, HD = 32, 8, 128
EPS = 1e-5
SM_SHIFT = 24.0
NCORES = 8
SCALE = 1.0 / math.sqrt(HD)

PAIR_SWAP = [1, 0, 3, 2, 5, 4, 7, 6, 9, 8, 11, 10, 13, 12, 15, 14,
             17, 16, 19, 18, 21, 20, 23, 22, 25, 24, 27, 26, 29, 28, 31, 30]


def _make_lut2acc():
    import concourse.dve_ops as _dve_ops
    from concourse.dve_op import DveOp as _DveOp
    from concourse.dve_spec import Spec as _Spec, lower as _lower
    from concourse.dve_spec import Src0, Src1, C0, C1, C2, Zero, One, eq as _eq

    a = _eq(Src0, C2) * C0
    b = _eq(Src0, C2 + One) * C1
    body = (Src1 + a) + b
    spec = _Spec(body=body)
    perf = {}
    shas = {}
    for ver in ("v3", "v4"):
        try:
            shas[ver] = _lower(spec, ver=ver, perf_en=True, name="LUT2ACC").sha(ver)
            perf[ver] = True
        except Exception:
            shas[ver] = _lower(spec, ver=ver, perf_en=False, name="LUT2ACC").sha(ver)
            perf[ver] = False
    op = _DveOp(
        "LUT2ACC", spec, subdim=False, uops_sha=shas, perf_en=perf,
        reference=lambda in0, in1, s0, s1, imm2:
            in1 + (in0 == imm2) * s0 + (in0 == (imm2 + 1.0)) * s1)
    if not any(o.name == "LUT2ACC" for o in _dve_ops.OPS):
        _dve_ops.OPS.append(op)
        _dve_ops._SUB_OPCODE_FOR_NAME[op.name] = (
            _dve_ops._CUSTOM_DVE_ROW_BASE + len(_dve_ops.OPS) - 1)
    return op


LUT2ACC = _make_lut2acc()


def _rope_l0():
    inv = 1.0 / 10000.0 ** (np.arange(0, HD, 2, dtype=np.float32) / HD)
    ang = np.arange(S, dtype=np.float32)[None, :] * inv[:, None]
    cos = np.zeros((HD, S), dtype=np.float32)
    sin = np.zeros((HD, S), dtype=np.float32)
    cos[0::2] = np.cos(ang)
    cos[1::2] = np.cos(ang)
    sin[0::2] = -np.sin(ang)
    sin[1::2] = np.sin(ang)
    return cos, sin


def _rope_l1():
    inv = 1.0 / 10000.0 ** (np.arange(0, HD, 2, dtype=np.float32) / HD)
    f = np.arange(S, dtype=np.float32)[None, :] * inv[:, None]
    cos = np.zeros((HD, S), dtype=np.float32)
    sin = np.zeros((HD, S), dtype=np.float32)
    cos[0::2] = np.cos(f)
    cos[1::2] = np.cos(f)
    sin[0::2] = np.sin(f)
    sin[1::2] = np.sin(f)
    return cos, sin


def _rotm_l1():
    P = np.zeros((128, 128), dtype=np.float32)
    for d_ in range(64):
        P[d_, d_ + 64] = -1.0
    for d_ in range(64, 128):
        P[d_, d_ - 64] = 1.0
    return np.ascontiguousarray(P.T.astype(BF))


def _tril_masks():
    kp = np.arange(128)[:, None]
    qf = np.arange(512)[None, :]
    m = np.stack([(qf >= j * 128 + kp) for j in range(4)]).astype(np.float32)
    return m.astype(BF).reshape(4 * 128, 512)


def _alpha_tables():
    a = np.linspace(0.0, 1.0, 256, dtype=np.float32)
    am = np.broadcast_to(a, (128, 256)).copy()
    am1 = np.broadcast_to(1.0 - a, (128, 256)).copy()
    return am, am1


class Sem:
    def __init__(self, h):
        self.h = h
        self.n = 0

    def inc(self, instr, k=1):
        instr.then_inc(self.h, k)
        self.n += k
        return self.n


class SerEng:

    _PASS = {"wait_ge", "nop", "dma_start", "dma_start_transpose", "wait_op"}

    def __init__(self, eng, sem):
        self._e = eng
        self._s = sem

    def __getattr__(self, name):
        f = getattr(self._e, name)
        if name in SerEng._PASS or name.startswith("_"):
            return f

        def wrapped(*a, **k):
            self._e.wait_ge(self._s.h, self._s.n)
            ins = f(*a, **k)
            ins.then_inc(self._s.h, 1)
            self._s.n += 1
            return ins

        return wrapped


class PB:
    def __init__(self):
        self.nc = bass.Bass("TRN2", target_bir_lowering=False)
        self.ctx = ExitStack()
        self.fvals = []
        self.ivals = []
        self._scopes = []

    def push_scope(self, pfx):
        self._scopes.append((ExitStack(), pfx))

    def pop_scope(self):
        st_, _ = self._scopes.pop()
        st_.close()

    def _alloc(self, factory, name):
        if self._scopes:
            st_, pfx = self._scopes[-1]
            return st_.enter_context(factory(pfx + name))
        return self.ctx.enter_context(factory(name))

    def sbuf(self, name, shape, dt):
        return self._alloc(lambda n: self.nc.sbuf_tensor(n, shape, dt), name)

    def psum(self, name, shape, dt=F32):
        return self._alloc(lambda n: self.nc.psum_tensor(n, shape, dt), name)

    def sem(self, name):
        return Sem(self.ctx.enter_context(self.nc.semaphore(name)))

    def make_sems(self):
        self.dse = self.sem("dse")
        self.gse2 = [self.sem("gse0"), self.sem("gse1")]
        self.gsev = self.sem("gsev")
        self.sv = self.sem("sv")
        self.ss = self.sem("ss")
        self.st = self.sem("st")
        self.sg = self.sem("sg")
        self.cc = self.sem("cc")
        self.dso = self.sem("dso")

    def all_sems(self):
        return [self.dse, self.gse2[0], self.gse2[1], self.gsev,
                self.sv, self.ss, self.st, self.sg, self.cc, self.dso]

    def barrier(self, skip_cc=False):
        sems = self.all_sems()
        if skip_cc:
            sems = [sm for sm in sems if sm is not self.cc]
        vals = [(sm.h, sm.n) for sm in sems]
        with self.nc.Block() as blk:
            for engname in ("sync", "gpsimd", "tensor", "vector", "scalar"):
                def _mk(eng, vals=vals):
                    for h, n in vals:
                        if n:
                            eng.wait_ge(h, n)
                getattr(blk, engname)(_mk)

    def alloc_consts(self, nf=24, ni=8):
        self.csb_f = self.sbuf("consts_f", [128, nf], F32)
        self.csb_i = self.sbuf("consts_i", [128, ni], I32)
        self.cin_f = self.nc.dram_tensor("consts_fin", [nf], F32, kind="ExternalInput")
        self.cin_i = self.nc.dram_tensor("consts_iin", [ni], I32, kind="ExternalInput")

    def cf(self, v):
        v = float(v)
        if v not in self.fvals:
            i = len(self.fvals)
            self.fvals.append(v)
            self.nc.const_aps.register(F32, v, self.csb_f[:, i : i + 1])
        return self.csb_f[:, self.fvals.index(v) : self.fvals.index(v) + 1]

    def ci(self, v):
        v = int(v)
        if v not in self.ivals:
            i = len(self.ivals)
            self.ivals.append(v)
            self.nc.const_aps.register(I32, v, self.csb_i[:, i : i + 1])
        return self.csb_i[:, self.ivals.index(v) : self.ivals.index(v) + 1]

    def load_consts(self, sync, dse):
        nf, ni = self.csb_f.shape[1], self.csb_i.shape[1]
        dse.inc(sync.dma_start(out=self.csb_f[:, :], in_=bass.AP(self.cin_f, 0, [[0, 128], [1, nf]])), 16)
        dse.inc(sync.dma_start(out=self.csb_i[:, :], in_=bass.AP(self.cin_i, 0, [[0, 128], [1, ni]])), 16)

    def const_arrays(self):
        nf, ni = self.csb_f.shape[1], self.csb_i.shape[1]
        cf = np.zeros(nf, dtype=np.float32)
        cf[: len(self.fvals)] = self.fvals
        ci_ = np.zeros(ni, dtype=np.int32)
        ci_[: len(self.ivals)] = self.ivals
        return {"consts_fin": cf, "consts_iin": ci_}

    def close(self):
        self.ctx.close()


def view(t, off, dims):
    X = t.shape[1]
    return bass.AP(t, off, [[X, 128]] + dims)


def wstrip_in(w, col0, ncols, nchunk=32):
    W = w.shape[1]
    return bass.AP(w, col0, [[W, 128], [128 * W, nchunk], [1, ncols]])


def emit_l0attn(p, x_in, wq, wk, wv, wo, cos_in, sin_in, tril_in, yp_out,
                xh_dram, q_dram, ao_dram):
    nc = p.nc
    p.push_scope("l0_")
    dse, dso, gse2, gsev = p.dse, p.dso, p.gse2, p.gsev
    sv, ss, st, sg = p.sv, p.ss, p.st, p.sg
    cosT = p.sbuf("cosT", [128, S], F32)
    sinT = p.sbuf("sinT", [128, S], F32)
    tril = p.sbuf("tril", [128, 4 * 512], BF16)
    ones_k = p.sbuf("ones_k", [128, 1], BF16)
    ones_1 = p.sbuf("ones_1", [1, 512], BF16)
    ks = p.sbuf("ks", [128, 4 * S], BF16)
    vs = p.sbuf("vs", [128, (S // 128) * 4 * HD], BF16)
    arA = p.sbuf("arA", [128, 16384], BF16)
    arB = p.sbuf("arB", [128, 32768], BF16)
    _xt0 = p.sbuf("xt0", [128, D], F32)
    xt = [_xt0, _xt0]
    sq = p.sbuf("sq", [128, D], BF16)
    _xh0 = p.sbuf("xhat0", [128, D], BF16)
    xhat = [_xh0, _xh0]
    stat = p.sbuf("stat", [128, 4], F32)
    qf = p.sbuf("qf", [128, 512], F32)
    qt0 = p.sbuf("qt0", [128, 512], F32)
    qrt = [p.sbuf(f"qrt{i}", [128, 512], BF16) for i in range(2)]
    qtile = p.sbuf("qtile", [128, 4 * 512], BF16)
    probs = [p.sbuf(f"probs{i}", [128, 512], BF16) for i in range(2)]
    srec = p.sbuf("srec", [1, 512], F32)
    srecb = [p.sbuf(f"srecb{i}", [1, 512], BF16) for i in range(2)]
    bc_sb = p.sbuf("bc_sb", [128, 512], F32)
    _ypt0 = p.sbuf("ypt0", [128, 512], F32)
    ypt = [_ypt0, _ypt0]

    ps_a = p.psum("ps_a", [128, 512], F32)
    ps_b = p.psum("ps_b", [128, 512], F32)
    ps_sc = [p.psum(f"ps_sc{i}", [128, 512], F32) for i in range(2)]
    ps_pv = p.psum("ps_pv", [128, 512], F32)
    ps_sum = p.psum("ps_sum", [1, 512], F32)
    ps_sum2 = p.psum("ps_sum2", [1, 512], F32)
    ps_bc = p.psum("ps_bc", [128, 512], F32)

    with nc.Block() as blk:
        @blk.sync
        def _(sync):
            dse.inc(sync.dma_start(out=cosT[:, :], in_=cos_in[:, :]), 16)
            dse.inc(sync.dma_start(out=sinT[:, :], in_=sin_in[:, :]), 16)
            dse.inc(sync.dma_start(
                out=tril[:, :],
                in_=bass.AP(tril_in, 0, [[512, 128], [128 * 512, 4], [1, 512]])), 16)
        boot_d = dse.n

        @blk.gpsimd
        def _(gpsimd):
            G = SerEng(gpsimd, sg)
            G.memset(ones_k[:, :], 1.0)
            G.memset(ones_1[:, :], 1.0)
        boot_g = sg.n

        c_eps = p.cf(EPS)
        c_im = p.cf(1.0 / D)
        c_shift = p.cf(-SM_SHIFT)
        c_scale = p.cf(SCALE)

        A_ld, A_sq, A_var, A_sd, A_xh, A_st = [], [], [], [], [], []
        for i in range(16):
            bi = i % 2

            @blk.sync
            def _(sync, i=i, bi=bi):
                if i == 0:
                    sync.wait_ge(dse.h, dse.n)
                    sync.wait_ge(dso.h, dso.n)
                if i >= 1:
                    sync.wait_ge(sv.h, A_xh[i - 1])
                dse.inc(sync.dma_start(out=xt[bi][:, :], in_=x_in[i * 128 : (i + 1) * 128, :]), 16)
            A_ld.append(dse.n)

            @blk.scalar
            def _(scalar, i=i, bi=bi):
                SS = SerEng(scalar, ss)
                scalar.wait_ge(dse.h, dse.n)
                scalar.wait_ge(dso.h, dso.n)
                if i >= 1:
                    scalar.wait_ge(sv.h, A_var[i - 1])
                SS.activation(out=sq[:, :], in_=xt[bi][:, :], func=AF.Identity,
                              accum_out=stat[:, 0:1])
                SS.activation(out=sq[:, :], in_=xt[bi][:, :], func=AF.Square,
                              accum_out=stat[:, 1:2])
            A_sq.append(ss.n)

            @blk.vector
            def _(vector, i=i, bi=bi):
                V = SerEng(vector, sv)
                vector.wait_ge(ss.h, A_sq[i])
                V.tensor_scalar(stat[:, 0:1], stat[:, 0:1], c_im, None, ALU.mult)
                V.tensor_scalar(stat[:, 1:2], stat[:, 1:2], c_im, None, ALU.mult)
                V.tensor_tensor(stat[:, 2:3], stat[:, 0:1], stat[:, 0:1], ALU.mult)
                V.tensor_tensor(stat[:, 2:3], stat[:, 1:2], stat[:, 2:3], ALU.subtract)
            A_var.append(sv.n)

            @blk.scalar
            def _(scalar, i=i):
                SS = SerEng(scalar, ss)
                scalar.wait_ge(sv.h, A_var[i])
                SS.activation(out=stat[:, 3:4], in_=stat[:, 2:3], func=AF.Sqrt, bias=c_eps)
            A_sd.append(ss.n)

            @blk.vector
            def _(vector, i=i, bi=bi):
                V = SerEng(vector, sv)
                vector.wait_ge(ss.h, A_sd[i])
                V.reciprocal(stat[:, 2:3], stat[:, 3:4])
                if i >= 2:
                    vector.wait_ge(dse.h, dse.n)
                    vector.wait_ge(dso.h, dso.n)
                V.tensor_scalar(
                    xhat[bi][:, :], xt[bi][:, :], stat[:, 0:1], stat[:, 2:3], ALU.subtract, ALU.mult)
            A_xh.append(sv.n)

            @blk.sync
            def _(sync, i=i, bi=bi):
                sync.wait_ge(sv.h, A_xh[i])
                dso.inc(sync.dma_start(out=xh_dram[i * 128 : (i + 1) * 128, :], in_=xhat[bi][:, :]), 16)
            A_st.append(dso.n)
        stageA_st = dso.n

        pe_tg = []
        q_ready = []
        kv_ready = []
        ev_marks = {}
        for tg in range(4):
            tg0 = tg * 512

            @blk.sync
            def _(sync, tg=tg):
                sync.wait_ge(dso.h, A_st[4 * tg + 3])
                if tg > 0:
                    sync.wait_ge(st.h, pe_tg[tg - 1])
                for c in range(32):
                    dse.inc(sync.dma_start(
                        out=view(arA, c * 512, [[1, 512]]),
                        in_=xh_dram[tg0 : tg0 + 512, c * 128 : (c + 1) * 128],
                        transpose=True), 16)
            xhT_ld = dse.n

            tg_ev0 = sv.n
            fgroups = [("k", 0)] + [("q", g) for g in range(4)]
            ev_idx = 0
            for gi, (kind, g) in enumerate(fgroups):
                sb = gi % 2
                win = wk if kind == "k" else wq

                @blk.gpsimd
                def _(gpsimd, tg=tg, g=g, sb=sb, gi=gi, win=win):
                    gpsimd.wait_ge(sg.h, boot_g)
                    key = ("peg", tg, gi - 2)
                    if key in ev_marks:
                        gpsimd.wait_ge(st.h, ev_marks[key])
                    elif tg > 0:
                        gpsimd.wait_ge(st.h, pe_tg[tg - 1])
                    gse2[sb].inc(gpsimd.dma_start(
                        out=view(arB, sb * 16384, [[512, 32], [1, 512]]),
                        in_=wstrip_in(win, g * 512, 512)), 16)
                ev_marks[("ld", tg, gi)] = gse2[sb].n

                for sub in range(4):
                    f = g * 4 + sub
                    idx = ev_idx
                    ev_idx += 1
                    ps = ps_a if idx % 2 == 0 else ps_b

                    @blk.tensor
                    def _(tensor, tg=tg, gi=gi, idx=idx, ps=ps, sb=sb, sub=sub):
                        tensor.wait_ge(gse2[sb].h, ev_marks[("ld", tg, gi)])
                        tensor.wait_ge(dse.h, dse.n)
                        tensor.wait_ge(dso.h, dso.n)
                        key = ("ev", tg, idx - 2)
                        tensor.wait_ge(sv.h, ev_marks.get(key, tg_ev0))
                        for c in range(32):
                            mm = tensor.matmul(
                                ps[:, :],
                                view(arB, sb * 16384 + c * 512 + sub * 128, [[1, 128]]),
                                view(arA, c * 512, [[1, 512]]),
                                start=(c == 0), stop=(c == 31))
                        st.inc(mm)
                    ev_marks[("peg", tg, gi)] = st.n
                    ev_marks[("pemm", tg, idx)] = st.n

                    @blk.vector
                    def _(vector, tg=tg, f=f, kind=kind, idx=idx, ps=ps, tg0=tg0):
                        V = SerEng(vector, sv)
                        vector.wait_ge(st.h, ev_marks[("pemm", tg, idx)])
                        V.tensor_copy(qf[:, :], ps[:, :])
                        V.stream_shuffle(qt0[:, :], qf[:, :], PAIR_SWAP)
                        V.tensor_tensor(qt0[:, :], qt0[:, :], sinT[:, tg0 : tg0 + 512], ALU.mult)
                        V.tensor_tensor(qf[:, :], qf[:, :], cosT[:, tg0 : tg0 + 512], ALU.mult)
                        if kind == "k":
                            dst = ks[:, f * S + tg0 : f * S + tg0 + 512]
                            V.tensor_tensor(dst, qf[:, :], qt0[:, :], ALU.add)
                        else:
                            par = idx % 2
                            vector.wait_ge(dse.h, dse.n)
                            vector.wait_ge(dso.h, dso.n)
                            V.tensor_tensor(qrt[par][:, :], qf[:, :], qt0[:, :], ALU.add)
                    ev_marks[("ev", tg, idx)] = sv.n

                    if kind == "q":
                        @blk.sync
                        def _(sync, tg=tg, f=f, idx=idx, tg0=tg0):
                            sync.wait_ge(sv.h, ev_marks[("ev", tg, idx)])
                            dso.inc(sync.dma_start(
                                out=q_dram[f * 128 : (f + 1) * 128, tg0 : tg0 + 512],
                                in_=qrt[idx % 2][:, :]), 16)

            @blk.gpsimd
            def _(gpsimd, tg=tg):
                gpsimd.wait_ge(st.h, st.n)
                gsev.inc(gpsimd.dma_start(
                    out=view(arB, 0, [[512, 32], [1, 512]]),
                    in_=wstrip_in(wv, 0, 512)), 16)
            vld = gsev.n

            for sub in range(4):
                @blk.tensor
                def _(tensor, tg=tg, sub=sub, vld=vld):
                    tensor.wait_ge(gsev.h, vld)
                    tensor.wait_ge(sv.h, sv.n)
                    for c in range(32):
                        mm = tensor.matmul(
                            ps_a[:, :],
                            view(arA, c * 512 + sub * 128, [[1, 128]]),
                            view(arB, c * 512, [[1, 512]]),
                            start=(c == 0), stop=(c == 31))
                    st.inc(mm)
                vpe = st.n

                @blk.vector
                def _(vector, tg=tg, sub=sub, vpe=vpe):
                    V = SerEng(vector, sv)
                    vector.wait_ge(st.h, vpe)
                    tt = tg * 4 + sub
                    V.tensor_copy(vs[:, tt * 512 : (tt + 1) * 512], ps_a[:, :])
            pe_tg.append(st.n)
            q_ready.append(dso.n)
            kv_ready.append(sv.n)
        stageB_v = sv.n

        expc = {0: 0, 1: 0}
        pvc = {0: 0, 1: 0}
        gp = 0
        last_ev_par = [sv.n, sv.n]
        pv_b = [ps_pv, ps_a]
        bc_b = [ps_bc, ps_b]
        for qb in range(4):
            nkt = (qb + 1) * 4
            for kvh in range(4):
                @blk.sync
                def _(sync, qb=qb, kvh=kvh):
                    sync.wait_ge(sv.h, kv_ready[qb])
                    sync.wait_ge(dso.h, q_ready[qb])
                    sync.wait_ge(st.h, st.n)
                    for j in range(4):
                        fr = (4 * kvh + j) * 128
                        dse.inc(sync.dma_start(
                            out=qtile[:, j * 512 : (j + 1) * 512],
                            in_=q_dram[fr : fr + 128, qb * 512 : (qb + 1) * 512]), 16)
                qld = dse.n

                for j in range(4):
                    par = gp % 2
                    pvx = pv_b[par]
                    bcx = bc_b[par]
                    sumx = ps_sum[0:1, :] if par == 0 else ps_sum2[0:1, :]
                    for kt in range(nkt):
                        pp = kt % 2

                        @blk.tensor
                        def _(tensor, kvh=kvh, j=j, kt=kt, pp=pp, qld=qld):
                            tensor.wait_ge(dse.h, qld)
                            tensor.wait_ge(ss.h, expc[pp])
                            st.inc(tensor.matmul(
                                ps_sc[pp][:, :],
                                ks[:, kvh * S + kt * 128 : kvh * S + kt * 128 + 128],
                                qtile[:, j * 512 : (j + 1) * 512],
                                start=True, stop=True))
                        scd = st.n

                        @blk.scalar
                        def _(scalar, kt=kt, pp=pp, scd=scd):
                            SS = SerEng(scalar, ss)
                            scalar.wait_ge(st.h, scd)
                            scalar.wait_ge(st.h, pvc[pp])
                            SS.activation(
                                out=probs[pp][:, :], in_=ps_sc[pp][:, :],
                                func=AF.Exp, bias=c_shift, scale=c_scale)
                        expc[pp] = ss.n
                        expd = ss.n

                        maskd = None
                        if kt >= qb * 4:
                            @blk.vector
                            def _(vector, kt=kt, qb=qb, pp=pp, expd=expd):
                                V = SerEng(vector, sv)
                                vector.wait_ge(ss.h, expd)
                                off = kt - qb * 4
                                V.tensor_tensor(
                                    probs[pp][:, :], probs[pp][:, :],
                                    tril[:, off * 512 : (off + 1) * 512], ALU.mult)
                            maskd = sv.n

                        @blk.tensor
                        def _(tensor, kvh=kvh, kt=kt, pp=pp, expd=expd, maskd=maskd,
                              nkt=nkt, pvx=pvx, sumx=sumx, par=par):
                            if maskd is not None:
                                tensor.wait_ge(sv.h, maskd)
                            else:
                                tensor.wait_ge(ss.h, expd)
                            if kt == 0:
                                tensor.wait_ge(sv.h, last_ev_par[par])
                            tensor.matmul(
                                pvx[:, :],
                                vs[:, kt * 512 + kvh * 128 : kt * 512 + kvh * 128 + 128],
                                probs[pp][:, :],
                                start=(kt == 0), stop=(kt == nkt - 1))
                            st.inc(tensor.matmul(
                                sumx, ones_k[:, :], probs[pp][:, :],
                                start=(kt == 0), stop=(kt == nkt - 1)))
                        pvc[pp] = st.n
                    pvdone = st.n

                    @blk.vector
                    def _(vector, pvdone=pvdone, sumx=sumx, par=par):
                        V = SerEng(vector, sv)
                        vector.wait_ge(st.h, pvdone)
                        V.reciprocal(srec[:, :], sumx)
                        V.tensor_copy(srecb[par][:, :], srec[:, :])
                    recd = sv.n

                    @blk.tensor
                    def _(tensor, recd=recd, bcx=bcx, par=par):
                        tensor.wait_ge(sv.h, recd)
                        tensor.wait_ge(sv.h, last_ev_par[par])
                        st.inc(tensor.matmul(
                            bcx[:, :], ones_1[0:1, 0:128], srecb[par][0:1, :],
                            start=True, stop=True))
                    bcd = st.n

                    @blk.vector
                    def _(vector, kvh=kvh, j=j, bcd=bcd, pvx=pvx, bcx=bcx):
                        V = SerEng(vector, sv)
                        vector.wait_ge(st.h, bcd)
                        V.tensor_copy(bc_sb[:, :], bcx[:, :])
                        c = 4 * kvh + j
                        vector.wait_ge(dso.h, dso.n)
                        V.tensor_tensor(
                            view(arA, c * 512, [[1, 512]]), pvx[:, :], bc_sb[:, :], ALU.mult)
                    last_ev_par[par] = sv.n
                    aoev = sv.n

                    @blk.sync
                    def _(sync, qb=qb, kvh=kvh, j=j, aoev=aoev):
                        sync.wait_ge(sv.h, aoev)
                        c = 4 * kvh + j
                        dso.inc(sync.dma_start(
                            out=ao_dram[c * 128 : (c + 1) * 128, qb * 512 : (qb + 1) * 512],
                            in_=view(arA, c * 512, [[1, 512]])), 16)
                    gp += 1
        stageC_ao = dso.n

        _aot0 = p.sbuf("aot0", [128, 16 * 128], BF16)
        aot = [_aot0, _aot0]
        aold = {}
        ypst = {}
        st_d0 = st.n
        for fg in range(8):
            sb = fg % 2

            @blk.gpsimd
            def _(gpsimd, fg=fg, sb=sb, st_d0=st_d0):
                if fg < 2:
                    gpsimd.wait_ge(st.h, st_d0)
                else:
                    gpsimd.wait_ge(st.h, aold[("pe", fg - 2, 15)])
                gse2[sb].inc(gpsimd.dma_start(
                    out=view(arB, sb * 8192, [[512, 16], [1, 512]]),
                    in_=bass.AP(wo, fg * 512, [[D, 128], [128 * D, 16], [1, 512]])), 16)
            wold = gse2[sb].n

            for tt in range(16):
                par = tt % 2

                @blk.sync
                def _(sync, fg=fg, tt=tt, par=par):
                    if fg == 0 and tt == 0:
                        sync.wait_ge(dse.h, dse.n)
                        sync.wait_ge(dso.h, dso.n)
                    key = ("pe", fg, tt - 1) if tt >= 1 else ("pe", fg - 1, tt + 15)
                    if key in aold:
                        sync.wait_ge(st.h, aold[key])
                    dse.inc(sync.dma_start(
                        out=aot[par][:, :],
                        in_=bass.AP(ao_dram, tt * 128, [[S, 128], [128 * S, 16], [1, 128]])), 16)
                aold[("ld", fg, tt)] = dse.n

                @blk.tensor
                def _(tensor, fg=fg, tt=tt, par=par, wold=wold, sb=sb):
                    tensor.wait_ge(gse2[sb].h, wold)
                    tensor.wait_ge(dse.h, dse.n)
                    tensor.wait_ge(dso.h, dso.n)
                    tensor.wait_ge(sv.h, sv.n)
                    for c in range(16):
                        mm = tensor.matmul(
                            ps_a[:, :],
                            view(aot[par], c * 128, [[1, 128]]),
                            view(arB, sb * 8192 + c * 512, [[1, 512]]),
                            start=(c == 0), stop=(c == 15))
                    st.inc(mm)
                aold[("pe", fg, tt)] = st.n

                @blk.vector
                def _(vector, fg=fg, tt=tt, par=par):
                    V = SerEng(vector, sv)
                    vector.wait_ge(st.h, aold[("pe", fg, tt)])
                    key = ("st", fg, tt - 2) if tt >= 2 else ("st", fg - 1, tt + 14)
                    vector.wait_ge(dse.h, dse.n)
                    vector.wait_ge(dso.h, dso.n)
                    V.tensor_copy(ypt[par][:, :], ps_a[:, :])
                ypev = sv.n

                @blk.sync
                def _(sync, fg=fg, tt=tt, par=par, ypev=ypev):
                    sync.wait_ge(sv.h, ypev)
                    dso.inc(sync.dma_start(
                        out=yp_out[tt * 128 : (tt + 1) * 128, fg * 512 : (fg + 1) * 512],
                        in_=ypt[par][:, :]), 16)
                ypst[("st", fg, tt)] = dse.n

        @blk.sync
        def _(sync):
            sync.wait_ge(dse.h, dse.n)
            sync.wait_ge(dso.h, dso.n)

    p.pop_scope()


def l0attn_inmaps(x_full, w_qkv, w_o, consts):
    cos, sin = _rope_l0()
    tril = _tril_masks()
    maps = []
    for c in range(8):
        b, hh = c // 2, c % 2
        m = dict(consts)
        m["xb"] = np.ascontiguousarray(x_full[b])
        m["wq_s"] = np.ascontiguousarray(w_qkv[:, hh * 2048 : (hh + 1) * 2048])
        m["wk_s"] = np.ascontiguousarray(w_qkv[:, 4096 + hh * 512 : 4096 + (hh + 1) * 512])
        m["wv_s"] = np.ascontiguousarray(w_qkv[:, 5120 + hh * 512 : 5120 + (hh + 1) * 512])
        m["wo_s"] = np.ascontiguousarray(w_o[hh * 2048 : (hh + 1) * 2048, :])
        m["cos_t"] = cos
        m["sin_t"] = sin
        m["tril_t"] = tril
        maps.append(m)
    return maps


def emit_dequant(p, qw, lut4, lut8, w1q, sc_in, zr_in, mk_in, s2_in,
                 shf_in, kramp_in, wl_out, wh_out, w1_out):
    nc = p.nc
    p.push_scope("dq_")
    dse, dso, sv = p.dse, p.dso, p.sv
    shf_sb = p.sbuf("shf_sb", [128, 32], I32)
    kramp_sb = p.sbuf("kramp_sb", [128, 256], F32)
    qws = p.sbuf("qws", [128, 8 * 128], I32)
    c8 = p.sbuf("c8", [128, D], I32)
    tbit = p.sbuf("tbit", [128, D], I32)
    c8b = p.sbuf("c8b", [128, D], BF16)
    c4b = p.sbuf("c4b", [128, D], BF16)
    eq = p.sbuf("eq", [128, D], BF16)
    accl = p.sbuf("accl", [128, D], BF16)
    acch = p.sbuf("acch", [128, D], BF16)
    lut4_sb = p.sbuf("lut4_sb", [128, 16], F32)
    lut8_sb = p.sbuf("lut8_sb", [128, 256], F32)
    w1qs = p.sbuf("w1qs", [128, D], I32)
    mks = p.sbuf("mks", [128, D], I32)
    qbf = p.sbuf("qbf", [128, D], BF16)
    mbf = p.sbuf("mbf", [128, D], BF16)
    w1t = p.sbuf("w1t", [128, D], BF16)
    sc_sb = p.sbuf("sc_sb", [128, 32], F32)
    zr_sb = p.sbuf("zr_sb", [128, 32], F32)
    s2_sb = p.sbuf("s2_sb", [128, 1], F32)
    A_sb = p.sbuf("A_sb", [128, 32], F32)
    Bz_sb = p.sbuf("Bz_sb", [128, 32], F32)

    with nc.Block() as blk:
        @blk.sync
        def _(sync):
            dse.inc(sync.dma_start(out=shf_sb[:, :], in_=shf_in[:, :]), 16)
            dse.inc(sync.dma_start(out=kramp_sb[:, :], in_=kramp_in[:, :]), 16)
        c1i = p.ci(1)

        for ot in range(4):
            o0 = ot * 128

            @blk.sync
            def _(sync, ot=ot, o0=o0):
                sync.wait_ge(sv.h, sv.n)
                dse.inc(sync.dma_start(
                    out=qws[:, :],
                    in_=bass.AP(qw, o0 * 128, [[128, 128], [512 * 128, 8], [1, 128]])), 16)
                dse.inc(sync.dma_start(out=lut4_sb[:, :], in_=lut4[o0 : o0 + 128, :]), 16)
                dse.inc(sync.dma_start(out=lut8_sb[:, :], in_=lut8[o0 : o0 + 128, :]), 16)
                dse.inc(sync.dma_start(out=w1qs[:, :], in_=w1q[o0 : o0 + 128, :]), 16)
                dse.inc(sync.dma_start(out=mks[:, :], in_=mk_in[o0 : o0 + 128, :]), 16)
                dse.inc(sync.dma_start(out=sc_sb[:, :], in_=sc_in[o0 : o0 + 128, :]), 16)
                dse.inc(sync.dma_start(out=zr_sb[:, :], in_=zr_in[o0 : o0 + 128, :]), 16)
                dse.inc(sync.dma_start(out=s2_sb[:, :], in_=s2_in[o0 : o0 + 128, :]), 16)

            @blk.vector
            def _(vector, ot=ot):
                V = SerEng(vector, sv)
                vector.wait_ge(dse.h, dse.n)
                vector.wait_ge(dso.h, dso.n)
                for pl in range(8):
                    a_w = view(qws, pl * 128, [[1, 128], [0, 32]])
                    a_s = bass.AP(shf_sb, 0, [[32, 128], [0, 128], [1, 32]])
                    V.tensor_tensor(tbit[:, :], a_w, a_s, ALU.logical_shift_right)
                    V.tensor_scalar(tbit[:, :], tbit[:, :], c1i, None, ALU.bitwise_and)
                    if pl == 0:
                        V.tensor_copy(c8[:, :], tbit[:, :])
                    else:
                        V.tensor_scalar(c8[:, :], tbit[:, :], p.ci(pl), c8[:, :], ALU.logical_shift_left, ALU.bitwise_or) if False else \
                            V.scalar_tensor_tensor(c8[:, :], tbit[:, :], p.ci(pl), c8[:, :], ALU.logical_shift_left, ALU.bitwise_or)
                V.tensor_copy(c8b[:, :], c8[:, :])
                V.tensor_scalar(c8[:, :], c8[:, :], p.ci(15), None, ALU.bitwise_and)
                V.tensor_copy(c4b[:, :], c8[:, :])
                for k in range(16):
                    if k == 0:
                        V.tensor_scalar(accl[:, :], c4b[:, :], kramp_sb[:, 0:1], lut4_sb[:, 0:1], ALU.is_equal, ALU.mult)
                    else:
                        V.tensor_scalar(eq[:, :], c4b[:, :], kramp_sb[:, k : k + 1], lut4_sb[:, k : k + 1], ALU.is_equal, ALU.mult)
                        V.tensor_tensor(accl[:, :], accl[:, :], eq[:, :], ALU.add)
                for k in range(256):
                    if k == 0:
                        V.tensor_scalar(acch[:, :], c8b[:, :], kramp_sb[:, 0:1], lut8_sb[:, 0:1], ALU.is_equal, ALU.mult)
                    else:
                        V.tensor_scalar(eq[:, :], c8b[:, :], kramp_sb[:, k : k + 1], lut8_sb[:, k : k + 1], ALU.is_equal, ALU.mult)
                        V.tensor_tensor(acch[:, :], acch[:, :], eq[:, :], ALU.add)
                V.tensor_scalar(A_sb[:, :], sc_sb[:, :], s2_sb[:, 0:1], None, ALU.mult)
                V.tensor_tensor(Bz_sb[:, :], zr_sb[:, :], A_sb[:, :], ALU.mult)
                V.tensor_scalar(Bz_sb[:, :], Bz_sb[:, :], -1.0, None, ALU.mult)
                V.tensor_copy(qbf[:, :], w1qs[:, :])
                V.tensor_copy(mbf[:, :], mks[:, :])
                for g in range(32):
                    V.tensor_scalar(
                        w1t[:, g * 128 : (g + 1) * 128], qbf[:, g * 128 : (g + 1) * 128],
                        A_sb[:, g : g + 1], Bz_sb[:, g : g + 1], ALU.mult, ALU.add)
                V.tensor_tensor(w1t[:, :], w1t[:, :], mbf[:, :], ALU.mult)

            @blk.sync
            def _(sync, ot=ot, o0=o0):
                sync.wait_ge(sv.h, sv.n)
                dso.inc(sync.dma_start(out=wl_out[o0 : o0 + 128, :], in_=accl[:, :]), 16)
                dso.inc(sync.dma_start(out=wh_out[o0 : o0 + 128, :], in_=acch[:, :]), 16)
                dso.inc(sync.dma_start(out=w1_out[o0 : o0 + 128, :], in_=w1t[:, :]), 16)

        @blk.sync
        def _(sync):
            sync.wait_ge(dse.h, dse.n)
            sync.wait_ge(dso.h, dso.n)

    p.pop_scope()


def dequant_inmaps(z, consts):
    shf = np.broadcast_to(np.arange(32, dtype=np.int32), (128, 32)).copy()
    kramp = np.broadcast_to(np.arange(256, dtype=np.float32), (128, 256)).copy()
    maps = []
    for c in range(8):
        r = slice(c * 512, (c + 1) * 512)
        m = dict(consts)
        m["qw_s"] = np.ascontiguousarray(z["L0_ffn_qweight"][:, r, :])
        m["lut4_s"] = np.ascontiguousarray(z["L0_ffn_lut4"][r])
        m["lut8_s"] = np.ascontiguousarray(z["L0_ffn_lut8"][r])
        m["w1q_s"] = np.ascontiguousarray(z["L1_ffn_W_q"][r])
        m["sc_s"] = np.ascontiguousarray(z["L1_ffn_scales"][r, :, 0])
        m["zr_s"] = np.ascontiguousarray(z["L1_ffn_zeros"][r, :, 0])
        m["mk_s"] = np.ascontiguousarray(z["L1_ffn_mask"][r])
        m["s2_s"] = np.ascontiguousarray(z["L1_ffn_scale2"][r])
        m["shf"] = shf
        m["kramp"] = kramp
        maps.append(m)
    return maps


def emit_ffn0(p, x_in, ps_in, wl_in, wh_in, bias_in, x2_out,
              x1_dram, xh2_dram, ydl_dram, ydh_dram):
    nc = p.nc
    p.push_scope("f0_")
    dse, dso, sv, ss, st, cc, gsev = p.dse, p.dso, p.sv, p.ss, p.st, p.cc, p.gsev
    ln2T = p.sbuf("ln2T", [128, 32 * 1024], BF16)
    arW = p.sbuf("arW", [128, 2 * 32 * 512], BF16)
    ta = p.sbuf("ta", [128, D], F32)
    tb = p.sbuf("tb", [128, D], F32)
    tc = p.sbuf("tc", [128, D], F32)
    sq = p.sbuf("sq", [128, D], BF16)
    stat = p.sbuf("stat", [128, 8], F32)
    masks = p.sbuf("masks", [128, 8], F32)
    biasbc = p.sbuf("biasbc", [128, 1024], F32)
    ypt = [p.sbuf(f"ypt{i}", [128, 512], F32) for i in range(2)]
    ps_a = p.psum("ps_a", [128, 512], F32)
    ps_b = p.psum("ps_b", [128, 512], F32)

    with nc.Block() as blk:
        c_eps = p.cf(EPS)
        c_im = p.cf(1.0 / D)
        c_64 = p.cf(64.0)

        for i in range(8):
            r0 = i * 128

            @blk.sync
            def _(sync, i=i, r0=r0):
                sync.wait_ge(sv.h, sv.n)
                sync.wait_ge(dse.h, dse.n)
                sync.wait_ge(dso.h, dso.n)
                dse.inc(sync.dma_start(out=ta[:, :], in_=x_in[r0 : r0 + 128, :]), 16)
                dse.inc(sync.dma_start(out=tb[:, :], in_=ps_in[r0 : r0 + 128, :]), 16)

            @blk.scalar
            def _(scalar, i=i):
                SS = SerEng(scalar, ss)
                scalar.wait_ge(sv.h, sv.n)
                scalar.wait_ge(dse.h, dse.n)
                scalar.wait_ge(dso.h, dso.n)
                SS.activation(out=sq[:, :], in_=tb[:, :], func=AF.Identity,
                              accum_out=stat[:, 0:1])
                SS.activation(out=sq[:, :], in_=tb[:, :], func=AF.Square,
                              accum_out=stat[:, 1:2])

            @blk.vector
            def _(vector, i=i):
                V = SerEng(vector, sv)
                vector.wait_ge(ss.h, ss.n)
                V.tensor_scalar(stat[:, 0:1], stat[:, 0:1], c_im, None, ALU.mult)
                V.tensor_scalar(stat[:, 1:2], stat[:, 1:2], c_im, None, ALU.mult)
                V.tensor_tensor(stat[:, 2:3], stat[:, 0:1], stat[:, 0:1], ALU.mult)
                V.tensor_tensor(stat[:, 2:3], stat[:, 1:2], stat[:, 2:3], ALU.subtract)

            @blk.scalar
            def _(scalar, i=i):
                SS = SerEng(scalar, ss)
                scalar.wait_ge(sv.h, sv.n)
                SS.activation(out=stat[:, 3:4], in_=stat[:, 2:3], func=AF.Sqrt, bias=c_eps)

            @blk.vector
            def _(vector, i=i, r0=r0):
                V = SerEng(vector, sv)
                vector.wait_ge(ss.h, ss.n)
                V.reciprocal(stat[:, 3:4], stat[:, 3:4])
                V.tensor_scalar(tc[:, :], tb[:, :], stat[:, 0:1], stat[:, 3:4], ALU.subtract, ALU.mult)
                V.tensor_tensor(ta[:, :], ta[:, :], tc[:, :], ALU.add)

            @blk.sync
            def _(sync, i=i, r0=r0):
                sync.wait_ge(sv.h, sv.n)
                dso.inc(sync.dma_start(out=x1_dram[r0 : r0 + 128, :], in_=ta[:, :]), 16)

            @blk.scalar
            def _(scalar, i=i):
                SS = SerEng(scalar, ss)
                scalar.wait_ge(sv.h, sv.n)
                scalar.wait_ge(dse.h, dse.n)
                scalar.wait_ge(dso.h, dso.n)
                SS.activation(out=sq[:, :], in_=ta[:, :], func=AF.Identity,
                              accum_out=stat[:, 0:1])
                SS.activation(out=sq[:, :], in_=ta[:, :], func=AF.Square,
                              accum_out=stat[:, 1:2])

            @blk.vector
            def _(vector, i=i):
                V = SerEng(vector, sv)
                vector.wait_ge(ss.h, ss.n)
                V.tensor_scalar(stat[:, 0:1], stat[:, 0:1], c_im, None, ALU.mult)
                V.tensor_scalar(stat[:, 1:2], stat[:, 1:2], c_im, None, ALU.mult)
                V.tensor_tensor(stat[:, 2:3], stat[:, 0:1], stat[:, 0:1], ALU.mult)
                V.tensor_tensor(stat[:, 2:3], stat[:, 1:2], stat[:, 2:3], ALU.subtract)
                V.tensor_scalar(stat[:, 4:5], stat[:, 2:3], c_eps, None, ALU.add)
                V.reciprocal(stat[:, 4:5], stat[:, 4:5])
                V.tensor_tensor(stat[:, 4:5], stat[:, 2:3], stat[:, 4:5], ALU.mult)

            @blk.scalar
            def _(scalar, i=i):
                SS = SerEng(scalar, ss)
                scalar.wait_ge(sv.h, sv.n)
                SS.activation(out=stat[:, 5:6], in_=stat[:, 4:5], func=AF.Sqrt)
                SS.activation(out=stat[:, 3:4], in_=stat[:, 2:3], func=AF.Sqrt, bias=c_eps)

            @blk.vector
            def _(vector, i=i):
                V = SerEng(vector, sv)
                vector.wait_ge(ss.h, ss.n)
                V.tensor_scalar(masks[:, i : i + 1], stat[:, 5:6], c_64, c_64, ALU.mult, ALU.is_gt)
                V.reciprocal(stat[:, 3:4], stat[:, 3:4])
                V.tensor_scalar(sq[:, :], ta[:, :], stat[:, 0:1], stat[:, 3:4], ALU.subtract, ALU.mult)

            @blk.sync
            def _(sync, i=i, r0=r0):
                sync.wait_ge(sv.h, sv.n)
                dso.inc(sync.dma_start(out=xh2_dram[r0 : r0 + 128, :], in_=sq[:, :]), 16)

            if i in (3, 7):
                hb = 0 if i == 3 else 512

                @blk.sync
                def _(sync, hb=hb):
                    sync.wait_ge(dso.h, dso.n)
                    for c in range(32):
                        dse.inc(sync.dma_start(
                            out=view(ln2T, c * 1024 + hb, [[1, 512]]),
                            in_=xh2_dram[hb : hb + 512, c * 128 : (c + 1) * 128],
                            transpose=True), 16)

        wmk = {}
        bmk = {}
        for wi, (win, ydram) in enumerate(((wl_in, ydl_dram), (wh_in, ydh_dram))):
            for og in range(8):
                gidx = wi * 8 + og
                wh_ = gidx % 2

                @blk.sync
                def _(sync, og=og, win=win, wh_=wh_, gidx=gidx):
                    if gidx == 0:
                        sync.wait_ge(dso.h, dso.n)
                        sync.wait_ge(cc.h, 2)
                    key = ("pe", gidx - 2)
                    if key in wmk:
                        sync.wait_ge(st.h, wmk[key])
                        sync.wait_ge(sv.h, bmk[gidx - 2])
                    for c in range(32):
                        dse.inc(sync.dma_start(
                            out=view(arW, wh_ * 16384 + c * 512, [[1, 512]]),
                            in_=win[og * 512 : (og + 1) * 512, c * 128 : (c + 1) * 128],
                            transpose=True), 16)
                    dse.inc(sync.dma_start(
                        out=biasbc[:, wh_ * 512 : wh_ * 512 + 512],
                        in_=bass.AP(bias_in, og * 512, [[0, 128], [1, 512]])), 16)
                    sync.wait_ge(dse.h, dse.n)
                wld = dse.n

                for tsub in range(8):
                    pp = tsub % 2
                    ps = ps_a if pp == 0 else ps_b

                    @blk.tensor
                    def _(tensor, og=og, tsub=tsub, ps=ps, wh_=wh_, wld=wld):
                        tensor.wait_ge(dse.h, wld)
                        tensor.wait_ge(sv.h, sv.n)
                        for c in range(32):
                            mm = tensor.matmul(
                                ps[:, :],
                                view(ln2T, c * 1024 + tsub * 128, [[1, 128]]),
                                view(arW, wh_ * 16384 + c * 512, [[1, 512]]),
                                start=(c == 0), stop=(c == 31))
                        st.inc(mm)
                    ped = st.n

                    @blk.vector
                    def _(vector, tsub=tsub, ped=ped, pp=pp, ps=ps, wh_=wh_):
                        V = SerEng(vector, sv)
                        vector.wait_ge(st.h, ped)
                        vector.wait_ge(gsev.h, gsev.n)
                        V.tensor_tensor(ypt[pp][:, :], ps[:, :],
                                        biasbc[:, wh_ * 512 : wh_ * 512 + 512], ALU.add)
                    yev = sv.n

                    @blk.gpsimd
                    def _(gpsimd, og=og, tsub=tsub, yev=yev, pp=pp, ydram=ydram):
                        gpsimd.wait_ge(sv.h, yev)
                        gsev.inc(gpsimd.dma_start(
                            out=ydram[tsub * 128 : (tsub + 1) * 128, og * 512 : (og + 1) * 512],
                            in_=ypt[pp][:, :]), 16)
                wmk[("pe", gidx)] = st.n
                bmk[gidx] = sv.n

        for i in range(8):
            r0 = i * 128

            @blk.sync
            def _(sync, i=i, r0=r0):
                sync.wait_ge(dse.h, dse.n)
                sync.wait_ge(dso.h, dso.n)
                sync.wait_ge(gsev.h, gsev.n)
                sync.wait_ge(sv.h, sv.n)
                dse.inc(sync.dma_start(out=ta[:, :], in_=x1_dram[r0 : r0 + 128, :]), 16)
                dse.inc(sync.dma_start(out=tb[:, :], in_=ydh_dram[r0 : r0 + 128, :]), 16)
                dse.inc(sync.dma_start(out=tc[:, :], in_=ydl_dram[r0 : r0 + 128, :]), 16)

            @blk.vector
            def _(vector, i=i):
                V = SerEng(vector, sv)
                vector.wait_ge(dse.h, dse.n)
                vector.wait_ge(dso.h, dso.n)
                V.tensor_tensor(tb[:, :], tb[:, :], tc[:, :], ALU.subtract)
                V.scalar_tensor_tensor(tb[:, :], tb[:, :], masks[:, i : i + 1], tc[:, :], ALU.mult, ALU.add)

            @blk.scalar
            def _(scalar, i=i):
                SS = SerEng(scalar, ss)
                scalar.wait_ge(sv.h, sv.n)
                scalar.wait_ge(dse.h, dse.n)
                scalar.wait_ge(dso.h, dso.n)
                SS.activation(out=sq[:, :], in_=tb[:, :], func=AF.Identity,
                              accum_out=stat[:, 0:1])
                SS.activation(out=sq[:, :], in_=tb[:, :], func=AF.Square,
                              accum_out=stat[:, 1:2])

            @blk.vector
            def _(vector, i=i):
                V = SerEng(vector, sv)
                vector.wait_ge(ss.h, ss.n)
                V.tensor_scalar(stat[:, 0:1], stat[:, 0:1], c_im, None, ALU.mult)
                V.tensor_scalar(stat[:, 1:2], stat[:, 1:2], c_im, None, ALU.mult)
                V.tensor_tensor(stat[:, 2:3], stat[:, 0:1], stat[:, 0:1], ALU.mult)
                V.tensor_tensor(stat[:, 2:3], stat[:, 1:2], stat[:, 2:3], ALU.subtract)

            @blk.scalar
            def _(scalar, i=i):
                SS = SerEng(scalar, ss)
                scalar.wait_ge(sv.h, sv.n)
                SS.activation(out=stat[:, 3:4], in_=stat[:, 2:3], func=AF.Sqrt, bias=c_eps)

            @blk.vector
            def _(vector, i=i):
                V = SerEng(vector, sv)
                vector.wait_ge(ss.h, ss.n)
                V.reciprocal(stat[:, 3:4], stat[:, 3:4])
                V.tensor_scalar(tc[:, :], tb[:, :], stat[:, 0:1], stat[:, 3:4], ALU.subtract, ALU.mult)
                V.tensor_tensor(ta[:, :], ta[:, :], tc[:, :], ALU.add)

            @blk.sync
            def _(sync, i=i, r0=r0):
                sync.wait_ge(sv.h, sv.n)
                dso.inc(sync.dma_start(out=x2_out[r0 : r0 + 128, :], in_=ta[:, :]), 16)

        @blk.sync
        def _(sync):
            sync.wait_ge(dse.h, dse.n)
            sync.wait_ge(dso.h, dso.n)

    p.pop_scope()


def ffn0_inmaps(x_full, yp0, yp1, wl_full, wh_full, bias, consts):
    maps = []
    for c in range(8):
        b, half = c // 2, c % 2
        rows = slice(half * 1024, (half + 1) * 1024)
        m = dict(consts)
        m["x_r"] = np.ascontiguousarray(x_full[b][rows])
        m["p0_r"] = np.ascontiguousarray(yp0[b][rows])
        m["p1_r"] = np.ascontiguousarray(yp1[b][rows])
        m["wl_f"] = wl_full
        m["wh_f"] = wh_full
        m["bias"] = bias
        maps.append(m)
    return maps


def emit_l1attn(p, x_in, wq, wk, wv, wo, cos_in, sin_in, am_in, am1_in,
                rotm_in, yp_out, xh_dram, q_dram, ao_dram):
    nc = p.nc
    p.push_scope("l1_")
    dse, dso, gse2, gsev = p.dse, p.dso, p.gse2, p.gsev
    sv, ss, st, sg = p.sv, p.ss, p.st, p.sg
    cosT = p.sbuf("cosT", [128, S], F32)
    sinT = p.sbuf("sinT", [128, S], F32)
    am_sb = p.sbuf("am_sb", [128, 256], F32)
    am1_sb = p.sbuf("am1_sb", [128, 256], F32)
    rotm_sb = p.sbuf("rotm_sb", [128, 128], BF16)
    qb16 = p.sbuf("qb16", [128, 512], BF16)
    ones_k = p.sbuf("ones_k", [128, 1], BF16)
    ones_1 = p.sbuf("ones_1", [1, 512], BF16)
    ks1 = p.sbuf("ks1", [128, S], BF16)
    vs1 = p.sbuf("vs1", [128, (S // 128) * HD], BF16)
    arA = p.sbuf("arA", [128, 16384], BF16)
    arB = p.sbuf("arB", [128, 32768], BF16)
    _xt0 = p.sbuf("xt0", [128, D], F32)
    xt = [_xt0, _xt0]
    sq = p.sbuf("sq", [128, D], BF16)
    _xh0 = p.sbuf("xhat0", [128, D], BF16)
    xhat = [_xh0, _xh0]
    stat = p.sbuf("stat", [128, 4], F32)
    qf = p.sbuf("qf", [128, 512], F32)
    qt0 = p.sbuf("qt0", [128, 512], F32)
    qrt = [p.sbuf(f"qrt{i}", [128, 512], BF16) for i in range(2)]
    qtile = p.sbuf("qtile", [128, 16 * 256], BF16)
    probs = [p.sbuf(f"probs{i}", [128, 512], BF16) for i in range(2)]
    srec = p.sbuf("srec", [1, 512], F32)
    srecb = [p.sbuf(f"srecb{i}", [1, 512], BF16) for i in range(2)]
    bc_sb = p.sbuf("bc_sb", [128, 512], F32)
    ow = [p.sbuf(f"ow{i}", [128, 512], F32) for i in range(2)]
    blt = [p.sbuf(f"blt{i}", [128, 512], F32) for i in range(2)]
    aoC = p.sbuf("aoC", [128, 16 * 256], BF16)
    _aot0 = p.sbuf("aot0", [128, 16 * 128], BF16)
    aot = [_aot0, _aot0]
    ypt = [p.sbuf(f"ypt{i}", [128, 512], F32) for i in range(2)]

    ps_a = p.psum("ps_a", [128, 512], F32)
    ps_b = p.psum("ps_b", [128, 512], F32)
    ps_sc = [p.psum(f"ps_sc{i}", [128, 512], F32) for i in range(2)]
    ps_rot = p.psum("ps_rot", [128, 512], F32)
    ps_pv = p.psum("ps_pv", [128, 512], F32)
    ps_sum = p.psum("ps_sum", [1, 512], F32)
    ps_bc = p.psum("ps_bc", [128, 512], F32)

    with nc.Block() as blk:
        @blk.sync
        def _(sync):
            dse.inc(sync.dma_start(out=cosT[:, :], in_=cos_in[:, :]), 16)
            dse.inc(sync.dma_start(out=sinT[:, :], in_=sin_in[:, :]), 16)
            dse.inc(sync.dma_start(out=am_sb[:, :], in_=am_in[:, :]), 16)
            dse.inc(sync.dma_start(out=am1_sb[:, :], in_=am1_in[:, :]), 16)
            dse.inc(sync.dma_start(out=rotm_sb[:, :], in_=rotm_in[:, :]), 16)

        @blk.gpsimd
        def _(gpsimd):
            G = SerEng(gpsimd, sg)
            G.memset(ones_k[:, :], 1.0)
            G.memset(ones_1[:, :], 1.0)
        boot_g = sg.n

        c_eps = p.cf(EPS)
        c_im = p.cf(1.0 / D)
        c_scale = p.cf(SCALE)

        A_ld, A_sq, A_var, A_sd, A_xh, A_st = [], [], [], [], [], []
        for i in range(16):
            bi = i % 2

            @blk.sync
            def _(sync, i=i, bi=bi):
                sync.wait_ge(dse.h, dse.n)
                sync.wait_ge(dso.h, dso.n)
                if i >= 1:
                    sync.wait_ge(sv.h, A_xh[i - 1])
                dse.inc(sync.dma_start(out=xt[bi][:, :], in_=x_in[i * 128 : (i + 1) * 128, :]), 16)
            A_ld.append(dse.n)

            @blk.scalar
            def _(scalar, i=i, bi=bi):
                SS = SerEng(scalar, ss)
                scalar.wait_ge(dse.h, dse.n)
                scalar.wait_ge(dso.h, dso.n)
                if i >= 1:
                    scalar.wait_ge(sv.h, A_var[i - 1])
                SS.activation(out=sq[:, :], in_=xt[bi][:, :], func=AF.Identity,
                              accum_out=stat[:, 0:1])
                SS.activation(out=sq[:, :], in_=xt[bi][:, :], func=AF.Square,
                              accum_out=stat[:, 1:2])
            A_sq.append(ss.n)

            @blk.vector
            def _(vector, i=i, bi=bi):
                V = SerEng(vector, sv)
                vector.wait_ge(ss.h, A_sq[i])
                V.tensor_scalar(stat[:, 0:1], stat[:, 0:1], c_im, None, ALU.mult)
                V.tensor_scalar(stat[:, 1:2], stat[:, 1:2], c_im, None, ALU.mult)
                V.tensor_tensor(stat[:, 2:3], stat[:, 0:1], stat[:, 0:1], ALU.mult)
                V.tensor_tensor(stat[:, 2:3], stat[:, 1:2], stat[:, 2:3], ALU.subtract)
            A_var.append(sv.n)

            @blk.scalar
            def _(scalar, i=i):
                SS = SerEng(scalar, ss)
                scalar.wait_ge(sv.h, A_var[i])
                SS.activation(out=stat[:, 3:4], in_=stat[:, 2:3], func=AF.Sqrt, bias=c_eps)
            A_sd.append(ss.n)

            @blk.vector
            def _(vector, i=i, bi=bi):
                V = SerEng(vector, sv)
                vector.wait_ge(ss.h, A_sd[i])
                V.reciprocal(stat[:, 2:3], stat[:, 3:4])
                vector.wait_ge(dse.h, dse.n)
                vector.wait_ge(dso.h, dso.n)
                V.tensor_scalar(
                    xhat[bi][:, :], xt[bi][:, :], stat[:, 0:1], stat[:, 2:3], ALU.subtract, ALU.mult)
            A_xh.append(sv.n)

            @blk.sync
            def _(sync, i=i, bi=bi):
                sync.wait_ge(sv.h, A_xh[i])
                dso.inc(sync.dma_start(out=xh_dram[i * 128 : (i + 1) * 128, :], in_=xhat[bi][:, :]), 16)
            A_st.append(dso.n)
        stageA_st = dso.n

        pe_tg = []
        q_ready = []
        kv_ready = []
        ev_marks = {}
        for tg in range(4):
            tg0 = tg * 512

            @blk.sync
            def _(sync, tg=tg):
                sync.wait_ge(dso.h, A_st[4 * tg + 3])
                if tg > 0:
                    sync.wait_ge(st.h, pe_tg[tg - 1])
                for c in range(32):
                    dse.inc(sync.dma_start(
                        out=view(arA, c * 512, [[1, 512]]),
                        in_=xh_dram[tg0 : tg0 + 512, c * 128 : (c + 1) * 128],
                        transpose=True), 16)
            xhT_ld = dse.n

            tg_ev0 = sv.n
            fgroups = [("k", 0)] + [("q", g) for g in range(4)]
            ev_idx = 0
            for gi, (kind, g) in enumerate(fgroups):
                sb = gi % 2
                win = wk if kind == "k" else wq
                ncols = 128 if kind == "k" else 512

                @blk.gpsimd
                def _(gpsimd, tg=tg, g=g, sb=sb, gi=gi, win=win, ncols=ncols):
                    gpsimd.wait_ge(sg.h, boot_g)
                    key = ("peg", tg, gi - 2)
                    if key in ev_marks:
                        gpsimd.wait_ge(st.h, ev_marks[key])
                    elif tg > 0:
                        gpsimd.wait_ge(st.h, pe_tg[tg - 1])
                    gse2[sb].inc(gpsimd.dma_start(
                        out=view(arB, sb * 16384, [[ncols, 32], [1, ncols]]),
                        in_=wstrip_in(win, g * 512 if kind == "q" else 0, ncols)), 16)
                ev_marks[("ld", tg, gi)] = gse2[sb].n

                nsub = 1 if kind == "k" else 4
                for sub in range(nsub):
                    f = g * 4 + sub
                    idx = ev_idx
                    ev_idx += 1
                    ps = ps_a if idx % 2 == 0 else ps_b

                    @blk.tensor
                    def _(tensor, tg=tg, gi=gi, idx=idx, ps=ps, sb=sb, sub=sub, ncols=ncols):
                        tensor.wait_ge(gse2[sb].h, ev_marks[("ld", tg, gi)])
                        tensor.wait_ge(dse.h, dse.n)
                        tensor.wait_ge(dso.h, dso.n)
                        key = ("ev", tg, idx - 2)
                        tensor.wait_ge(sv.h, ev_marks.get(key, tg_ev0))
                        for c in range(32):
                            mm = tensor.matmul(
                                ps[:, :],
                                view(arB, sb * 16384 + c * ncols + sub * 128, [[1, 128]]),
                                view(arA, c * 512, [[1, 512]]),
                                start=(c == 0), stop=(c == 31))
                        st.inc(mm)
                    ev_marks[("peg", tg, gi)] = st.n
                    ev_marks[("pemm", tg, idx)] = st.n

                    @blk.vector
                    def _(vector, tg=tg, idx=idx, ps=ps):
                        V = SerEng(vector, sv)
                        vector.wait_ge(st.h, ev_marks[("pemm", tg, idx)])
                        V.tensor_copy(qb16[:, :], ps[:, :])
                    v1 = sv.n

                    @blk.tensor
                    def _(tensor, v1=v1):
                        tensor.wait_ge(sv.h, v1)
                        st.inc(tensor.matmul(ps_rot[:, :], rotm_sb[:, :], qb16[:, :],
                                             start=True, stop=True))
                    t1 = st.n

                    @blk.vector
                    def _(vector, tg=tg, f=f, kind=kind, idx=idx, ps=ps, tg0=tg0, t1=t1):
                        V = SerEng(vector, sv)
                        vector.wait_ge(st.h, t1)
                        V.tensor_tensor(qt0[:, :], ps_rot[:, :], sinT[:, tg0 : tg0 + 512], ALU.mult)
                        V.tensor_tensor(qf[:, :], ps[:, :], cosT[:, tg0 : tg0 + 512], ALU.mult)
                        if kind == "k":
                            dst = ks1[:, tg0 : tg0 + 512]
                            V.tensor_tensor(dst, qf[:, :], qt0[:, :], ALU.add)
                        else:
                            par = idx % 2
                            vector.wait_ge(dse.h, dse.n)
                            vector.wait_ge(dso.h, dso.n)
                            V.tensor_tensor(qrt[par][:, :], qf[:, :], qt0[:, :], ALU.add)
                    ev_marks[("ev", tg, idx)] = sv.n

                    if kind == "q":
                        @blk.sync
                        def _(sync, tg=tg, f=f, idx=idx, tg0=tg0):
                            sync.wait_ge(sv.h, ev_marks[("ev", tg, idx)])
                            dso.inc(sync.dma_start(
                                out=q_dram[f * 128 : (f + 1) * 128, tg0 : tg0 + 512],
                                in_=qrt[idx % 2][:, :]), 16)

            @blk.gpsimd
            def _(gpsimd, tg=tg):
                gpsimd.wait_ge(st.h, st.n)
                gsev.inc(gpsimd.dma_start(
                    out=view(arB, 0, [[128, 32], [1, 128]]),
                    in_=wstrip_in(wv, 0, 128)), 16)
            vld = gsev.n

            for sub in range(4):
                @blk.tensor
                def _(tensor, tg=tg, sub=sub, vld=vld):
                    tensor.wait_ge(gsev.h, vld)
                    tensor.wait_ge(sv.h, sv.n)
                    for c in range(32):
                        mm = tensor.matmul(
                            ps_a[0:128, 0:128],
                            view(arA, c * 512 + sub * 128, [[1, 128]]),
                            view(arB, c * 128, [[1, 128]]),
                            start=(c == 0), stop=(c == 31))
                    st.inc(mm)
                vpe = st.n

                @blk.vector
                def _(vector, tg=tg, sub=sub, vpe=vpe):
                    V = SerEng(vector, sv)
                    vector.wait_ge(st.h, vpe)
                    tt = tg * 4 + sub
                    V.tensor_copy(vs1[:, tt * HD : (tt + 1) * HD], ps_a[0:128, 0:128])
            pe_tg.append(st.n)
            q_ready.append(dso.n)
            kv_ready.append(sv.n)
        stageB_v = sv.n

        expc = {0: 0, 1: 0}
        pvc = {0: 0, 1: 0}
        last_ao_ev = 0
        wcount = 0
        last_ev_par = [sv.n, sv.n]
        pv_b = [ps_pv, ps_a]
        bc_b = [ps_bc, ps_b]
        for ch in range(8):
            if ch == 0:
                wins = [(0, 4, 1)]
            elif ch < 7:
                wins = [(2 * (ch - 1), 4, 0), (2 * ch, 4, 1)]
            else:
                wins = [(12, 4, 0), (14, 2, 1)]

            @blk.sync
            def _(sync, ch=ch):
                tg_need = min((2 * ch + 3) // 4, 3)
                sync.wait_ge(sv.h, kv_ready[max(tg_need, ch // 2)])
                sync.wait_ge(dso.h, q_ready[ch // 2])
                sync.wait_ge(st.h, st.n)
                for j in range(16):
                    dse.inc(sync.dma_start(
                        out=qtile[:, j * 256 : (j + 1) * 256],
                        in_=q_dram[j * 128 : (j + 1) * 128, ch * 256 : (ch + 1) * 256]), 16)
            qld = dse.n

            for j in range(0, 16, 2):
                for (kt0, nkts, wslot) in wins:
                    par = wcount % 2
                    pvx = pv_b[par]
                    bcx = bc_b[par]
                    sumx = ps_sum[0:1, :] if par == 0 else ps_rot[0:1, :]
                    for ki in range(nkts):
                        kt = kt0 + ki
                        pp = kt % 2

                        @blk.tensor
                        def _(tensor, j=j, kt=kt, pp=pp, qld=qld):
                            tensor.wait_ge(dse.h, dse.n)
                            tensor.wait_ge(dso.h, dso.n)
                            tensor.wait_ge(ss.h, expc[pp])
                            st.inc(tensor.matmul(
                                ps_sc[pp][:, :],
                                ks1[:, kt * 128 : (kt + 1) * 128],
                                qtile[:, j * 256 : (j + 2) * 256],
                                start=True, stop=True))
                        scd = st.n

                        @blk.scalar
                        def _(scalar, pp=pp, scd=scd):
                            SS = SerEng(scalar, ss)
                            scalar.wait_ge(st.h, scd)
                            scalar.wait_ge(st.h, pvc[pp])
                            SS.activation(
                                out=probs[pp][:, :], in_=ps_sc[pp][:, :],
                                func=AF.Sigmoid, scale=c_scale)
                        expc[pp] = ss.n
                        expd = ss.n

                        @blk.tensor
                        def _(tensor, kt=kt, pp=pp, expd=expd, ki=ki, nkts=nkts,
                              pvx=pvx, sumx=sumx, par=par):
                            tensor.wait_ge(ss.h, expd)
                            if ki == 0:
                                tensor.wait_ge(sv.h, last_ev_par[par])
                            tensor.matmul(
                                pvx[:, :],
                                vs1[:, kt * HD : (kt + 1) * HD],
                                probs[pp][:, :],
                                start=(ki == 0), stop=(ki == nkts - 1))
                            st.inc(tensor.matmul(
                                sumx, ones_k[:, :], probs[pp][:, :],
                                start=(ki == 0), stop=(ki == nkts - 1)))
                        pvc[pp] = st.n
                    pvdone = st.n

                    @blk.vector
                    def _(vector, pvdone=pvdone, sumx=sumx, par=par):
                        V = SerEng(vector, sv)
                        vector.wait_ge(st.h, pvdone)
                        V.reciprocal(srec[:, :], sumx)
                        V.tensor_copy(srecb[par][:, :], srec[:, :])
                    recd = sv.n

                    @blk.tensor
                    def _(tensor, recd=recd, bcx=bcx, par=par):
                        tensor.wait_ge(sv.h, recd)
                        tensor.wait_ge(sv.h, last_ev_par[par])
                        st.inc(tensor.matmul(
                            bcx[:, :], ones_1[0:1, 0:128], srecb[par][0:1, :],
                            start=True, stop=True))
                    bcd = st.n

                    @blk.vector
                    def _(vector, wslot=wslot, bcd=bcd, pvx=pvx, bcx=bcx):
                        V = SerEng(vector, sv)
                        vector.wait_ge(st.h, bcd)
                        V.tensor_copy(bc_sb[:, :], bcx[:, :])
                        V.tensor_tensor(ow[wslot][:, :], pvx[:, :], bc_sb[:, :], ALU.mult)
                    last_ev_par[par] = sv.n
                    last_ao_ev = sv.n
                    wcount += 1

                am2 = bass.AP(am_sb, 0, [[256, 128], [0, 2], [1, 256]])
                am12 = bass.AP(am1_sb, 0, [[256, 128], [0, 2], [1, 256]])

                @blk.vector
                def _(vector, ch=ch, j=j, nw=len(wins), am2=am2, am12=am12):
                    V = SerEng(vector, sv)
                    vector.wait_ge(dse.h, dse.n)
                    vector.wait_ge(dso.h, dso.n)
                    dst = aoC[:, j * 256 : (j + 2) * 256]
                    if nw == 1:
                        V.tensor_copy(dst, ow[1][:, :])
                    else:
                        V.tensor_tensor(blt[0][:, :], ow[0][:, :], am12, ALU.mult)
                        V.tensor_tensor(blt[1][:, :], ow[1][:, :], am2, ALU.mult)
                        V.tensor_tensor(dst, blt[0][:, :], blt[1][:, :], ALU.add)
                last_ao_ev = sv.n

                @blk.sync
                def _(sync, ch=ch, j=j, aoev=last_ao_ev):
                    sync.wait_ge(sv.h, aoev)
                    dso.inc(sync.dma_start(
                        out=bass.AP(ao_dram, (j * 128) * S + ch * 256,
                                    [[S, 128], [128 * S, 2], [1, 256]]),
                        in_=bass.AP(aoC, j * 256, [[4096, 128], [256, 2], [1, 256]])), 16)
        stageC_ao = dse.n

        aold = {}
        ypst = {}
        st_d0 = st.n
        for fg in range(8):
            sb = fg % 2

            @blk.gpsimd
            def _(gpsimd, fg=fg, sb=sb, st_d0=st_d0):
                if fg < 2:
                    gpsimd.wait_ge(st.h, st_d0)
                else:
                    gpsimd.wait_ge(st.h, aold[("pe", fg - 2, 15)])
                gse2[sb].inc(gpsimd.dma_start(
                    out=view(arB, sb * 8192, [[512, 16], [1, 512]]),
                    in_=bass.AP(wo, fg * 512, [[D, 128], [128 * D, 16], [1, 512]])), 16)
            wold = gse2[sb].n

            for tt in range(16):
                par = tt % 2

                @blk.sync
                def _(sync, fg=fg, tt=tt, par=par):
                    if fg == 0 and tt == 0:
                        sync.wait_ge(dse.h, dse.n)
                        sync.wait_ge(dso.h, dso.n)
                    key = ("pe", fg, tt - 1) if tt >= 1 else ("pe", fg - 1, tt + 15)
                    if key in aold:
                        sync.wait_ge(st.h, aold[key])
                    dse.inc(sync.dma_start(
                        out=aot[par][:, :],
                        in_=bass.AP(ao_dram, tt * 128, [[S, 128], [128 * S, 16], [1, 128]])), 16)
                aold[("ld", fg, tt)] = dse.n

                @blk.tensor
                def _(tensor, fg=fg, tt=tt, par=par, wold=wold, sb=sb):
                    tensor.wait_ge(gse2[sb].h, wold)
                    tensor.wait_ge(dse.h, dse.n)
                    tensor.wait_ge(dso.h, dso.n)
                    tensor.wait_ge(sv.h, sv.n)
                    for c in range(16):
                        mm = tensor.matmul(
                            ps_a[:, :],
                            view(aot[par], c * 128, [[1, 128]]),
                            view(arB, sb * 8192 + c * 512, [[1, 512]]),
                            start=(c == 0), stop=(c == 15))
                    st.inc(mm)
                aold[("pe", fg, tt)] = st.n

                @blk.vector
                def _(vector, fg=fg, tt=tt, par=par):
                    V = SerEng(vector, sv)
                    vector.wait_ge(st.h, aold[("pe", fg, tt)])
                    vector.wait_ge(dse.h, dse.n)
                    vector.wait_ge(dso.h, dso.n)
                    V.tensor_copy(ypt[par][:, :], ps_a[:, :])
                ypev = sv.n

                @blk.sync
                def _(sync, fg=fg, tt=tt, par=par, ypev=ypev):
                    sync.wait_ge(sv.h, ypev)
                    dso.inc(sync.dma_start(
                        out=yp_out[tt * 128 : (tt + 1) * 128, fg * 512 : (fg + 1) * 512],
                        in_=ypt[par][:, :]), 16)
                ypst[("st", fg, tt)] = dse.n

    p.pop_scope()


def l1attn_inmaps(x2_full, Wq, Wk, Wv, Wo, consts):
    cos, sin = _rope_l1()
    am, am1 = _alpha_tables()
    WqT = np.ascontiguousarray(Wq.T)
    WkT = np.ascontiguousarray(Wk.T)
    WvT = np.ascontiguousarray(Wv.T)
    WoT = np.ascontiguousarray(Wo.T)
    maps = []
    for c in range(8):
        b, hh = c // 2, c % 2
        m = dict(consts)
        m["xb2"] = np.ascontiguousarray(x2_full[b])
        m["wqT_s"] = np.ascontiguousarray(WqT[:, hh * 2048 : (hh + 1) * 2048])
        m["wkT"] = WkT
        m["wvT"] = WvT
        m["woT_s"] = np.ascontiguousarray(WoT[hh * 2048 : (hh + 1) * 2048, :])
        m["cos1_t"] = cos
        m["sin1_t"] = sin
        m["am_t"] = am
        m["am1_t"] = am1
        m["rotm_t"] = _rotm_l1()
        maps.append(m)
    return maps


def emit_ffn1(p, x_in, ps_in, w1_in, bias_in, x4_out,
              x3_dram, xh3_dram, yd_dram):
    nc = p.nc
    p.push_scope("f1_")
    dse, dso, sv, ss, st, cc, gsev = p.dse, p.dso, p.sv, p.ss, p.st, p.cc, p.gsev
    ln2T = p.sbuf("ln2T", [128, 32 * 1024], BF16)
    arW = p.sbuf("arW", [128, 2 * 32 * 512], BF16)
    ta = p.sbuf("ta", [128, D], F32)
    tb = p.sbuf("tb", [128, D], F32)
    tc = p.sbuf("tc", [128, D], F32)
    sq = p.sbuf("sq", [128, D], BF16)
    stat = p.sbuf("stat", [128, 8], F32)
    biasbc = p.sbuf("biasbc", [128, 1024], F32)
    ypt = [p.sbuf(f"ypt{i}", [128, 512], F32) for i in range(2)]
    ps_a = p.psum("ps_a", [128, 512], F32)
    ps_b = p.psum("ps_b", [128, 512], F32)

    with nc.Block() as blk:
        c_eps = p.cf(EPS)
        c_im = p.cf(1.0 / D)

        for i in range(8):
            r0 = i * 128

            @blk.sync
            def _(sync, i=i, r0=r0):
                sync.wait_ge(sv.h, sv.n)
                sync.wait_ge(dse.h, dse.n)
                sync.wait_ge(dso.h, dso.n)
                dse.inc(sync.dma_start(out=ta[:, :], in_=x_in[r0 : r0 + 128, :]), 16)
                dse.inc(sync.dma_start(out=tb[:, :], in_=ps_in[r0 : r0 + 128, :]), 16)

            @blk.scalar
            def _(scalar, i=i):
                SS = SerEng(scalar, ss)
                scalar.wait_ge(sv.h, sv.n)
                scalar.wait_ge(dse.h, dse.n)
                scalar.wait_ge(dso.h, dso.n)
                SS.activation(out=sq[:, :], in_=tb[:, :], func=AF.Identity,
                              accum_out=stat[:, 0:1])
                SS.activation(out=sq[:, :], in_=tb[:, :], func=AF.Square,
                              accum_out=stat[:, 1:2])

            @blk.vector
            def _(vector, i=i):
                V = SerEng(vector, sv)
                vector.wait_ge(ss.h, ss.n)
                V.tensor_scalar(stat[:, 0:1], stat[:, 0:1], c_im, None, ALU.mult)
                V.tensor_scalar(stat[:, 1:2], stat[:, 1:2], c_im, None, ALU.mult)
                V.tensor_tensor(stat[:, 2:3], stat[:, 0:1], stat[:, 0:1], ALU.mult)
                V.tensor_tensor(stat[:, 2:3], stat[:, 1:2], stat[:, 2:3], ALU.subtract)

            @blk.scalar
            def _(scalar, i=i):
                SS = SerEng(scalar, ss)
                scalar.wait_ge(sv.h, sv.n)
                SS.activation(out=stat[:, 3:4], in_=stat[:, 2:3], func=AF.Sqrt, bias=c_eps)

            @blk.vector
            def _(vector, i=i, r0=r0):
                V = SerEng(vector, sv)
                vector.wait_ge(ss.h, ss.n)
                V.reciprocal(stat[:, 3:4], stat[:, 3:4])
                V.tensor_scalar(tc[:, :], tb[:, :], stat[:, 0:1], stat[:, 3:4], ALU.subtract, ALU.mult)
                V.tensor_tensor(ta[:, :], ta[:, :], tc[:, :], ALU.add)

            @blk.sync
            def _(sync, i=i, r0=r0):
                sync.wait_ge(sv.h, sv.n)
                dso.inc(sync.dma_start(out=x3_dram[r0 : r0 + 128, :], in_=ta[:, :]), 16)

            @blk.scalar
            def _(scalar, i=i):
                SS = SerEng(scalar, ss)
                scalar.wait_ge(sv.h, sv.n)
                scalar.wait_ge(dse.h, dse.n)
                scalar.wait_ge(dso.h, dso.n)
                SS.activation(out=sq[:, :], in_=ta[:, :], func=AF.Identity,
                              accum_out=stat[:, 0:1])
                SS.activation(out=sq[:, :], in_=ta[:, :], func=AF.Square,
                              accum_out=stat[:, 1:2])

            @blk.vector
            def _(vector, i=i):
                V = SerEng(vector, sv)
                vector.wait_ge(ss.h, ss.n)
                V.tensor_scalar(stat[:, 0:1], stat[:, 0:1], c_im, None, ALU.mult)
                V.tensor_scalar(stat[:, 1:2], stat[:, 1:2], c_im, None, ALU.mult)
                V.tensor_tensor(stat[:, 2:3], stat[:, 0:1], stat[:, 0:1], ALU.mult)
                V.tensor_tensor(stat[:, 2:3], stat[:, 1:2], stat[:, 2:3], ALU.subtract)

            @blk.scalar
            def _(scalar, i=i):
                SS = SerEng(scalar, ss)
                scalar.wait_ge(sv.h, sv.n)
                SS.activation(out=stat[:, 3:4], in_=stat[:, 2:3], func=AF.Sqrt, bias=c_eps)

            @blk.vector
            def _(vector, i=i):
                V = SerEng(vector, sv)
                vector.wait_ge(ss.h, ss.n)
                V.reciprocal(stat[:, 3:4], stat[:, 3:4])
                V.tensor_scalar(sq[:, :], ta[:, :], stat[:, 0:1], stat[:, 3:4], ALU.subtract, ALU.mult)

            @blk.sync
            def _(sync, i=i, r0=r0):
                sync.wait_ge(sv.h, sv.n)
                dso.inc(sync.dma_start(out=xh3_dram[r0 : r0 + 128, :], in_=sq[:, :]), 16)

            if i in (3, 7):
                hb = 0 if i == 3 else 512

                @blk.sync
                def _(sync, hb=hb):
                    sync.wait_ge(dso.h, dso.n)
                    for c in range(32):
                        dse.inc(sync.dma_start(
                            out=view(ln2T, c * 1024 + hb, [[1, 512]]),
                            in_=xh3_dram[hb : hb + 512, c * 128 : (c + 1) * 128],
                            transpose=True), 16)

        wmk = {}
        bmk = {}
        for og in range(8):
            wh_ = og % 2

            @blk.sync
            def _(sync, og=og, wh_=wh_):
                if og == 0:
                    sync.wait_ge(dso.h, dso.n)
                    sync.wait_ge(cc.h, cc.n)
                key = ("pe", og - 2)
                if key in wmk:
                    sync.wait_ge(st.h, wmk[key])
                    sync.wait_ge(sv.h, bmk[og - 2])
                for c in range(32):
                    dse.inc(sync.dma_start(
                        out=view(arW, wh_ * 16384 + c * 512, [[1, 512]]),
                        in_=w1_in[og * 512 : (og + 1) * 512, c * 128 : (c + 1) * 128],
                        transpose=True), 16)
                dse.inc(sync.dma_start(
                    out=biasbc[:, wh_ * 512 : wh_ * 512 + 512],
                    in_=bass.AP(bias_in, og * 512, [[0, 128], [1, 512]])), 16)
                sync.wait_ge(dse.h, dse.n)
            wld = dse.n

            for tsub in range(8):
                pp = tsub % 2
                ps = ps_a if pp == 0 else ps_b

                @blk.tensor
                def _(tensor, og=og, tsub=tsub, ps=ps, wh_=wh_, wld=wld):
                    tensor.wait_ge(dse.h, wld)
                    tensor.wait_ge(sv.h, sv.n)
                    for c in range(32):
                        mm = tensor.matmul(
                            ps[:, :],
                            view(ln2T, c * 1024 + tsub * 128, [[1, 128]]),
                            view(arW, wh_ * 16384 + c * 512, [[1, 512]]),
                            start=(c == 0), stop=(c == 31))
                    st.inc(mm)
                ped = st.n

                @blk.vector
                def _(vector, tsub=tsub, ped=ped, pp=pp, ps=ps, wh_=wh_):
                    V = SerEng(vector, sv)
                    vector.wait_ge(st.h, ped)
                    vector.wait_ge(gsev.h, gsev.n)
                    V.tensor_tensor(ypt[pp][:, :], ps[:, :],
                                    biasbc[:, wh_ * 512 : wh_ * 512 + 512], ALU.add)
                yev = sv.n

                @blk.gpsimd
                def _(gpsimd, og=og, tsub=tsub, yev=yev, pp=pp):
                    gpsimd.wait_ge(sv.h, yev)
                    gsev.inc(gpsimd.dma_start(
                        out=yd_dram[tsub * 128 : (tsub + 1) * 128, og * 512 : (og + 1) * 512],
                        in_=ypt[pp][:, :]), 16)
            wmk[("pe", og)] = st.n
            bmk[og] = sv.n

        for i in range(8):
            r0 = i * 128

            @blk.sync
            def _(sync, i=i, r0=r0):
                sync.wait_ge(dse.h, dse.n)
                sync.wait_ge(dso.h, dso.n)
                sync.wait_ge(gsev.h, gsev.n)
                sync.wait_ge(sv.h, sv.n)
                dse.inc(sync.dma_start(out=ta[:, :], in_=x3_dram[r0 : r0 + 128, :]), 16)
                dse.inc(sync.dma_start(out=tb[:, :], in_=yd_dram[r0 : r0 + 128, :]), 16)

            @blk.scalar
            def _(scalar, i=i):
                SS = SerEng(scalar, ss)
                scalar.wait_ge(dse.h, dse.n)
                scalar.wait_ge(dso.h, dso.n)
                scalar.wait_ge(sv.h, sv.n)
                SS.activation(out=sq[:, :], in_=tb[:, :], func=AF.Identity,
                              accum_out=stat[:, 0:1])
                SS.activation(out=sq[:, :], in_=tb[:, :], func=AF.Square,
                              accum_out=stat[:, 1:2])

            @blk.vector
            def _(vector, i=i):
                V = SerEng(vector, sv)
                vector.wait_ge(dse.h, dse.n)
                vector.wait_ge(dso.h, dso.n)
                vector.wait_ge(ss.h, ss.n)
                V.tensor_scalar(stat[:, 0:1], stat[:, 0:1], c_im, None, ALU.mult)
                V.tensor_scalar(stat[:, 1:2], stat[:, 1:2], c_im, None, ALU.mult)
                V.tensor_tensor(stat[:, 2:3], stat[:, 0:1], stat[:, 0:1], ALU.mult)
                V.tensor_tensor(stat[:, 2:3], stat[:, 1:2], stat[:, 2:3], ALU.subtract)

            @blk.scalar
            def _(scalar, i=i):
                SS = SerEng(scalar, ss)
                scalar.wait_ge(sv.h, sv.n)
                SS.activation(out=stat[:, 3:4], in_=stat[:, 2:3], func=AF.Sqrt, bias=c_eps)

            @blk.vector
            def _(vector, i=i):
                V = SerEng(vector, sv)
                vector.wait_ge(ss.h, ss.n)
                V.reciprocal(stat[:, 3:4], stat[:, 3:4])
                V.tensor_scalar(tc[:, :], tb[:, :], stat[:, 0:1], stat[:, 3:4], ALU.subtract, ALU.mult)
                V.tensor_tensor(ta[:, :], ta[:, :], tc[:, :], ALU.add)

            @blk.sync
            def _(sync, i=i, r0=r0):
                sync.wait_ge(sv.h, sv.n)
                dso.inc(sync.dma_start(out=x4_out[r0 : r0 + 128, :], in_=ta[:, :]), 16)

        @blk.sync
        def _(sync):
            sync.wait_ge(dse.h, dse.n)
            sync.wait_ge(dso.h, dso.n)

    p.pop_scope()


def ffn1_inmaps(x2_full, yp0, yp1, w1_full, bias, consts):
    maps = []
    for c in range(8):
        b, half = c // 2, c % 2
        rows = slice(half * 1024, (half + 1) * 1024)
        m = dict(consts)
        m["x_r"] = np.ascontiguousarray(x2_full[b][rows])
        m["p0_r"] = np.ascontiguousarray(yp0[b][rows])
        m["p1_r"] = np.ascontiguousarray(yp1[b][rows])
        m["w1_f"] = w1_full
        m["bias"] = bias
        maps.append(m)
    return maps


def build_fused():
    p = PB()
    nc = p.nc
    EI, IN = "ExternalInput", "Internal"
    xb = nc.dram_tensor("xb", [S, D], F32, kind=EI)
    x_half = nc.dram_tensor("x_half", [1024, D], F32, kind=EI)
    wq_s = nc.dram_tensor("wq_s", [D, 2048], F32, kind=EI)
    wk_s = nc.dram_tensor("wk_s", [D, 512], F32, kind=EI)
    wv_s = nc.dram_tensor("wv_s", [D, 512], F32, kind=EI)
    wo_s = nc.dram_tensor("wo_s", [2048, D], F32, kind=EI)
    cos_t = nc.dram_tensor("cos_t", [HD, S], F32, kind=EI)
    sin_t = nc.dram_tensor("sin_t", [HD, S], F32, kind=EI)
    tril_t = nc.dram_tensor("tril_t", [4 * 128, 512], BF16, kind=EI)
    qw_s = nc.dram_tensor("qw_s", [8, 512, 128], I32, kind=EI)
    lut4_s = nc.dram_tensor("lut4_s", [512, 16], F32, kind=EI)
    lut8_s = nc.dram_tensor("lut8_s", [512, 256], F32, kind=EI)
    w1q_s = nc.dram_tensor("w1q_s", [512, D], I32, kind=EI)
    sc_s = nc.dram_tensor("sc_s", [512, 32], F32, kind=EI)
    zr_s = nc.dram_tensor("zr_s", [512, 32], F32, kind=EI)
    mk_s = nc.dram_tensor("mk_s", [512, D], I32, kind=EI)
    s2_s = nc.dram_tensor("s2_s", [512, 1], F32, kind=EI)
    shf = nc.dram_tensor("shf", [128, 32], I32, kind=EI)
    kramp = nc.dram_tensor("kramp", [128, 256], F32, kind=EI)
    bias0 = nc.dram_tensor("bias0", [D], F32, kind=EI)
    bias1 = nc.dram_tensor("bias1", [D], F32, kind=EI)
    wqT_s = nc.dram_tensor("wqT_s", [D, 2048], F32, kind=EI)
    wkT = nc.dram_tensor("wkT", [D, HD], F32, kind=EI)
    wvT = nc.dram_tensor("wvT", [D, HD], F32, kind=EI)
    woT_s = nc.dram_tensor("woT_s", [2048, D], F32, kind=EI)
    cos1_t = nc.dram_tensor("cos1_t", [HD, S], F32, kind=EI)
    sin1_t = nc.dram_tensor("sin1_t", [HD, S], F32, kind=EI)
    am_t = nc.dram_tensor("am_t", [128, 256], F32, kind=EI)
    am1_t = nc.dram_tensor("am1_t", [128, 256], F32, kind=EI)
    rotm_t = nc.dram_tensor("rotm_t", [128, 128], BF16, kind=EI)
    x4_r = nc.dram_tensor("x4_r", [1024, D], F32, kind="ExternalOutput")
    wl_sl = nc.dram_tensor("wl_sl", [512, D], BF16, kind=IN)
    wh_sl = nc.dram_tensor("wh_sl", [512, D], BF16, kind=IN)
    w1_sl = nc.dram_tensor("w1_sl", [512, D], BF16, kind=IN)
    wl_full = nc.dram_tensor("wl_full", [D, D], BF16, kind=IN)
    wh_full = nc.dram_tensor("wh_full", [D, D], BF16, kind=IN)
    w1_full = nc.dram_tensor("w1_full", [D, D], BF16, kind=IN)
    xh0_d = nc.dram_tensor("xh0_d", [S, D], BF16, kind=IN)
    q0_d = nc.dram_tensor("q0_d", [2048, S], BF16, kind=IN)
    ao0_d = nc.dram_tensor("ao0_d", [2048, S], BF16, kind=IN)
    yp0_d = nc.dram_tensor("yp0_d", [S, D], F32, kind=IN)
    psum0_d = nc.dram_tensor("psum0_d", [1024, D], F32, kind=IN)
    x1_d = nc.dram_tensor("x1_d", [1024, D], F32, kind=IN)
    xh2_d = nc.dram_tensor("xh2_d", [1024, D], BF16, kind=IN)
    ydl_d = nc.dram_tensor("ydl_d", [1024, D], F32, kind=IN)
    ydh_d = nc.dram_tensor("ydh_d", [1024, D], F32, kind=IN)
    x2h_d = nc.dram_tensor("x2h_d", [1024, D], F32, kind=IN)
    x2f_d = nc.dram_tensor("x2f_d", [S, D], F32, kind=IN)
    xh1_d = nc.dram_tensor("xh1_d", [S, D], BF16, kind=IN)
    q1_d = nc.dram_tensor("q1_d", [2048, S], BF16, kind=IN)
    ao1_d = nc.dram_tensor("ao1_d", [2048, S], BF16, kind=IN)
    yp1_d = nc.dram_tensor("yp1_d", [S, D], F32, kind=IN)
    psum1_d = nc.dram_tensor("psum1_d", [1024, D], F32, kind=IN)
    x3_d = nc.dram_tensor("x3_d", [1024, D], F32, kind=IN)
    xh3_d = nc.dram_tensor("xh3_d", [1024, D], BF16, kind=IN)
    yd_d = nc.dram_tensor("yd_d", [1024, D], F32, kind=IN)

    p.alloc_consts()
    p.make_sems()
    G8 = [list(range(8))]
    GP = [[0, 1], [2, 3], [4, 5], [6, 7]]

    with nc.Block() as blk:
        @blk.sync
        def _(sync):
            p.load_consts(sync, p.dse)

    emit_dequant(p, qw_s, lut4_s, lut8_s, w1q_s, sc_s, zr_s, mk_s, s2_s,
                 shf, kramp, wl_sl, wh_sl, w1_sl)

    with nc.Block() as blk:
        @blk.gpsimd
        def _(g):
            g.wait_ge(p.dse.h, p.dse.n)
            g.wait_ge(p.dso.h, p.dso.n)
            p.cc.inc(g.collective_compute(
                "AllGather", ALU.bypass, replica_groups=G8,
                ins=[wl_sl.ap().opt()], outs=[wl_full.ap().opt()]))
            p.cc.inc(g.collective_compute(
                "AllGather", ALU.bypass, replica_groups=G8,
                ins=[wh_sl.ap().opt()], outs=[wh_full.ap().opt()]))
            p.cc.inc(g.collective_compute(
                "AllGather", ALU.bypass, replica_groups=G8,
                ins=[w1_sl.ap().opt()], outs=[w1_full.ap().opt()]))
    p.barrier(skip_cc=True)

    emit_l0attn(p, xb, wq_s, wk_s, wv_s, wo_s, cos_t, sin_t, tril_t, yp0_d,
                xh0_d, q0_d, ao0_d)

    with nc.Block() as blk:
        @blk.gpsimd
        def _(g):
            g.wait_ge(p.dse.h, p.dse.n)
            g.wait_ge(p.dso.h, p.dso.n)
            p.cc.inc(g.collective_compute(
                "ReduceScatter", ALU.add, replica_groups=GP,
                ins=[yp0_d.ap().opt()], outs=[psum0_d.ap().opt()]))
    p.barrier()

    emit_ffn0(p, x_half, psum0_d, wl_full, wh_full, bias0, x2h_d,
              x1_d, xh2_d, ydl_d, ydh_d)

    with nc.Block() as blk:
        @blk.gpsimd
        def _(g):
            g.wait_ge(p.dse.h, p.dse.n)
            g.wait_ge(p.dso.h, p.dso.n)
            p.cc.inc(g.collective_compute(
                "AllGather", ALU.bypass, replica_groups=GP,
                ins=[x2h_d.ap().opt()], outs=[x2f_d.ap().opt()]))
    p.barrier()

    emit_l1attn(p, x2f_d, wqT_s, wkT, wvT, woT_s, cos1_t, sin1_t, am_t, am1_t,
                rotm_t, yp1_d, xh1_d, q1_d, ao1_d)

    with nc.Block() as blk:
        @blk.gpsimd
        def _(g):
            g.wait_ge(p.dse.h, p.dse.n)
            g.wait_ge(p.dso.h, p.dso.n)
            p.cc.inc(g.collective_compute(
                "ReduceScatter", ALU.add, replica_groups=GP,
                ins=[yp1_d.ap().opt()], outs=[psum1_d.ap().opt()]))
    p.barrier()

    emit_ffn1(p, x2h_d, psum1_d, w1_full, bias1, x4_r,
              x3_d, xh3_d, yd_d)

    p.barrier()
    p.close()
    return p


def fused_inmaps(z, consts):
    x = np.asarray(z["L0_attn_x"], dtype=np.float32)
    w_qkv = np.asarray(z["L0_attn_w_qkv"], dtype=np.float32)
    w_o = np.asarray(z["L0_attn_w_o"], dtype=np.float32)
    cos0, sin0 = _rope_l0()
    tril = _tril_masks()
    cos1, sin1 = _rope_l1()
    am, am1 = _alpha_tables()
    rotm = _rotm_l1()
    WqT = np.ascontiguousarray(np.asarray(z["L1_attn_Wq"], dtype=np.float32).T)
    WkT = np.ascontiguousarray(np.asarray(z["L1_attn_Wk"], dtype=np.float32).T)
    WvT = np.ascontiguousarray(np.asarray(z["L1_attn_Wv"], dtype=np.float32).T)
    WoT = np.ascontiguousarray(np.asarray(z["L1_attn_Wo"], dtype=np.float32).T)
    shf_a = np.broadcast_to(np.arange(32, dtype=np.int32), (128, 32)).copy()
    kramp_a = np.broadcast_to(np.arange(256, dtype=np.float32), (128, 256)).copy()
    maps = []
    for c in range(8):
        b, hh = c // 2, c % 2
        r = slice(c * 512, (c + 1) * 512)
        m = dict(consts)
        m["xb"] = np.ascontiguousarray(x[b])
        m["x_half"] = np.ascontiguousarray(x[b][hh * 1024 : (hh + 1) * 1024])
        m["wq_s"] = np.ascontiguousarray(w_qkv[:, hh * 2048 : (hh + 1) * 2048])
        m["wk_s"] = np.ascontiguousarray(w_qkv[:, 4096 + hh * 512 : 4096 + (hh + 1) * 512])
        m["wv_s"] = np.ascontiguousarray(w_qkv[:, 5120 + hh * 512 : 5120 + (hh + 1) * 512])
        m["wo_s"] = np.ascontiguousarray(w_o[hh * 2048 : (hh + 1) * 2048, :])
        m["cos_t"] = cos0
        m["sin_t"] = sin0
        m["tril_t"] = tril
        m["qw_s"] = np.ascontiguousarray(np.asarray(z["L0_ffn_qweight"])[:, r, :])
        m["lut4_s"] = np.ascontiguousarray(np.asarray(z["L0_ffn_lut4"], dtype=np.float32)[r])
        m["lut8_s"] = np.ascontiguousarray(np.asarray(z["L0_ffn_lut8"], dtype=np.float32)[r])
        m["w1q_s"] = np.ascontiguousarray(np.asarray(z["L1_ffn_W_q"])[r])
        m["sc_s"] = np.ascontiguousarray(np.asarray(z["L1_ffn_scales"], dtype=np.float32)[r, :, 0])
        m["zr_s"] = np.ascontiguousarray(np.asarray(z["L1_ffn_zeros"], dtype=np.float32)[r, :, 0])
        m["mk_s"] = np.ascontiguousarray(np.asarray(z["L1_ffn_mask"])[r])
        m["s2_s"] = np.ascontiguousarray(np.asarray(z["L1_ffn_scale2"], dtype=np.float32)[r])
        m["shf"] = shf_a
        m["kramp"] = kramp_a
        m["bias0"] = np.asarray(z["L0_ffn_bias"], dtype=np.float32)
        m["bias1"] = np.asarray(z["L1_ffn_bias"], dtype=np.float32)
        m["wqT_s"] = np.ascontiguousarray(WqT[:, hh * 2048 : (hh + 1) * 2048])
        m["wkT"] = WkT
        m["wvT"] = WvT
        m["woT_s"] = np.ascontiguousarray(WoT[hh * 2048 : (hh + 1) * 2048, :])
        m["cos1_t"] = cos1
        m["sin1_t"] = sin1
        m["am_t"] = am
        m["am1_t"] = am1
        m["rotm_t"] = rotm
        maps.append(m)
    return maps


def kernel(**inputs):
    p = build_fused()
    maps = fused_inmaps(inputs, p.const_arrays())
    res = run_bass_kernel_spmd(p.nc, maps, core_ids=list(range(8)))
    out = np.stack([
        np.concatenate(
            [np.asarray(res.results[2 * b]["x4_r"]), np.asarray(res.results[2 * b + 1]["x4_r"])],
            axis=0)
        for b in range(4)])
    return np.ascontiguousarray(out.astype(np.float32))


# version 11 (final)
# speedup vs baseline: 1.0624x; 1.0212x over previous
import math
from contextlib import ExitStack

import numpy as np
import ml_dtypes

import concourse.bass as bass
import concourse.mybir as mybir
from concourse.bass_utils import run_bass_kernel_spmd

F32 = mybir.dt.float32
BF16 = mybir.dt.bfloat16
I32 = mybir.dt.int32
AF = mybir.ActivationFunctionType
ALU = mybir.AluOpType
BF = ml_dtypes.bfloat16
AX = mybir.AxisListType.X

B, S, D = 4, 2048, 4096
NH, NKV, HD = 32, 8, 128
EPS = 1e-5
SM_SHIFT = 24.0
NCORES = 8
SCALE = 1.0 / math.sqrt(HD)

PAIR_SWAP = [1, 0, 3, 2, 5, 4, 7, 6, 9, 8, 11, 10, 13, 12, 15, 14,
             17, 16, 19, 18, 21, 20, 23, 22, 25, 24, 27, 26, 29, 28, 31, 30]


def _make_lut2acc():
    import concourse.dve_ops as _dve_ops
    from concourse.dve_op import DveOp as _DveOp
    from concourse.dve_spec import Spec as _Spec, lower as _lower
    from concourse.dve_spec import Src0, Src1, C0, C1, C2, Zero, One, eq as _eq

    a = _eq(Src0, C2) * C0
    b = _eq(Src0, C2 + One) * C1
    body = (Src1 + a) + b
    spec = _Spec(body=body)
    perf = {}
    shas = {}
    for ver in ("v3", "v4"):
        try:
            shas[ver] = _lower(spec, ver=ver, perf_en=True, name="LUT2ACC").sha(ver)
            perf[ver] = True
        except Exception:
            shas[ver] = _lower(spec, ver=ver, perf_en=False, name="LUT2ACC").sha(ver)
            perf[ver] = False
    op = _DveOp(
        "LUT2ACC", spec, subdim=False, uops_sha=shas, perf_en=perf,
        reference=lambda in0, in1, s0, s1, imm2:
            in1 + (in0 == imm2) * s0 + (in0 == (imm2 + 1.0)) * s1)
    if not any(o.name == "LUT2ACC" for o in _dve_ops.OPS):
        _dve_ops.OPS.append(op)
        _dve_ops._SUB_OPCODE_FOR_NAME[op.name] = (
            _dve_ops._CUSTOM_DVE_ROW_BASE + len(_dve_ops.OPS) - 1)
    return op


LUT2ACC = _make_lut2acc()


def _rope_l0():
    inv = 1.0 / 10000.0 ** (np.arange(0, HD, 2, dtype=np.float32) / HD)
    ang = np.arange(S, dtype=np.float32)[None, :] * inv[:, None]
    cos = np.zeros((HD, S), dtype=np.float32)
    sin = np.zeros((HD, S), dtype=np.float32)
    cos[0::2] = np.cos(ang)
    cos[1::2] = np.cos(ang)
    sin[0::2] = -np.sin(ang)
    sin[1::2] = np.sin(ang)
    return cos, sin


def _rope_l1():
    inv = 1.0 / 10000.0 ** (np.arange(0, HD, 2, dtype=np.float32) / HD)
    f = np.arange(S, dtype=np.float32)[None, :] * inv[:, None]
    cos = np.zeros((HD, S), dtype=np.float32)
    sin = np.zeros((HD, S), dtype=np.float32)
    cos[0::2] = np.cos(f)
    cos[1::2] = np.cos(f)
    sin[0::2] = np.sin(f)
    sin[1::2] = np.sin(f)
    return cos, sin


def _rotm_l1():
    P = np.zeros((128, 128), dtype=np.float32)
    for d_ in range(64):
        P[d_, d_ + 64] = -1.0
    for d_ in range(64, 128):
        P[d_, d_ - 64] = 1.0
    return np.ascontiguousarray(P.T.astype(BF))


def _tril_masks():
    kp = np.arange(128)[:, None]
    qf = np.arange(512)[None, :]
    m = np.stack([(qf >= j * 128 + kp) for j in range(4)]).astype(np.float32)
    return m.astype(BF).reshape(4 * 128, 512)


def _alpha_tables():
    a = np.linspace(0.0, 1.0, 256, dtype=np.float32)
    am = np.broadcast_to(a, (128, 256)).copy()
    am1 = np.broadcast_to(1.0 - a, (128, 256)).copy()
    return am, am1


class Sem:
    def __init__(self, h):
        self.h = h
        self.n = 0

    def inc(self, instr, k=1):
        instr.then_inc(self.h, k)
        self.n += k
        return self.n


class SerEng:

    _PASS = {"wait_ge", "nop", "dma_start", "dma_start_transpose", "wait_op"}

    def __init__(self, eng, sem):
        self._e = eng
        self._s = sem

    def __getattr__(self, name):
        f = getattr(self._e, name)
        if name in SerEng._PASS or name.startswith("_"):
            return f

        def wrapped(*a, **k):
            self._e.wait_ge(self._s.h, self._s.n)
            ins = f(*a, **k)
            ins.then_inc(self._s.h, 1)
            self._s.n += 1
            return ins

        return wrapped


class PB:
    def __init__(self):
        self.nc = bass.Bass("TRN2", target_bir_lowering=False)
        self.ctx = ExitStack()
        self.fvals = []
        self.ivals = []
        self._scopes = []

    def push_scope(self, pfx):
        self._scopes.append((ExitStack(), pfx))

    def pop_scope(self):
        st_, _ = self._scopes.pop()
        st_.close()

    def _alloc(self, factory, name):
        if self._scopes:
            st_, pfx = self._scopes[-1]
            return st_.enter_context(factory(pfx + name))
        return self.ctx.enter_context(factory(name))

    def sbuf(self, name, shape, dt):
        return self._alloc(lambda n: self.nc.sbuf_tensor(n, shape, dt), name)

    def psum(self, name, shape, dt=F32):
        return self._alloc(lambda n: self.nc.psum_tensor(n, shape, dt), name)

    def sem(self, name):
        return Sem(self.ctx.enter_context(self.nc.semaphore(name)))

    def make_sems(self):
        self.dse = self.sem("dse")
        self.gse2 = [self.sem("gse0"), self.sem("gse1")]
        self.gsev = self.sem("gsev")
        self.sv = self.sem("sv")
        self.ss = self.sem("ss")
        self.st = self.sem("st")
        self.sg = self.sem("sg")
        self.cc = self.sem("cc")
        self.dso = self.sem("dso")

    def all_sems(self):
        return [self.dse, self.gse2[0], self.gse2[1], self.gsev,
                self.sv, self.ss, self.st, self.sg, self.cc, self.dso]

    def barrier(self, skip_cc=False):
        sems = self.all_sems()
        if skip_cc:
            sems = [sm for sm in sems if sm is not self.cc]
        vals = [(sm.h, sm.n) for sm in sems]
        with self.nc.Block() as blk:
            for engname in ("sync", "gpsimd", "tensor", "vector", "scalar"):
                def _mk(eng, vals=vals):
                    for h, n in vals:
                        if n:
                            eng.wait_ge(h, n)
                getattr(blk, engname)(_mk)

    def alloc_consts(self, nf=24, ni=8):
        self.csb_f = self.sbuf("consts_f", [128, nf], F32)
        self.csb_i = self.sbuf("consts_i", [128, ni], I32)
        self.cin_f = self.nc.dram_tensor("consts_fin", [nf], F32, kind="ExternalInput")
        self.cin_i = self.nc.dram_tensor("consts_iin", [ni], I32, kind="ExternalInput")

    def cf(self, v):
        v = float(v)
        if v not in self.fvals:
            i = len(self.fvals)
            self.fvals.append(v)
            self.nc.const_aps.register(F32, v, self.csb_f[:, i : i + 1])
        return self.csb_f[:, self.fvals.index(v) : self.fvals.index(v) + 1]

    def ci(self, v):
        v = int(v)
        if v not in self.ivals:
            i = len(self.ivals)
            self.ivals.append(v)
            self.nc.const_aps.register(I32, v, self.csb_i[:, i : i + 1])
        return self.csb_i[:, self.ivals.index(v) : self.ivals.index(v) + 1]

    def load_consts(self, sync, dse):
        nf, ni = self.csb_f.shape[1], self.csb_i.shape[1]
        dse.inc(sync.dma_start(out=self.csb_f[:, :], in_=bass.AP(self.cin_f, 0, [[0, 128], [1, nf]])), 16)
        dse.inc(sync.dma_start(out=self.csb_i[:, :], in_=bass.AP(self.cin_i, 0, [[0, 128], [1, ni]])), 16)

    def const_arrays(self):
        nf, ni = self.csb_f.shape[1], self.csb_i.shape[1]
        cf = np.zeros(nf, dtype=np.float32)
        cf[: len(self.fvals)] = self.fvals
        ci_ = np.zeros(ni, dtype=np.int32)
        ci_[: len(self.ivals)] = self.ivals
        return {"consts_fin": cf, "consts_iin": ci_}

    def close(self):
        self.ctx.close()


def view(t, off, dims):
    X = t.shape[1]
    return bass.AP(t, off, [[X, 128]] + dims)


def wstrip_in(w, col0, ncols, nchunk=32):
    W = w.shape[1]
    return bass.AP(w, col0, [[W, 128], [128 * W, nchunk], [1, ncols]])


def _prow(tt):
    return ((tt % 8) // 2) * 512 + (tt // 8) * 256 + (tt % 2) * 128


def emit_l0attn(p, x_in, wq, wk, wv, wo, cos_in, sin_in, tril_in, yp_out,
                xh_dram, q_dram, ao_dram):
    nc = p.nc
    p.push_scope("l0_")
    dse, dso, gse2, gsev = p.dse, p.dso, p.gse2, p.gsev
    sv, ss, st, sg = p.sv, p.ss, p.st, p.sg
    cosT = p.sbuf("cosT", [128, S], F32)
    sinT = p.sbuf("sinT", [128, S], F32)
    tril = p.sbuf("tril", [128, 4 * 512], BF16)
    ones_k = p.sbuf("ones_k", [128, 1], BF16)
    ones_1 = p.sbuf("ones_1", [1, 512], BF16)
    ks = p.sbuf("ks", [128, 4 * S], BF16)
    vs = p.sbuf("vs", [128, (S // 128) * 4 * HD], BF16)
    arA = p.sbuf("arA", [128, 16384], BF16)
    arB = p.sbuf("arB", [128, 32768], BF16)
    _xt0 = p.sbuf("xt0", [128, D], F32)
    xt = [_xt0, _xt0]
    sq = p.sbuf("sq", [128, D], BF16)
    _xh0 = p.sbuf("xhat0", [128, D], BF16)
    xhat = [_xh0, _xh0]
    stat = p.sbuf("stat", [128, 4], F32)
    qf = p.sbuf("qf", [128, 512], F32)
    qt0 = p.sbuf("qt0", [128, 512], F32)
    qrt = [p.sbuf(f"qrt{i}", [128, 512], BF16) for i in range(2)]
    qtile = p.sbuf("qtile", [128, 4 * 512], BF16)
    probs = [p.sbuf(f"probs{i}", [128, 512], BF16) for i in range(2)]
    srec = p.sbuf("srec", [1, 512], F32)
    srecb = [p.sbuf(f"srecb{i}", [1, 512], BF16) for i in range(2)]
    bc_sb = p.sbuf("bc_sb", [128, 512], F32)
    _ypt0 = p.sbuf("ypt0", [128, 512], F32)
    ypt = [_ypt0, _ypt0]

    ps_a = p.psum("ps_a", [128, 512], F32)
    ps_b = p.psum("ps_b", [128, 512], F32)
    ps_sc = [p.psum(f"ps_sc{i}", [128, 512], F32) for i in range(2)]
    ps_pv = p.psum("ps_pv", [128, 512], F32)
    ps_sum = p.psum("ps_sum", [1, 512], F32)
    ps_sum2 = p.psum("ps_sum2", [1, 512], F32)
    ps_bc = p.psum("ps_bc", [128, 512], F32)

    with nc.Block() as blk:
        @blk.sync
        def _(sync):
            dse.inc(sync.dma_start(out=cosT[:, :], in_=cos_in[:, :]), 16)
            dse.inc(sync.dma_start(out=sinT[:, :], in_=sin_in[:, :]), 16)
            dse.inc(sync.dma_start(
                out=tril[:, :],
                in_=bass.AP(tril_in, 0, [[512, 128], [128 * 512, 4], [1, 512]])), 16)
        boot_d = dse.n

        @blk.gpsimd
        def _(gpsimd):
            G = SerEng(gpsimd, sg)
            G.memset(ones_k[:, :], 1.0)
            G.memset(ones_1[:, :], 1.0)
        boot_g = sg.n

        c_eps = p.cf(EPS)
        c_im = p.cf(1.0 / D)
        c_shift = p.cf(-SM_SHIFT)
        c_scale = p.cf(SCALE)

        A_ld, A_sq, A_var, A_sd, A_xh, A_st = [], [], [], [], [], []
        for i in range(16):
            bi = i % 2

            @blk.sync
            def _(sync, i=i, bi=bi):
                if i == 0:
                    sync.wait_ge(dse.h, dse.n)
                    sync.wait_ge(dso.h, dso.n)
                if i >= 1:
                    sync.wait_ge(sv.h, A_xh[i - 1])
                dse.inc(sync.dma_start(out=xt[bi][:, :], in_=x_in[i * 128 : (i + 1) * 128, :]), 16)
            A_ld.append(dse.n)

            @blk.scalar
            def _(scalar, i=i, bi=bi):
                SS = SerEng(scalar, ss)
                scalar.wait_ge(dse.h, dse.n)
                scalar.wait_ge(dso.h, dso.n)
                if i >= 1:
                    scalar.wait_ge(sv.h, A_var[i - 1])
                SS.activation(out=sq[:, :], in_=xt[bi][:, :], func=AF.Identity,
                              accum_out=stat[:, 0:1])
                SS.activation(out=sq[:, :], in_=xt[bi][:, :], func=AF.Square,
                              accum_out=stat[:, 1:2])
            A_sq.append(ss.n)

            @blk.vector
            def _(vector, i=i, bi=bi):
                V = SerEng(vector, sv)
                vector.wait_ge(ss.h, A_sq[i])
                V.tensor_scalar(stat[:, 0:1], stat[:, 0:1], c_im, None, ALU.mult)
                V.tensor_scalar(stat[:, 1:2], stat[:, 1:2], c_im, None, ALU.mult)
                V.tensor_tensor(stat[:, 2:3], stat[:, 0:1], stat[:, 0:1], ALU.mult)
                V.tensor_tensor(stat[:, 2:3], stat[:, 1:2], stat[:, 2:3], ALU.subtract)
            A_var.append(sv.n)

            @blk.scalar
            def _(scalar, i=i):
                SS = SerEng(scalar, ss)
                scalar.wait_ge(sv.h, A_var[i])
                SS.activation(out=stat[:, 3:4], in_=stat[:, 2:3], func=AF.Sqrt, bias=c_eps)
            A_sd.append(ss.n)

            @blk.vector
            def _(vector, i=i, bi=bi):
                V = SerEng(vector, sv)
                vector.wait_ge(ss.h, A_sd[i])
                V.reciprocal(stat[:, 2:3], stat[:, 3:4])
                if i >= 2:
                    vector.wait_ge(dse.h, dse.n)
                    vector.wait_ge(dso.h, dso.n)
                V.tensor_scalar(
                    xhat[bi][:, :], xt[bi][:, :], stat[:, 0:1], stat[:, 2:3], ALU.subtract, ALU.mult)
            A_xh.append(sv.n)

            @blk.sync
            def _(sync, i=i, bi=bi):
                sync.wait_ge(sv.h, A_xh[i])
                dso.inc(sync.dma_start(out=xh_dram[i * 128 : (i + 1) * 128, :], in_=xhat[bi][:, :]), 16)
            A_st.append(dso.n)
        stageA_st = dso.n

        pe_tg = []
        q_ready = []
        kv_ready = []
        ev_marks = {}
        for tg in range(4):
            tg0 = tg * 512

            @blk.sync
            def _(sync, tg=tg):
                sync.wait_ge(dso.h, A_st[4 * tg + 3])
                if tg > 0:
                    sync.wait_ge(st.h, pe_tg[tg - 1])
                for c in range(32):
                    dse.inc(sync.dma_start(
                        out=view(arA, c * 512, [[1, 512]]),
                        in_=xh_dram[tg0 : tg0 + 512, c * 128 : (c + 1) * 128],
                        transpose=True), 16)
            xhT_ld = dse.n

            tg_ev0 = sv.n
            fgroups = [("k", 0)] + [("q", g) for g in range(4)]
            ev_idx = 0
            for gi, (kind, g) in enumerate(fgroups):
                sb = gi % 2
                win = wk if kind == "k" else wq

                @blk.gpsimd
                def _(gpsimd, tg=tg, g=g, sb=sb, gi=gi, win=win):
                    gpsimd.wait_ge(sg.h, boot_g)
                    key = ("peg", tg, gi - 2)
                    if key in ev_marks:
                        gpsimd.wait_ge(st.h, ev_marks[key])
                    elif tg > 0:
                        gpsimd.wait_ge(st.h, pe_tg[tg - 1])
                    gse2[sb].inc(gpsimd.dma_start(
                        out=view(arB, sb * 16384, [[512, 32], [1, 512]]),
                        in_=wstrip_in(win, g * 512, 512)), 16)
                ev_marks[("ld", tg, gi)] = gse2[sb].n

                for sub in range(4):
                    f = g * 4 + sub
                    idx = ev_idx
                    ev_idx += 1
                    ps = ps_a if idx % 2 == 0 else ps_b

                    @blk.tensor
                    def _(tensor, tg=tg, gi=gi, idx=idx, ps=ps, sb=sb, sub=sub):
                        tensor.wait_ge(gse2[sb].h, ev_marks[("ld", tg, gi)])
                        tensor.wait_ge(dse.h, dse.n)
                        tensor.wait_ge(dso.h, dso.n)
                        key = ("ev", tg, idx - 2)
                        tensor.wait_ge(sv.h, ev_marks.get(key, tg_ev0))
                        for c in range(32):
                            mm = tensor.matmul(
                                ps[:, :],
                                view(arB, sb * 16384 + c * 512 + sub * 128, [[1, 128]]),
                                view(arA, c * 512, [[1, 512]]),
                                start=(c == 0), stop=(c == 31))
                        st.inc(mm)
                    ev_marks[("peg", tg, gi)] = st.n
                    ev_marks[("pemm", tg, idx)] = st.n

                    @blk.vector
                    def _(vector, tg=tg, f=f, kind=kind, idx=idx, ps=ps, tg0=tg0):
                        V = SerEng(vector, sv)
                        vector.wait_ge(st.h, ev_marks[("pemm", tg, idx)])
                        V.tensor_copy(qf[:, :], ps[:, :])
                        V.stream_shuffle(qt0[:, :], qf[:, :], PAIR_SWAP)
                        V.tensor_tensor(qt0[:, :], qt0[:, :], sinT[:, tg0 : tg0 + 512], ALU.mult)
                        V.tensor_tensor(qf[:, :], qf[:, :], cosT[:, tg0 : tg0 + 512], ALU.mult)
                        if kind == "k":
                            dst = ks[:, f * S + tg0 : f * S + tg0 + 512]
                            V.tensor_tensor(dst, qf[:, :], qt0[:, :], ALU.add)
                        else:
                            par = idx % 2
                            vector.wait_ge(dse.h, dse.n)
                            vector.wait_ge(dso.h, dso.n)
                            V.tensor_tensor(qrt[par][:, :], qf[:, :], qt0[:, :], ALU.add)
                    ev_marks[("ev", tg, idx)] = sv.n

                    if kind == "q":
                        @blk.sync
                        def _(sync, tg=tg, f=f, idx=idx, tg0=tg0):
                            sync.wait_ge(sv.h, ev_marks[("ev", tg, idx)])
                            dso.inc(sync.dma_start(
                                out=q_dram[f * 128 : (f + 1) * 128, tg0 : tg0 + 512],
                                in_=qrt[idx % 2][:, :]), 16)

            @blk.gpsimd
            def _(gpsimd, tg=tg):
                gpsimd.wait_ge(st.h, st.n)
                gsev.inc(gpsimd.dma_start(
                    out=view(arB, 0, [[512, 32], [1, 512]]),
                    in_=wstrip_in(wv, 0, 512)), 16)
            vld = gsev.n

            for sub in range(4):
                @blk.tensor
                def _(tensor, tg=tg, sub=sub, vld=vld):
                    tensor.wait_ge(gsev.h, vld)
                    tensor.wait_ge(sv.h, sv.n)
                    for c in range(32):
                        mm = tensor.matmul(
                            ps_a[:, :],
                            view(arA, c * 512 + sub * 128, [[1, 128]]),
                            view(arB, c * 512, [[1, 512]]),
                            start=(c == 0), stop=(c == 31))
                    st.inc(mm)
                vpe = st.n

                @blk.vector
                def _(vector, tg=tg, sub=sub, vpe=vpe):
                    V = SerEng(vector, sv)
                    vector.wait_ge(st.h, vpe)
                    tt = tg * 4 + sub
                    V.tensor_copy(vs[:, tt * 512 : (tt + 1) * 512], ps_a[:, :])
            pe_tg.append(st.n)
            q_ready.append(dso.n)
            kv_ready.append(sv.n)
        stageB_v = sv.n

        expc = {0: 0, 1: 0}
        pvc = {0: 0, 1: 0}
        gp = 0
        last_ev_par = [sv.n, sv.n]
        pv_b = [ps_pv, ps_a]
        bc_b = [ps_bc, ps_b]
        for qb in range(4):
            nkt = (qb + 1) * 4
            for kvh in range(4):
                @blk.sync
                def _(sync, qb=qb, kvh=kvh):
                    sync.wait_ge(sv.h, kv_ready[qb])
                    sync.wait_ge(dso.h, q_ready[qb])
                    sync.wait_ge(st.h, st.n)
                    for j in range(4):
                        fr = (4 * kvh + j) * 128
                        dse.inc(sync.dma_start(
                            out=qtile[:, j * 512 : (j + 1) * 512],
                            in_=q_dram[fr : fr + 128, qb * 512 : (qb + 1) * 512]), 16)
                qld = dse.n

                for j in range(4):
                    par = gp % 2
                    pvx = pv_b[par]
                    bcx = bc_b[par]
                    sumx = ps_sum[0:1, :] if par == 0 else ps_sum2[0:1, :]
                    for kt in range(nkt):
                        pp = kt % 2

                        @blk.tensor
                        def _(tensor, kvh=kvh, j=j, kt=kt, pp=pp, qld=qld):
                            tensor.wait_ge(dse.h, qld)
                            tensor.wait_ge(ss.h, expc[pp])
                            st.inc(tensor.matmul(
                                ps_sc[pp][:, :],
                                ks[:, kvh * S + kt * 128 : kvh * S + kt * 128 + 128],
                                qtile[:, j * 512 : (j + 1) * 512],
                                start=True, stop=True))
                        scd = st.n

                        @blk.scalar
                        def _(scalar, kt=kt, pp=pp, scd=scd):
                            SS = SerEng(scalar, ss)
                            scalar.wait_ge(st.h, scd)
                            scalar.wait_ge(st.h, pvc[pp])
                            SS.activation(
                                out=probs[pp][:, :], in_=ps_sc[pp][:, :],
                                func=AF.Exp, bias=c_shift, scale=c_scale)
                        expc[pp] = ss.n
                        expd = ss.n

                        maskd = None
                        if kt >= qb * 4:
                            @blk.vector
                            def _(vector, kt=kt, qb=qb, pp=pp, expd=expd):
                                V = SerEng(vector, sv)
                                vector.wait_ge(ss.h, expd)
                                off = kt - qb * 4
                                V.tensor_tensor(
                                    probs[pp][:, :], probs[pp][:, :],
                                    tril[:, off * 512 : (off + 1) * 512], ALU.mult)
                            maskd = sv.n

                        @blk.tensor
                        def _(tensor, kvh=kvh, kt=kt, pp=pp, expd=expd, maskd=maskd,
                              nkt=nkt, pvx=pvx, sumx=sumx, par=par):
                            if maskd is not None:
                                tensor.wait_ge(sv.h, maskd)
                            else:
                                tensor.wait_ge(ss.h, expd)
                            if kt == 0:
                                tensor.wait_ge(sv.h, last_ev_par[par])
                            tensor.matmul(
                                pvx[:, :],
                                vs[:, kt * 512 + kvh * 128 : kt * 512 + kvh * 128 + 128],
                                probs[pp][:, :],
                                start=(kt == 0), stop=(kt == nkt - 1))
                            st.inc(tensor.matmul(
                                sumx, ones_k[:, :], probs[pp][:, :],
                                start=(kt == 0), stop=(kt == nkt - 1)))
                        pvc[pp] = st.n
                    pvdone = st.n

                    @blk.vector
                    def _(vector, pvdone=pvdone, sumx=sumx, par=par):
                        V = SerEng(vector, sv)
                        vector.wait_ge(st.h, pvdone)
                        V.reciprocal(srec[:, :], sumx)
                        V.tensor_copy(srecb[par][:, :], srec[:, :])
                    recd = sv.n

                    @blk.tensor
                    def _(tensor, recd=recd, bcx=bcx, par=par):
                        tensor.wait_ge(sv.h, recd)
                        tensor.wait_ge(sv.h, last_ev_par[par])
                        st.inc(tensor.matmul(
                            bcx[:, :], ones_1[0:1, 0:128], srecb[par][0:1, :],
                            start=True, stop=True))
                    bcd = st.n

                    @blk.vector
                    def _(vector, kvh=kvh, j=j, bcd=bcd, pvx=pvx, bcx=bcx):
                        V = SerEng(vector, sv)
                        vector.wait_ge(st.h, bcd)
                        V.tensor_copy(bc_sb[:, :], bcx[:, :])
                        c = 4 * kvh + j
                        vector.wait_ge(dso.h, dso.n)
                        V.tensor_tensor(
                            view(arA, c * 512, [[1, 512]]), pvx[:, :], bc_sb[:, :], ALU.mult)
                    last_ev_par[par] = sv.n
                    aoev = sv.n

                    @blk.sync
                    def _(sync, qb=qb, kvh=kvh, j=j, aoev=aoev):
                        sync.wait_ge(sv.h, aoev)
                        c = 4 * kvh + j
                        dso.inc(sync.dma_start(
                            out=ao_dram[c * 128 : (c + 1) * 128, qb * 512 : (qb + 1) * 512],
                            in_=view(arA, c * 512, [[1, 512]])), 16)
                    gp += 1
        stageC_ao = dso.n

        _aot0 = p.sbuf("aot0", [128, 16 * 128], BF16)
        aot = [_aot0, _aot0]
        aold = {}
        ypst = {}
        st_d0 = st.n
        for fg in range(8):
            sb = fg % 2

            @blk.gpsimd
            def _(gpsimd, fg=fg, sb=sb, st_d0=st_d0):
                if fg < 2:
                    gpsimd.wait_ge(st.h, st_d0)
                else:
                    gpsimd.wait_ge(st.h, aold[("pe", fg - 2, 15)])
                gse2[sb].inc(gpsimd.dma_start(
                    out=view(arB, sb * 8192, [[512, 16], [1, 512]]),
                    in_=bass.AP(wo, fg * 512, [[D, 128], [128 * D, 16], [1, 512]])), 16)
            wold = gse2[sb].n

            for tt in range(16):
                par = tt % 2

                @blk.sync
                def _(sync, fg=fg, tt=tt, par=par):
                    if fg == 0 and tt == 0:
                        sync.wait_ge(dse.h, dse.n)
                        sync.wait_ge(dso.h, dso.n)
                    key = ("pe", fg, tt - 1) if tt >= 1 else ("pe", fg - 1, tt + 15)
                    if key in aold:
                        sync.wait_ge(st.h, aold[key])
                    dse.inc(sync.dma_start(
                        out=aot[par][:, :],
                        in_=bass.AP(ao_dram, tt * 128, [[S, 128], [128 * S, 16], [1, 128]])), 16)
                aold[("ld", fg, tt)] = dse.n

                @blk.tensor
                def _(tensor, fg=fg, tt=tt, par=par, wold=wold, sb=sb):
                    tensor.wait_ge(gse2[sb].h, wold)
                    tensor.wait_ge(dse.h, dse.n)
                    tensor.wait_ge(dso.h, dso.n)
                    tensor.wait_ge(sv.h, sv.n)
                    for c in range(16):
                        mm = tensor.matmul(
                            ps_a[:, :],
                            view(aot[par], c * 128, [[1, 128]]),
                            view(arB, sb * 8192 + c * 512, [[1, 512]]),
                            start=(c == 0), stop=(c == 15))
                    st.inc(mm)
                aold[("pe", fg, tt)] = st.n

                @blk.vector
                def _(vector, fg=fg, tt=tt, par=par):
                    V = SerEng(vector, sv)
                    vector.wait_ge(st.h, aold[("pe", fg, tt)])
                    key = ("st", fg, tt - 2) if tt >= 2 else ("st", fg - 1, tt + 14)
                    vector.wait_ge(dse.h, dse.n)
                    vector.wait_ge(dso.h, dso.n)
                    V.tensor_copy(ypt[par][:, :], ps_a[:, :])
                ypev = sv.n

                @blk.sync
                def _(sync, fg=fg, tt=tt, par=par, ypev=ypev):
                    sync.wait_ge(sv.h, ypev)
                    dso.inc(sync.dma_start(
                        out=yp_out[_prow(tt) : _prow(tt) + 128, fg * 512 : (fg + 1) * 512],
                        in_=ypt[par][:, :]), 16)
                ypst[("st", fg, tt)] = dse.n

        @blk.sync
        def _(sync):
            sync.wait_ge(dse.h, dse.n)
            sync.wait_ge(dso.h, dso.n)

    p.pop_scope()


def l0attn_inmaps(x_full, w_qkv, w_o, consts):
    cos, sin = _rope_l0()
    tril = _tril_masks()
    maps = []
    for c in range(8):
        b, hh = c // 2, c % 2
        m = dict(consts)
        m["xb"] = np.ascontiguousarray(x_full[b])
        m["wq_s"] = np.ascontiguousarray(w_qkv[:, hh * 2048 : (hh + 1) * 2048])
        m["wk_s"] = np.ascontiguousarray(w_qkv[:, 4096 + hh * 512 : 4096 + (hh + 1) * 512])
        m["wv_s"] = np.ascontiguousarray(w_qkv[:, 5120 + hh * 512 : 5120 + (hh + 1) * 512])
        m["wo_s"] = np.ascontiguousarray(w_o[hh * 2048 : (hh + 1) * 2048, :])
        m["cos_t"] = cos
        m["sin_t"] = sin
        m["tril_t"] = tril
        maps.append(m)
    return maps


def emit_dequant(p, qw, lut4, lut8, w1q, sc_in, zr_in, mk_in, s2_in,
                 shf_in, kramp_in, wl_out, wh_out, w1_out):
    nc = p.nc
    p.push_scope("dq_")
    dse, dso, sv = p.dse, p.dso, p.sv
    shf_sb = p.sbuf("shf_sb", [128, 32], I32)
    kramp_sb = p.sbuf("kramp_sb", [128, 256], F32)
    qws = p.sbuf("qws", [128, 8 * 128], I32)
    c8 = p.sbuf("c8", [128, D], I32)
    tbit = p.sbuf("tbit", [128, D], I32)
    c8b = p.sbuf("c8b", [128, D], BF16)
    c4b = p.sbuf("c4b", [128, D], BF16)
    eq = p.sbuf("eq", [128, D], BF16)
    accl = p.sbuf("accl", [128, D], BF16)
    acch = p.sbuf("acch", [128, D], BF16)
    lut4_sb = p.sbuf("lut4_sb", [128, 16], F32)
    lut8_sb = p.sbuf("lut8_sb", [128, 256], F32)
    w1qs = p.sbuf("w1qs", [128, D], I32)
    mks = p.sbuf("mks", [128, D], I32)
    qbf = p.sbuf("qbf", [128, D], BF16)
    mbf = p.sbuf("mbf", [128, D], BF16)
    w1t = p.sbuf("w1t", [128, D], BF16)
    sc_sb = p.sbuf("sc_sb", [128, 32], F32)
    zr_sb = p.sbuf("zr_sb", [128, 32], F32)
    s2_sb = p.sbuf("s2_sb", [128, 1], F32)
    A_sb = p.sbuf("A_sb", [128, 32], F32)
    Bz_sb = p.sbuf("Bz_sb", [128, 32], F32)

    with nc.Block() as blk:
        @blk.sync
        def _(sync):
            dse.inc(sync.dma_start(out=shf_sb[:, :], in_=shf_in[:, :]), 16)
            dse.inc(sync.dma_start(out=kramp_sb[:, :], in_=kramp_in[:, :]), 16)
        c1i = p.ci(1)

        for ot in range(4):
            o0 = ot * 128

            @blk.sync
            def _(sync, ot=ot, o0=o0):
                sync.wait_ge(sv.h, sv.n)
                dse.inc(sync.dma_start(
                    out=qws[:, :],
                    in_=bass.AP(qw, o0 * 128, [[128, 128], [512 * 128, 8], [1, 128]])), 16)
                dse.inc(sync.dma_start(out=lut4_sb[:, :], in_=lut4[o0 : o0 + 128, :]), 16)
                dse.inc(sync.dma_start(out=lut8_sb[:, :], in_=lut8[o0 : o0 + 128, :]), 16)
                dse.inc(sync.dma_start(out=w1qs[:, :], in_=w1q[o0 : o0 + 128, :]), 16)
                dse.inc(sync.dma_start(out=mks[:, :], in_=mk_in[o0 : o0 + 128, :]), 16)
                dse.inc(sync.dma_start(out=sc_sb[:, :], in_=sc_in[o0 : o0 + 128, :]), 16)
                dse.inc(sync.dma_start(out=zr_sb[:, :], in_=zr_in[o0 : o0 + 128, :]), 16)
                dse.inc(sync.dma_start(out=s2_sb[:, :], in_=s2_in[o0 : o0 + 128, :]), 16)

            @blk.vector
            def _(vector, ot=ot):
                V = SerEng(vector, sv)
                vector.wait_ge(dse.h, dse.n)
                vector.wait_ge(dso.h, dso.n)
                for pl in range(8):
                    a_w = view(qws, pl * 128, [[1, 128], [0, 32]])
                    a_s = bass.AP(shf_sb, 0, [[32, 128], [0, 128], [1, 32]])
                    V.tensor_tensor(tbit[:, :], a_w, a_s, ALU.logical_shift_right)
                    V.tensor_scalar(tbit[:, :], tbit[:, :], c1i, None, ALU.bitwise_and)
                    if pl == 0:
                        V.tensor_copy(c8[:, :], tbit[:, :])
                    else:
                        V.tensor_scalar(c8[:, :], tbit[:, :], p.ci(pl), c8[:, :], ALU.logical_shift_left, ALU.bitwise_or) if False else \
                            V.scalar_tensor_tensor(c8[:, :], tbit[:, :], p.ci(pl), c8[:, :], ALU.logical_shift_left, ALU.bitwise_or)
                V.tensor_copy(c8b[:, :], c8[:, :])
                V.tensor_scalar(c8[:, :], c8[:, :], p.ci(15), None, ALU.bitwise_and)
                V.tensor_copy(c4b[:, :], c8[:, :])
                for k in range(16):
                    if k == 0:
                        V.tensor_scalar(accl[:, :], c4b[:, :], kramp_sb[:, 0:1], lut4_sb[:, 0:1], ALU.is_equal, ALU.mult)
                    else:
                        V.tensor_scalar(eq[:, :], c4b[:, :], kramp_sb[:, k : k + 1], lut4_sb[:, k : k + 1], ALU.is_equal, ALU.mult)
                        V.tensor_tensor(accl[:, :], accl[:, :], eq[:, :], ALU.add)
                for k in range(256):
                    if k == 0:
                        V.tensor_scalar(acch[:, :], c8b[:, :], kramp_sb[:, 0:1], lut8_sb[:, 0:1], ALU.is_equal, ALU.mult)
                    else:
                        V.tensor_scalar(eq[:, :], c8b[:, :], kramp_sb[:, k : k + 1], lut8_sb[:, k : k + 1], ALU.is_equal, ALU.mult)
                        V.tensor_tensor(acch[:, :], acch[:, :], eq[:, :], ALU.add)
                V.tensor_scalar(A_sb[:, :], sc_sb[:, :], s2_sb[:, 0:1], None, ALU.mult)
                V.tensor_tensor(Bz_sb[:, :], zr_sb[:, :], A_sb[:, :], ALU.mult)
                V.tensor_scalar(Bz_sb[:, :], Bz_sb[:, :], -1.0, None, ALU.mult)
                V.tensor_copy(qbf[:, :], w1qs[:, :])
                V.tensor_copy(mbf[:, :], mks[:, :])
                for g in range(32):
                    V.tensor_scalar(
                        w1t[:, g * 128 : (g + 1) * 128], qbf[:, g * 128 : (g + 1) * 128],
                        A_sb[:, g : g + 1], Bz_sb[:, g : g + 1], ALU.mult, ALU.add)
                V.tensor_tensor(w1t[:, :], w1t[:, :], mbf[:, :], ALU.mult)

            @blk.sync
            def _(sync, ot=ot, o0=o0):
                sync.wait_ge(sv.h, sv.n)
                dso.inc(sync.dma_start(out=wl_out[o0 : o0 + 128, :], in_=accl[:, :]), 16)
                dso.inc(sync.dma_start(out=wh_out[o0 : o0 + 128, :], in_=acch[:, :]), 16)
                dso.inc(sync.dma_start(out=w1_out[o0 : o0 + 128, :], in_=w1t[:, :]), 16)

        @blk.sync
        def _(sync):
            sync.wait_ge(dse.h, dse.n)
            sync.wait_ge(dso.h, dso.n)

    p.pop_scope()


def dequant_inmaps(z, consts):
    shf = np.broadcast_to(np.arange(32, dtype=np.int32), (128, 32)).copy()
    kramp = np.broadcast_to(np.arange(256, dtype=np.float32), (128, 256)).copy()
    maps = []
    for c in range(8):
        r = slice(c * 512, (c + 1) * 512)
        m = dict(consts)
        m["qw_s"] = np.ascontiguousarray(z["L0_ffn_qweight"][:, r, :])
        m["lut4_s"] = np.ascontiguousarray(z["L0_ffn_lut4"][r])
        m["lut8_s"] = np.ascontiguousarray(z["L0_ffn_lut8"][r])
        m["w1q_s"] = np.ascontiguousarray(z["L1_ffn_W_q"][r])
        m["sc_s"] = np.ascontiguousarray(z["L1_ffn_scales"][r, :, 0])
        m["zr_s"] = np.ascontiguousarray(z["L1_ffn_zeros"][r, :, 0])
        m["mk_s"] = np.ascontiguousarray(z["L1_ffn_mask"][r])
        m["s2_s"] = np.ascontiguousarray(z["L1_ffn_scale2"][r])
        m["shf"] = shf
        m["kramp"] = kramp
        maps.append(m)
    return maps


def emit_ffn0(p, x_in, ps_in, wl_in, wh_in, bias_in, x2_out,
              x1_dram, xh2_dram, ydl_dram, ydh_dram, ccbase):
    nc = p.nc
    p.push_scope("f0_")
    dse, dso, sv, ss, st, cc, gsev = p.dse, p.dso, p.sv, p.ss, p.st, p.cc, p.gsev
    ln2T = p.sbuf("ln2T", [128, 32 * 1024], BF16)
    arW = p.sbuf("arW", [128, 2 * 32 * 512], BF16)
    ta = p.sbuf("ta", [128, D], F32)
    tb = p.sbuf("tb", [128, D], F32)
    tc = p.sbuf("tc", [128, D], F32)
    sq = p.sbuf("sq", [128, D], BF16)
    stat = p.sbuf("stat", [128, 8], F32)
    masks = p.sbuf("masks", [128, 8], F32)
    biasbc = p.sbuf("biasbc", [128, 1024], F32)
    ypt = [p.sbuf(f"ypt{i}", [128, 512], F32) for i in range(2)]
    ps_a = p.psum("ps_a", [128, 512], F32)
    ps_b = p.psum("ps_b", [128, 512], F32)

    with nc.Block() as blk:
        c_eps = p.cf(EPS)
        c_im = p.cf(1.0 / D)
        c_64 = p.cf(64.0)

        for i in range(8):
            r0 = i * 128

            @blk.sync
            def _(sync, i=i, r0=r0):
                sync.wait_ge(cc.h, ccbase + i // 2 + 1)
                sync.wait_ge(sv.h, sv.n)
                sync.wait_ge(dse.h, dse.n)
                sync.wait_ge(dso.h, dso.n)
                dse.inc(sync.dma_start(out=ta[:, :], in_=x_in[r0 : r0 + 128, :]), 16)
                dse.inc(sync.dma_start(out=tb[:, :], in_=ps_in[r0 : r0 + 128, :]), 16)

            @blk.scalar
            def _(scalar, i=i):
                SS = SerEng(scalar, ss)
                scalar.wait_ge(sv.h, sv.n)
                scalar.wait_ge(dse.h, dse.n)
                scalar.wait_ge(dso.h, dso.n)
                SS.activation(out=sq[:, :], in_=tb[:, :], func=AF.Identity,
                              accum_out=stat[:, 0:1])
                SS.activation(out=sq[:, :], in_=tb[:, :], func=AF.Square,
                              accum_out=stat[:, 1:2])

            @blk.vector
            def _(vector, i=i):
                V = SerEng(vector, sv)
                vector.wait_ge(ss.h, ss.n)
                V.tensor_scalar(stat[:, 0:1], stat[:, 0:1], c_im, None, ALU.mult)
                V.tensor_scalar(stat[:, 1:2], stat[:, 1:2], c_im, None, ALU.mult)
                V.tensor_tensor(stat[:, 2:3], stat[:, 0:1], stat[:, 0:1], ALU.mult)
                V.tensor_tensor(stat[:, 2:3], stat[:, 1:2], stat[:, 2:3], ALU.subtract)

            @blk.scalar
            def _(scalar, i=i):
                SS = SerEng(scalar, ss)
                scalar.wait_ge(sv.h, sv.n)
                SS.activation(out=stat[:, 3:4], in_=stat[:, 2:3], func=AF.Sqrt, bias=c_eps)

            @blk.vector
            def _(vector, i=i, r0=r0):
                V = SerEng(vector, sv)
                vector.wait_ge(ss.h, ss.n)
                V.reciprocal(stat[:, 3:4], stat[:, 3:4])
                V.tensor_scalar(tc[:, :], tb[:, :], stat[:, 0:1], stat[:, 3:4], ALU.subtract, ALU.mult)
                V.tensor_tensor(ta[:, :], ta[:, :], tc[:, :], ALU.add)

            @blk.sync
            def _(sync, i=i, r0=r0):
                sync.wait_ge(sv.h, sv.n)
                dso.inc(sync.dma_start(out=x1_dram[r0 : r0 + 128, :], in_=ta[:, :]), 16)

            @blk.scalar
            def _(scalar, i=i):
                SS = SerEng(scalar, ss)
                scalar.wait_ge(sv.h, sv.n)
                scalar.wait_ge(dse.h, dse.n)
                scalar.wait_ge(dso.h, dso.n)
                SS.activation(out=sq[:, :], in_=ta[:, :], func=AF.Identity,
                              accum_out=stat[:, 0:1])
                SS.activation(out=sq[:, :], in_=ta[:, :], func=AF.Square,
                              accum_out=stat[:, 1:2])

            @blk.vector
            def _(vector, i=i):
                V = SerEng(vector, sv)
                vector.wait_ge(ss.h, ss.n)
                V.tensor_scalar(stat[:, 0:1], stat[:, 0:1], c_im, None, ALU.mult)
                V.tensor_scalar(stat[:, 1:2], stat[:, 1:2], c_im, None, ALU.mult)
                V.tensor_tensor(stat[:, 2:3], stat[:, 0:1], stat[:, 0:1], ALU.mult)
                V.tensor_tensor(stat[:, 2:3], stat[:, 1:2], stat[:, 2:3], ALU.subtract)
                V.tensor_scalar(stat[:, 4:5], stat[:, 2:3], c_eps, None, ALU.add)
                V.reciprocal(stat[:, 4:5], stat[:, 4:5])
                V.tensor_tensor(stat[:, 4:5], stat[:, 2:3], stat[:, 4:5], ALU.mult)

            @blk.scalar
            def _(scalar, i=i):
                SS = SerEng(scalar, ss)
                scalar.wait_ge(sv.h, sv.n)
                SS.activation(out=stat[:, 5:6], in_=stat[:, 4:5], func=AF.Sqrt)
                SS.activation(out=stat[:, 3:4], in_=stat[:, 2:3], func=AF.Sqrt, bias=c_eps)

            @blk.vector
            def _(vector, i=i):
                V = SerEng(vector, sv)
                vector.wait_ge(ss.h, ss.n)
                V.tensor_scalar(masks[:, i : i + 1], stat[:, 5:6], c_64, c_64, ALU.mult, ALU.is_gt)
                V.reciprocal(stat[:, 3:4], stat[:, 3:4])
                V.tensor_scalar(sq[:, :], ta[:, :], stat[:, 0:1], stat[:, 3:4], ALU.subtract, ALU.mult)

            @blk.sync
            def _(sync, i=i, r0=r0):
                sync.wait_ge(sv.h, sv.n)
                dso.inc(sync.dma_start(out=xh2_dram[r0 : r0 + 128, :], in_=sq[:, :]), 16)

            if i in (3, 7):
                hb = 0 if i == 3 else 512

                @blk.sync
                def _(sync, hb=hb):
                    sync.wait_ge(dso.h, dso.n)
                    for c in range(32):
                        dse.inc(sync.dma_start(
                            out=view(ln2T, c * 1024 + hb, [[1, 512]]),
                            in_=xh2_dram[hb : hb + 512, c * 128 : (c + 1) * 128],
                            transpose=True), 16)

        wmk = {}
        bmk = {}
        for wi, (win, ydram) in enumerate(((wl_in, ydl_dram), (wh_in, ydh_dram))):
            for og in range(8):
                gidx = wi * 8 + og
                wh_ = gidx % 2

                @blk.sync
                def _(sync, og=og, win=win, wh_=wh_, gidx=gidx):
                    if gidx == 0:
                        sync.wait_ge(dso.h, dso.n)
                        sync.wait_ge(cc.h, 2)
                    key = ("pe", gidx - 2)
                    if key in wmk:
                        sync.wait_ge(st.h, wmk[key])
                        sync.wait_ge(sv.h, bmk[gidx - 2])
                    for c in range(32):
                        dse.inc(sync.dma_start(
                            out=view(arW, wh_ * 16384 + c * 512, [[1, 512]]),
                            in_=win[og * 512 : (og + 1) * 512, c * 128 : (c + 1) * 128],
                            transpose=True), 16)
                    dse.inc(sync.dma_start(
                        out=biasbc[:, wh_ * 512 : wh_ * 512 + 512],
                        in_=bass.AP(bias_in, og * 512, [[0, 128], [1, 512]])), 16)
                    sync.wait_ge(dse.h, dse.n)
                wld = dse.n

                for tsub in range(8):
                    pp = tsub % 2
                    ps = ps_a if pp == 0 else ps_b

                    @blk.tensor
                    def _(tensor, og=og, tsub=tsub, ps=ps, wh_=wh_, wld=wld):
                        tensor.wait_ge(dse.h, wld)
                        tensor.wait_ge(sv.h, sv.n)
                        for c in range(32):
                            mm = tensor.matmul(
                                ps[:, :],
                                view(ln2T, c * 1024 + tsub * 128, [[1, 128]]),
                                view(arW, wh_ * 16384 + c * 512, [[1, 512]]),
                                start=(c == 0), stop=(c == 31))
                        st.inc(mm)
                    ped = st.n

                    @blk.vector
                    def _(vector, tsub=tsub, ped=ped, pp=pp, ps=ps, wh_=wh_):
                        V = SerEng(vector, sv)
                        vector.wait_ge(st.h, ped)
                        vector.wait_ge(gsev.h, gsev.n)
                        V.tensor_tensor(ypt[pp][:, :], ps[:, :],
                                        biasbc[:, wh_ * 512 : wh_ * 512 + 512], ALU.add)
                    yev = sv.n

                    @blk.gpsimd
                    def _(gpsimd, og=og, tsub=tsub, yev=yev, pp=pp, ydram=ydram):
                        gpsimd.wait_ge(sv.h, yev)
                        gsev.inc(gpsimd.dma_start(
                            out=ydram[tsub * 128 : (tsub + 1) * 128, og * 512 : (og + 1) * 512],
                            in_=ypt[pp][:, :]), 16)
                wmk[("pe", gidx)] = st.n
                bmk[gidx] = sv.n

        for i in range(8):
            r0 = i * 128

            @blk.sync
            def _(sync, i=i, r0=r0):
                sync.wait_ge(dse.h, dse.n)
                sync.wait_ge(dso.h, dso.n)
                sync.wait_ge(gsev.h, gsev.n)
                sync.wait_ge(sv.h, sv.n)
                dse.inc(sync.dma_start(out=ta[:, :], in_=x1_dram[r0 : r0 + 128, :]), 16)
                dse.inc(sync.dma_start(out=tb[:, :], in_=ydh_dram[r0 : r0 + 128, :]), 16)
                dse.inc(sync.dma_start(out=tc[:, :], in_=ydl_dram[r0 : r0 + 128, :]), 16)

            @blk.vector
            def _(vector, i=i):
                V = SerEng(vector, sv)
                vector.wait_ge(dse.h, dse.n)
                vector.wait_ge(dso.h, dso.n)
                V.tensor_tensor(tb[:, :], tb[:, :], tc[:, :], ALU.subtract)
                V.scalar_tensor_tensor(tb[:, :], tb[:, :], masks[:, i : i + 1], tc[:, :], ALU.mult, ALU.add)

            @blk.scalar
            def _(scalar, i=i):
                SS = SerEng(scalar, ss)
                scalar.wait_ge(sv.h, sv.n)
                scalar.wait_ge(dse.h, dse.n)
                scalar.wait_ge(dso.h, dso.n)
                SS.activation(out=sq[:, :], in_=tb[:, :], func=AF.Identity,
                              accum_out=stat[:, 0:1])
                SS.activation(out=sq[:, :], in_=tb[:, :], func=AF.Square,
                              accum_out=stat[:, 1:2])

            @blk.vector
            def _(vector, i=i):
                V = SerEng(vector, sv)
                vector.wait_ge(ss.h, ss.n)
                V.tensor_scalar(stat[:, 0:1], stat[:, 0:1], c_im, None, ALU.mult)
                V.tensor_scalar(stat[:, 1:2], stat[:, 1:2], c_im, None, ALU.mult)
                V.tensor_tensor(stat[:, 2:3], stat[:, 0:1], stat[:, 0:1], ALU.mult)
                V.tensor_tensor(stat[:, 2:3], stat[:, 1:2], stat[:, 2:3], ALU.subtract)

            @blk.scalar
            def _(scalar, i=i):
                SS = SerEng(scalar, ss)
                scalar.wait_ge(sv.h, sv.n)
                SS.activation(out=stat[:, 3:4], in_=stat[:, 2:3], func=AF.Sqrt, bias=c_eps)

            @blk.vector
            def _(vector, i=i):
                V = SerEng(vector, sv)
                vector.wait_ge(ss.h, ss.n)
                V.reciprocal(stat[:, 3:4], stat[:, 3:4])
                V.tensor_scalar(tc[:, :], tb[:, :], stat[:, 0:1], stat[:, 3:4], ALU.subtract, ALU.mult)
                V.tensor_tensor(ta[:, :], ta[:, :], tc[:, :], ALU.add)

            @blk.sync
            def _(sync, i=i, r0=r0):
                sync.wait_ge(sv.h, sv.n)
                dso.inc(sync.dma_start(out=x2_out[r0 : r0 + 128, :], in_=ta[:, :]), 16)

        @blk.sync
        def _(sync):
            sync.wait_ge(dse.h, dse.n)
            sync.wait_ge(dso.h, dso.n)

    p.pop_scope()


def ffn0_inmaps(x_full, yp0, yp1, wl_full, wh_full, bias, consts):
    maps = []
    for c in range(8):
        b, half = c // 2, c % 2
        rows = slice(half * 1024, (half + 1) * 1024)
        m = dict(consts)
        m["x_r"] = np.ascontiguousarray(x_full[b][rows])
        m["p0_r"] = np.ascontiguousarray(yp0[b][rows])
        m["p1_r"] = np.ascontiguousarray(yp1[b][rows])
        m["wl_f"] = wl_full
        m["wh_f"] = wh_full
        m["bias"] = bias
        maps.append(m)
    return maps


def emit_l1attn(p, x_in, wq, wk, wv, wo, cos_in, sin_in, am_in, am1_in,
                rotm_in, yp_out, xh_dram, q_dram, ao_dram, ccbase):
    nc = p.nc
    p.push_scope("l1_")
    dse, dso, gse2, gsev = p.dse, p.dso, p.gse2, p.gsev
    cc = p.cc
    sv, ss, st, sg = p.sv, p.ss, p.st, p.sg
    cosT = p.sbuf("cosT", [128, S], F32)
    sinT = p.sbuf("sinT", [128, S], F32)
    am_sb = p.sbuf("am_sb", [128, 256], F32)
    am1_sb = p.sbuf("am1_sb", [128, 256], F32)
    rotm_sb = p.sbuf("rotm_sb", [128, 128], BF16)
    qb16 = p.sbuf("qb16", [128, 512], BF16)
    ones_k = p.sbuf("ones_k", [128, 1], BF16)
    ones_1 = p.sbuf("ones_1", [1, 512], BF16)
    ks1 = p.sbuf("ks1", [128, S], BF16)
    vs1 = p.sbuf("vs1", [128, (S // 128) * HD], BF16)
    arA = p.sbuf("arA", [128, 16384], BF16)
    arB = p.sbuf("arB", [128, 32768], BF16)
    _xt0 = p.sbuf("xt0", [128, D], F32)
    xt = [_xt0, _xt0]
    sq = p.sbuf("sq", [128, D], BF16)
    _xh0 = p.sbuf("xhat0", [128, D], BF16)
    xhat = [_xh0, _xh0]
    stat = p.sbuf("stat", [128, 4], F32)
    qf = p.sbuf("qf", [128, 512], F32)
    qt0 = p.sbuf("qt0", [128, 512], F32)
    qrt = [p.sbuf(f"qrt{i}", [128, 512], BF16) for i in range(2)]
    qtile = p.sbuf("qtile", [128, 16 * 256], BF16)
    probs = [p.sbuf(f"probs{i}", [128, 512], BF16) for i in range(2)]
    srec = p.sbuf("srec", [1, 512], F32)
    srecb = [p.sbuf(f"srecb{i}", [1, 512], BF16) for i in range(2)]
    bc_sb = p.sbuf("bc_sb", [128, 512], F32)
    ow = [p.sbuf(f"ow{i}", [128, 512], F32) for i in range(2)]
    blt = [p.sbuf(f"blt{i}", [128, 512], F32) for i in range(2)]
    aoC = p.sbuf("aoC", [128, 16 * 256], BF16)
    _aot0 = p.sbuf("aot0", [128, 16 * 128], BF16)
    aot = [_aot0, _aot0]
    ypt = [p.sbuf(f"ypt{i}", [128, 512], F32) for i in range(2)]

    ps_a = p.psum("ps_a", [128, 512], F32)
    ps_b = p.psum("ps_b", [128, 512], F32)
    ps_sc = [p.psum(f"ps_sc{i}", [128, 512], F32) for i in range(2)]
    ps_rot = p.psum("ps_rot", [128, 512], F32)
    ps_pv = p.psum("ps_pv", [128, 512], F32)
    ps_sum = p.psum("ps_sum", [1, 512], F32)
    ps_bc = p.psum("ps_bc", [128, 512], F32)

    with nc.Block() as blk:
        @blk.sync
        def _(sync):
            dse.inc(sync.dma_start(out=cosT[:, :], in_=cos_in[:, :]), 16)
            dse.inc(sync.dma_start(out=sinT[:, :], in_=sin_in[:, :]), 16)
            dse.inc(sync.dma_start(out=am_sb[:, :], in_=am_in[:, :]), 16)
            dse.inc(sync.dma_start(out=am1_sb[:, :], in_=am1_in[:, :]), 16)
            dse.inc(sync.dma_start(out=rotm_sb[:, :], in_=rotm_in[:, :]), 16)

        @blk.gpsimd
        def _(gpsimd):
            G = SerEng(gpsimd, sg)
            G.memset(ones_k[:, :], 1.0)
            G.memset(ones_1[:, :], 1.0)
        boot_g = sg.n

        c_eps = p.cf(EPS)
        c_im = p.cf(1.0 / D)
        c_scale = p.cf(SCALE)

        A_ld, A_sq, A_var, A_sd, A_xh, A_st = [], [], [], [], [], []
        for i in range(16):
            bi = i % 2

            @blk.sync
            def _(sync, i=i, bi=bi):
                sync.wait_ge(cc.h, ccbase + (i % 8) // 2 + 1)
                sync.wait_ge(dse.h, dse.n)
                sync.wait_ge(dso.h, dso.n)
                if i >= 1:
                    sync.wait_ge(sv.h, A_xh[i - 1])
                pr = _prow(i)
                dse.inc(sync.dma_start(out=xt[bi][:, :], in_=x_in[pr : pr + 128, :]), 16)
            A_ld.append(dse.n)

            @blk.scalar
            def _(scalar, i=i, bi=bi):
                SS = SerEng(scalar, ss)
                scalar.wait_ge(dse.h, dse.n)
                scalar.wait_ge(dso.h, dso.n)
                if i >= 1:
                    scalar.wait_ge(sv.h, A_var[i - 1])
                SS.activation(out=sq[:, :], in_=xt[bi][:, :], func=AF.Identity,
                              accum_out=stat[:, 0:1])
                SS.activation(out=sq[:, :], in_=xt[bi][:, :], func=AF.Square,
                              accum_out=stat[:, 1:2])
            A_sq.append(ss.n)

            @blk.vector
            def _(vector, i=i, bi=bi):
                V = SerEng(vector, sv)
                vector.wait_ge(ss.h, A_sq[i])
                V.tensor_scalar(stat[:, 0:1], stat[:, 0:1], c_im, None, ALU.mult)
                V.tensor_scalar(stat[:, 1:2], stat[:, 1:2], c_im, None, ALU.mult)
                V.tensor_tensor(stat[:, 2:3], stat[:, 0:1], stat[:, 0:1], ALU.mult)
                V.tensor_tensor(stat[:, 2:3], stat[:, 1:2], stat[:, 2:3], ALU.subtract)
            A_var.append(sv.n)

            @blk.scalar
            def _(scalar, i=i):
                SS = SerEng(scalar, ss)
                scalar.wait_ge(sv.h, A_var[i])
                SS.activation(out=stat[:, 3:4], in_=stat[:, 2:3], func=AF.Sqrt, bias=c_eps)
            A_sd.append(ss.n)

            @blk.vector
            def _(vector, i=i, bi=bi):
                V = SerEng(vector, sv)
                vector.wait_ge(ss.h, A_sd[i])
                V.reciprocal(stat[:, 2:3], stat[:, 3:4])
                vector.wait_ge(dse.h, dse.n)
                vector.wait_ge(dso.h, dso.n)
                V.tensor_scalar(
                    xhat[bi][:, :], xt[bi][:, :], stat[:, 0:1], stat[:, 2:3], ALU.subtract, ALU.mult)
            A_xh.append(sv.n)

            @blk.sync
            def _(sync, i=i, bi=bi):
                sync.wait_ge(sv.h, A_xh[i])
                dso.inc(sync.dma_start(out=xh_dram[i * 128 : (i + 1) * 128, :], in_=xhat[bi][:, :]), 16)
            A_st.append(dso.n)
        stageA_st = dso.n

        pe_tg = []
        q_ready = []
        kv_ready = []
        ev_marks = {}
        for tg in range(4):
            tg0 = tg * 512

            @blk.sync
            def _(sync, tg=tg):
                sync.wait_ge(dso.h, A_st[4 * tg + 3])
                if tg > 0:
                    sync.wait_ge(st.h, pe_tg[tg - 1])
                for c in range(32):
                    dse.inc(sync.dma_start(
                        out=view(arA, c * 512, [[1, 512]]),
                        in_=xh_dram[tg0 : tg0 + 512, c * 128 : (c + 1) * 128],
                        transpose=True), 16)
            xhT_ld = dse.n

            tg_ev0 = sv.n
            fgroups = [("k", 0)] + [("q", g) for g in range(4)]
            ev_idx = 0
            for gi, (kind, g) in enumerate(fgroups):
                sb = gi % 2
                win = wk if kind == "k" else wq
                ncols = 128 if kind == "k" else 512

                @blk.gpsimd
                def _(gpsimd, tg=tg, g=g, sb=sb, gi=gi, win=win, ncols=ncols):
                    gpsimd.wait_ge(sg.h, boot_g)
                    key = ("peg", tg, gi - 2)
                    if key in ev_marks:
                        gpsimd.wait_ge(st.h, ev_marks[key])
                    elif tg > 0:
                        gpsimd.wait_ge(st.h, pe_tg[tg - 1])
                    gse2[sb].inc(gpsimd.dma_start(
                        out=view(arB, sb * 16384, [[ncols, 32], [1, ncols]]),
                        in_=wstrip_in(win, g * 512 if kind == "q" else 0, ncols)), 16)
                ev_marks[("ld", tg, gi)] = gse2[sb].n

                nsub = 1 if kind == "k" else 4
                for sub in range(nsub):
                    f = g * 4 + sub
                    idx = ev_idx
                    ev_idx += 1
                    ps = ps_a if idx % 2 == 0 else ps_b

                    @blk.tensor
                    def _(tensor, tg=tg, gi=gi, idx=idx, ps=ps, sb=sb, sub=sub, ncols=ncols):
                        tensor.wait_ge(gse2[sb].h, ev_marks[("ld", tg, gi)])
                        tensor.wait_ge(dse.h, dse.n)
                        tensor.wait_ge(dso.h, dso.n)
                        key = ("ev", tg, idx - 2)
                        tensor.wait_ge(sv.h, ev_marks.get(key, tg_ev0))
                        for c in range(32):
                            mm = tensor.matmul(
                                ps[:, :],
                                view(arB, sb * 16384 + c * ncols + sub * 128, [[1, 128]]),
                                view(arA, c * 512, [[1, 512]]),
                                start=(c == 0), stop=(c == 31))
                        st.inc(mm)
                    ev_marks[("peg", tg, gi)] = st.n
                    ev_marks[("pemm", tg, idx)] = st.n

                    @blk.vector
                    def _(vector, tg=tg, idx=idx, ps=ps):
                        V = SerEng(vector, sv)
                        vector.wait_ge(st.h, ev_marks[("pemm", tg, idx)])
                        V.tensor_copy(qb16[:, :], ps[:, :])
                    v1 = sv.n

                    @blk.tensor
                    def _(tensor, v1=v1):
                        tensor.wait_ge(sv.h, v1)
                        st.inc(tensor.matmul(ps_rot[:, :], rotm_sb[:, :], qb16[:, :],
                                             start=True, stop=True))
                    t1 = st.n

                    @blk.vector
                    def _(vector, tg=tg, f=f, kind=kind, idx=idx, ps=ps, tg0=tg0, t1=t1):
                        V = SerEng(vector, sv)
                        vector.wait_ge(st.h, t1)
                        V.tensor_tensor(qt0[:, :], ps_rot[:, :], sinT[:, tg0 : tg0 + 512], ALU.mult)
                        V.tensor_tensor(qf[:, :], ps[:, :], cosT[:, tg0 : tg0 + 512], ALU.mult)
                        if kind == "k":
                            dst = ks1[:, tg0 : tg0 + 512]
                            V.tensor_tensor(dst, qf[:, :], qt0[:, :], ALU.add)
                        else:
                            par = idx % 2
                            vector.wait_ge(dse.h, dse.n)
                            vector.wait_ge(dso.h, dso.n)
                            V.tensor_tensor(qrt[par][:, :], qf[:, :], qt0[:, :], ALU.add)
                    ev_marks[("ev", tg, idx)] = sv.n

                    if kind == "q":
                        @blk.sync
                        def _(sync, tg=tg, f=f, idx=idx, tg0=tg0):
                            sync.wait_ge(sv.h, ev_marks[("ev", tg, idx)])
                            dso.inc(sync.dma_start(
                                out=q_dram[f * 128 : (f + 1) * 128, tg0 : tg0 + 512],
                                in_=qrt[idx % 2][:, :]), 16)

            @blk.gpsimd
            def _(gpsimd, tg=tg):
                gpsimd.wait_ge(st.h, st.n)
                gsev.inc(gpsimd.dma_start(
                    out=view(arB, 0, [[128, 32], [1, 128]]),
                    in_=wstrip_in(wv, 0, 128)), 16)
            vld = gsev.n

            for sub in range(4):
                @blk.tensor
                def _(tensor, tg=tg, sub=sub, vld=vld):
                    tensor.wait_ge(gsev.h, vld)
                    tensor.wait_ge(sv.h, sv.n)
                    for c in range(32):
                        mm = tensor.matmul(
                            ps_a[0:128, 0:128],
                            view(arA, c * 512 + sub * 128, [[1, 128]]),
                            view(arB, c * 128, [[1, 128]]),
                            start=(c == 0), stop=(c == 31))
                    st.inc(mm)
                vpe = st.n

                @blk.vector
                def _(vector, tg=tg, sub=sub, vpe=vpe):
                    V = SerEng(vector, sv)
                    vector.wait_ge(st.h, vpe)
                    tt = tg * 4 + sub
                    V.tensor_copy(vs1[:, tt * HD : (tt + 1) * HD], ps_a[0:128, 0:128])
            pe_tg.append(st.n)
            q_ready.append(dso.n)
            kv_ready.append(sv.n)
        stageB_v = sv.n

        expc = {0: 0, 1: 0}
        pvc = {0: 0, 1: 0}
        last_ao_ev = 0
        wcount = 0
        last_ev_par = [sv.n, sv.n]
        pv_b = [ps_pv, ps_a]
        bc_b = [ps_bc, ps_b]
        for ch in range(8):
            if ch == 0:
                wins = [(0, 4, 1)]
            elif ch < 7:
                wins = [(2 * (ch - 1), 4, 0), (2 * ch, 4, 1)]
            else:
                wins = [(12, 4, 0), (14, 2, 1)]

            @blk.sync
            def _(sync, ch=ch):
                tg_need = min((2 * ch + 3) // 4, 3)
                sync.wait_ge(sv.h, kv_ready[max(tg_need, ch // 2)])
                sync.wait_ge(dso.h, q_ready[ch // 2])
                sync.wait_ge(st.h, st.n)
                for j in range(16):
                    dse.inc(sync.dma_start(
                        out=qtile[:, j * 256 : (j + 1) * 256],
                        in_=q_dram[j * 128 : (j + 1) * 128, ch * 256 : (ch + 1) * 256]), 16)
            qld = dse.n

            for j in range(0, 16, 2):
                for (kt0, nkts, wslot) in wins:
                    par = wcount % 2
                    pvx = pv_b[par]
                    bcx = bc_b[par]
                    sumx = ps_sum[0:1, :] if par == 0 else ps_rot[0:1, :]
                    for ki in range(nkts):
                        kt = kt0 + ki
                        pp = kt % 2

                        @blk.tensor
                        def _(tensor, j=j, kt=kt, pp=pp, qld=qld):
                            tensor.wait_ge(dse.h, dse.n)
                            tensor.wait_ge(dso.h, dso.n)
                            tensor.wait_ge(ss.h, expc[pp])
                            st.inc(tensor.matmul(
                                ps_sc[pp][:, :],
                                ks1[:, kt * 128 : (kt + 1) * 128],
                                qtile[:, j * 256 : (j + 2) * 256],
                                start=True, stop=True))
                        scd = st.n

                        @blk.scalar
                        def _(scalar, pp=pp, scd=scd):
                            SS = SerEng(scalar, ss)
                            scalar.wait_ge(st.h, scd)
                            scalar.wait_ge(st.h, pvc[pp])
                            SS.activation(
                                out=probs[pp][:, :], in_=ps_sc[pp][:, :],
                                func=AF.Sigmoid, scale=c_scale)
                        expc[pp] = ss.n
                        expd = ss.n

                        @blk.tensor
                        def _(tensor, kt=kt, pp=pp, expd=expd, ki=ki, nkts=nkts,
                              pvx=pvx, sumx=sumx, par=par):
                            tensor.wait_ge(ss.h, expd)
                            if ki == 0:
                                tensor.wait_ge(sv.h, last_ev_par[par])
                            tensor.matmul(
                                pvx[:, :],
                                vs1[:, kt * HD : (kt + 1) * HD],
                                probs[pp][:, :],
                                start=(ki == 0), stop=(ki == nkts - 1))
                            st.inc(tensor.matmul(
                                sumx, ones_k[:, :], probs[pp][:, :],
                                start=(ki == 0), stop=(ki == nkts - 1)))
                        pvc[pp] = st.n
                    pvdone = st.n

                    @blk.vector
                    def _(vector, pvdone=pvdone, sumx=sumx, par=par):
                        V = SerEng(vector, sv)
                        vector.wait_ge(st.h, pvdone)
                        V.reciprocal(srec[:, :], sumx)
                        V.tensor_copy(srecb[par][:, :], srec[:, :])
                    recd = sv.n

                    @blk.tensor
                    def _(tensor, recd=recd, bcx=bcx, par=par):
                        tensor.wait_ge(sv.h, recd)
                        tensor.wait_ge(sv.h, last_ev_par[par])
                        st.inc(tensor.matmul(
                            bcx[:, :], ones_1[0:1, 0:128], srecb[par][0:1, :],
                            start=True, stop=True))
                    bcd = st.n

                    @blk.vector
                    def _(vector, wslot=wslot, bcd=bcd, pvx=pvx, bcx=bcx):
                        V = SerEng(vector, sv)
                        vector.wait_ge(st.h, bcd)
                        V.tensor_copy(bc_sb[:, :], bcx[:, :])
                        V.tensor_tensor(ow[wslot][:, :], pvx[:, :], bc_sb[:, :], ALU.mult)
                    last_ev_par[par] = sv.n
                    last_ao_ev = sv.n
                    wcount += 1

                am2 = bass.AP(am_sb, 0, [[256, 128], [0, 2], [1, 256]])
                am12 = bass.AP(am1_sb, 0, [[256, 128], [0, 2], [1, 256]])

                @blk.vector
                def _(vector, ch=ch, j=j, nw=len(wins), am2=am2, am12=am12):
                    V = SerEng(vector, sv)
                    vector.wait_ge(dse.h, dse.n)
                    vector.wait_ge(dso.h, dso.n)
                    dst = aoC[:, j * 256 : (j + 2) * 256]
                    if nw == 1:
                        V.tensor_copy(dst, ow[1][:, :])
                    else:
                        V.tensor_tensor(blt[0][:, :], ow[0][:, :], am12, ALU.mult)
                        V.tensor_tensor(blt[1][:, :], ow[1][:, :], am2, ALU.mult)
                        V.tensor_tensor(dst, blt[0][:, :], blt[1][:, :], ALU.add)
                last_ao_ev = sv.n

                @blk.sync
                def _(sync, ch=ch, j=j, aoev=last_ao_ev):
                    sync.wait_ge(sv.h, aoev)
                    dso.inc(sync.dma_start(
                        out=bass.AP(ao_dram, (j * 128) * S + ch * 256,
                                    [[S, 128], [128 * S, 2], [1, 256]]),
                        in_=bass.AP(aoC, j * 256, [[4096, 128], [256, 2], [1, 256]])), 16)
        stageC_ao = dse.n

        aold = {}
        ypst = {}
        st_d0 = st.n
        for fg in range(8):
            sb = fg % 2

            @blk.gpsimd
            def _(gpsimd, fg=fg, sb=sb, st_d0=st_d0):
                if fg < 2:
                    gpsimd.wait_ge(st.h, st_d0)
                else:
                    gpsimd.wait_ge(st.h, aold[("pe", fg - 2, 15)])
                gse2[sb].inc(gpsimd.dma_start(
                    out=view(arB, sb * 8192, [[512, 16], [1, 512]]),
                    in_=bass.AP(wo, fg * 512, [[D, 128], [128 * D, 16], [1, 512]])), 16)
            wold = gse2[sb].n

            for tt in range(16):
                par = tt % 2

                @blk.sync
                def _(sync, fg=fg, tt=tt, par=par):
                    if fg == 0 and tt == 0:
                        sync.wait_ge(dse.h, dse.n)
                        sync.wait_ge(dso.h, dso.n)
                    key = ("pe", fg, tt - 1) if tt >= 1 else ("pe", fg - 1, tt + 15)
                    if key in aold:
                        sync.wait_ge(st.h, aold[key])
                    dse.inc(sync.dma_start(
                        out=aot[par][:, :],
                        in_=bass.AP(ao_dram, tt * 128, [[S, 128], [128 * S, 16], [1, 128]])), 16)
                aold[("ld", fg, tt)] = dse.n

                @blk.tensor
                def _(tensor, fg=fg, tt=tt, par=par, wold=wold, sb=sb):
                    tensor.wait_ge(gse2[sb].h, wold)
                    tensor.wait_ge(dse.h, dse.n)
                    tensor.wait_ge(dso.h, dso.n)
                    tensor.wait_ge(sv.h, sv.n)
                    for c in range(16):
                        mm = tensor.matmul(
                            ps_a[:, :],
                            view(aot[par], c * 128, [[1, 128]]),
                            view(arB, sb * 8192 + c * 512, [[1, 512]]),
                            start=(c == 0), stop=(c == 15))
                    st.inc(mm)
                aold[("pe", fg, tt)] = st.n

                @blk.vector
                def _(vector, fg=fg, tt=tt, par=par):
                    V = SerEng(vector, sv)
                    vector.wait_ge(st.h, aold[("pe", fg, tt)])
                    vector.wait_ge(dse.h, dse.n)
                    vector.wait_ge(dso.h, dso.n)
                    V.tensor_copy(ypt[par][:, :], ps_a[:, :])
                ypev = sv.n

                @blk.sync
                def _(sync, fg=fg, tt=tt, par=par, ypev=ypev):
                    sync.wait_ge(sv.h, ypev)
                    dso.inc(sync.dma_start(
                        out=yp_out[_prow(tt) : _prow(tt) + 128, fg * 512 : (fg + 1) * 512],
                        in_=ypt[par][:, :]), 16)
                ypst[("st", fg, tt)] = dse.n

    p.pop_scope()


def l1attn_inmaps(x2_full, Wq, Wk, Wv, Wo, consts):
    cos, sin = _rope_l1()
    am, am1 = _alpha_tables()
    WqT = np.ascontiguousarray(Wq.T)
    WkT = np.ascontiguousarray(Wk.T)
    WvT = np.ascontiguousarray(Wv.T)
    WoT = np.ascontiguousarray(Wo.T)
    maps = []
    for c in range(8):
        b, hh = c // 2, c % 2
        m = dict(consts)
        m["xb2"] = np.ascontiguousarray(x2_full[b])
        m["wqT_s"] = np.ascontiguousarray(WqT[:, hh * 2048 : (hh + 1) * 2048])
        m["wkT"] = WkT
        m["wvT"] = WvT
        m["woT_s"] = np.ascontiguousarray(WoT[hh * 2048 : (hh + 1) * 2048, :])
        m["cos1_t"] = cos
        m["sin1_t"] = sin
        m["am_t"] = am
        m["am1_t"] = am1
        m["rotm_t"] = _rotm_l1()
        maps.append(m)
    return maps


def emit_ffn1(p, x_in, ps_in, w1_in, bias_in, x4_out,
              x3_dram, xh3_dram, yd_dram, ccbase):
    nc = p.nc
    p.push_scope("f1_")
    dse, dso, sv, ss, st, cc, gsev = p.dse, p.dso, p.sv, p.ss, p.st, p.cc, p.gsev
    ln2T = p.sbuf("ln2T", [128, 32 * 1024], BF16)
    arW = p.sbuf("arW", [128, 2 * 32 * 512], BF16)
    ta = p.sbuf("ta", [128, D], F32)
    tb = p.sbuf("tb", [128, D], F32)
    tc = p.sbuf("tc", [128, D], F32)
    sq = p.sbuf("sq", [128, D], BF16)
    stat = p.sbuf("stat", [128, 8], F32)
    biasbc = p.sbuf("biasbc", [128, 1024], F32)
    ypt = [p.sbuf(f"ypt{i}", [128, 512], F32) for i in range(2)]
    ps_a = p.psum("ps_a", [128, 512], F32)
    ps_b = p.psum("ps_b", [128, 512], F32)

    with nc.Block() as blk:
        c_eps = p.cf(EPS)
        c_im = p.cf(1.0 / D)

        for i in range(8):
            r0 = i * 128

            @blk.sync
            def _(sync, i=i, r0=r0):
                sync.wait_ge(cc.h, ccbase + i // 2 + 1)
                sync.wait_ge(sv.h, sv.n)
                sync.wait_ge(dse.h, dse.n)
                sync.wait_ge(dso.h, dso.n)
                dse.inc(sync.dma_start(out=ta[:, :], in_=x_in[r0 : r0 + 128, :]), 16)
                dse.inc(sync.dma_start(out=tb[:, :], in_=ps_in[r0 : r0 + 128, :]), 16)

            @blk.scalar
            def _(scalar, i=i):
                SS = SerEng(scalar, ss)
                scalar.wait_ge(sv.h, sv.n)
                scalar.wait_ge(dse.h, dse.n)
                scalar.wait_ge(dso.h, dso.n)
                SS.activation(out=sq[:, :], in_=tb[:, :], func=AF.Identity,
                              accum_out=stat[:, 0:1])
                SS.activation(out=sq[:, :], in_=tb[:, :], func=AF.Square,
                              accum_out=stat[:, 1:2])

            @blk.vector
            def _(vector, i=i):
                V = SerEng(vector, sv)
                vector.wait_ge(ss.h, ss.n)
                V.tensor_scalar(stat[:, 0:1], stat[:, 0:1], c_im, None, ALU.mult)
                V.tensor_scalar(stat[:, 1:2], stat[:, 1:2], c_im, None, ALU.mult)
                V.tensor_tensor(stat[:, 2:3], stat[:, 0:1], stat[:, 0:1], ALU.mult)
                V.tensor_tensor(stat[:, 2:3], stat[:, 1:2], stat[:, 2:3], ALU.subtract)

            @blk.scalar
            def _(scalar, i=i):
                SS = SerEng(scalar, ss)
                scalar.wait_ge(sv.h, sv.n)
                SS.activation(out=stat[:, 3:4], in_=stat[:, 2:3], func=AF.Sqrt, bias=c_eps)

            @blk.vector
            def _(vector, i=i, r0=r0):
                V = SerEng(vector, sv)
                vector.wait_ge(ss.h, ss.n)
                V.reciprocal(stat[:, 3:4], stat[:, 3:4])
                V.tensor_scalar(tc[:, :], tb[:, :], stat[:, 0:1], stat[:, 3:4], ALU.subtract, ALU.mult)
                V.tensor_tensor(ta[:, :], ta[:, :], tc[:, :], ALU.add)

            @blk.sync
            def _(sync, i=i, r0=r0):
                sync.wait_ge(sv.h, sv.n)
                dso.inc(sync.dma_start(out=x3_dram[r0 : r0 + 128, :], in_=ta[:, :]), 16)

            @blk.scalar
            def _(scalar, i=i):
                SS = SerEng(scalar, ss)
                scalar.wait_ge(sv.h, sv.n)
                scalar.wait_ge(dse.h, dse.n)
                scalar.wait_ge(dso.h, dso.n)
                SS.activation(out=sq[:, :], in_=ta[:, :], func=AF.Identity,
                              accum_out=stat[:, 0:1])
                SS.activation(out=sq[:, :], in_=ta[:, :], func=AF.Square,
                              accum_out=stat[:, 1:2])

            @blk.vector
            def _(vector, i=i):
                V = SerEng(vector, sv)
                vector.wait_ge(ss.h, ss.n)
                V.tensor_scalar(stat[:, 0:1], stat[:, 0:1], c_im, None, ALU.mult)
                V.tensor_scalar(stat[:, 1:2], stat[:, 1:2], c_im, None, ALU.mult)
                V.tensor_tensor(stat[:, 2:3], stat[:, 0:1], stat[:, 0:1], ALU.mult)
                V.tensor_tensor(stat[:, 2:3], stat[:, 1:2], stat[:, 2:3], ALU.subtract)

            @blk.scalar
            def _(scalar, i=i):
                SS = SerEng(scalar, ss)
                scalar.wait_ge(sv.h, sv.n)
                SS.activation(out=stat[:, 3:4], in_=stat[:, 2:3], func=AF.Sqrt, bias=c_eps)

            @blk.vector
            def _(vector, i=i):
                V = SerEng(vector, sv)
                vector.wait_ge(ss.h, ss.n)
                V.reciprocal(stat[:, 3:4], stat[:, 3:4])
                V.tensor_scalar(sq[:, :], ta[:, :], stat[:, 0:1], stat[:, 3:4], ALU.subtract, ALU.mult)

            @blk.sync
            def _(sync, i=i, r0=r0):
                sync.wait_ge(sv.h, sv.n)
                dso.inc(sync.dma_start(out=xh3_dram[r0 : r0 + 128, :], in_=sq[:, :]), 16)

            if i in (3, 7):
                hb = 0 if i == 3 else 512

                @blk.sync
                def _(sync, hb=hb):
                    sync.wait_ge(dso.h, dso.n)
                    for c in range(32):
                        dse.inc(sync.dma_start(
                            out=view(ln2T, c * 1024 + hb, [[1, 512]]),
                            in_=xh3_dram[hb : hb + 512, c * 128 : (c + 1) * 128],
                            transpose=True), 16)

        wmk = {}
        bmk = {}
        for og in range(8):
            wh_ = og % 2

            @blk.sync
            def _(sync, og=og, wh_=wh_):
                if og == 0:
                    sync.wait_ge(dso.h, dso.n)
                    sync.wait_ge(cc.h, cc.n)
                key = ("pe", og - 2)
                if key in wmk:
                    sync.wait_ge(st.h, wmk[key])
                    sync.wait_ge(sv.h, bmk[og - 2])
                for c in range(32):
                    dse.inc(sync.dma_start(
                        out=view(arW, wh_ * 16384 + c * 512, [[1, 512]]),
                        in_=w1_in[og * 512 : (og + 1) * 512, c * 128 : (c + 1) * 128],
                        transpose=True), 16)
                dse.inc(sync.dma_start(
                    out=biasbc[:, wh_ * 512 : wh_ * 512 + 512],
                    in_=bass.AP(bias_in, og * 512, [[0, 128], [1, 512]])), 16)
                sync.wait_ge(dse.h, dse.n)
            wld = dse.n

            for tsub in range(8):
                pp = tsub % 2
                ps = ps_a if pp == 0 else ps_b

                @blk.tensor
                def _(tensor, og=og, tsub=tsub, ps=ps, wh_=wh_, wld=wld):
                    tensor.wait_ge(dse.h, wld)
                    tensor.wait_ge(sv.h, sv.n)
                    for c in range(32):
                        mm = tensor.matmul(
                            ps[:, :],
                            view(ln2T, c * 1024 + tsub * 128, [[1, 128]]),
                            view(arW, wh_ * 16384 + c * 512, [[1, 512]]),
                            start=(c == 0), stop=(c == 31))
                    st.inc(mm)
                ped = st.n

                @blk.vector
                def _(vector, tsub=tsub, ped=ped, pp=pp, ps=ps, wh_=wh_):
                    V = SerEng(vector, sv)
                    vector.wait_ge(st.h, ped)
                    vector.wait_ge(gsev.h, gsev.n)
                    V.tensor_tensor(ypt[pp][:, :], ps[:, :],
                                    biasbc[:, wh_ * 512 : wh_ * 512 + 512], ALU.add)
                yev = sv.n

                @blk.gpsimd
                def _(gpsimd, og=og, tsub=tsub, yev=yev, pp=pp):
                    gpsimd.wait_ge(sv.h, yev)
                    gsev.inc(gpsimd.dma_start(
                        out=yd_dram[tsub * 128 : (tsub + 1) * 128, og * 512 : (og + 1) * 512],
                        in_=ypt[pp][:, :]), 16)
            wmk[("pe", og)] = st.n
            bmk[og] = sv.n

        for i in range(8):
            r0 = i * 128

            @blk.sync
            def _(sync, i=i, r0=r0):
                sync.wait_ge(dse.h, dse.n)
                sync.wait_ge(dso.h, dso.n)
                sync.wait_ge(gsev.h, gsev.n)
                sync.wait_ge(sv.h, sv.n)
                dse.inc(sync.dma_start(out=ta[:, :], in_=x3_dram[r0 : r0 + 128, :]), 16)
                dse.inc(sync.dma_start(out=tb[:, :], in_=yd_dram[r0 : r0 + 128, :]), 16)

            @blk.scalar
            def _(scalar, i=i):
                SS = SerEng(scalar, ss)
                scalar.wait_ge(dse.h, dse.n)
                scalar.wait_ge(dso.h, dso.n)
                scalar.wait_ge(sv.h, sv.n)
                SS.activation(out=sq[:, :], in_=tb[:, :], func=AF.Identity,
                              accum_out=stat[:, 0:1])
                SS.activation(out=sq[:, :], in_=tb[:, :], func=AF.Square,
                              accum_out=stat[:, 1:2])

            @blk.vector
            def _(vector, i=i):
                V = SerEng(vector, sv)
                vector.wait_ge(dse.h, dse.n)
                vector.wait_ge(dso.h, dso.n)
                vector.wait_ge(ss.h, ss.n)
                V.tensor_scalar(stat[:, 0:1], stat[:, 0:1], c_im, None, ALU.mult)
                V.tensor_scalar(stat[:, 1:2], stat[:, 1:2], c_im, None, ALU.mult)
                V.tensor_tensor(stat[:, 2:3], stat[:, 0:1], stat[:, 0:1], ALU.mult)
                V.tensor_tensor(stat[:, 2:3], stat[:, 1:2], stat[:, 2:3], ALU.subtract)

            @blk.scalar
            def _(scalar, i=i):
                SS = SerEng(scalar, ss)
                scalar.wait_ge(sv.h, sv.n)
                SS.activation(out=stat[:, 3:4], in_=stat[:, 2:3], func=AF.Sqrt, bias=c_eps)

            @blk.vector
            def _(vector, i=i):
                V = SerEng(vector, sv)
                vector.wait_ge(ss.h, ss.n)
                V.reciprocal(stat[:, 3:4], stat[:, 3:4])
                V.tensor_scalar(tc[:, :], tb[:, :], stat[:, 0:1], stat[:, 3:4], ALU.subtract, ALU.mult)
                V.tensor_tensor(ta[:, :], ta[:, :], tc[:, :], ALU.add)

            @blk.sync
            def _(sync, i=i, r0=r0):
                sync.wait_ge(sv.h, sv.n)
                dso.inc(sync.dma_start(out=x4_out[r0 : r0 + 128, :], in_=ta[:, :]), 16)

        @blk.sync
        def _(sync):
            sync.wait_ge(dse.h, dse.n)
            sync.wait_ge(dso.h, dso.n)

    p.pop_scope()


def ffn1_inmaps(x2_full, yp0, yp1, w1_full, bias, consts):
    maps = []
    for c in range(8):
        b, half = c // 2, c % 2
        rows = slice(half * 1024, (half + 1) * 1024)
        m = dict(consts)
        m["x_r"] = np.ascontiguousarray(x2_full[b][rows])
        m["p0_r"] = np.ascontiguousarray(yp0[b][rows])
        m["p1_r"] = np.ascontiguousarray(yp1[b][rows])
        m["w1_f"] = w1_full
        m["bias"] = bias
        maps.append(m)
    return maps


def build_fused():
    p = PB()
    nc = p.nc
    EI, IN = "ExternalInput", "Internal"
    xb = nc.dram_tensor("xb", [S, D], F32, kind=EI)
    x_half = nc.dram_tensor("x_half", [1024, D], F32, kind=EI)
    wq_s = nc.dram_tensor("wq_s", [D, 2048], F32, kind=EI)
    wk_s = nc.dram_tensor("wk_s", [D, 512], F32, kind=EI)
    wv_s = nc.dram_tensor("wv_s", [D, 512], F32, kind=EI)
    wo_s = nc.dram_tensor("wo_s", [2048, D], F32, kind=EI)
    cos_t = nc.dram_tensor("cos_t", [HD, S], F32, kind=EI)
    sin_t = nc.dram_tensor("sin_t", [HD, S], F32, kind=EI)
    tril_t = nc.dram_tensor("tril_t", [4 * 128, 512], BF16, kind=EI)
    qw_s = nc.dram_tensor("qw_s", [8, 512, 128], I32, kind=EI)
    lut4_s = nc.dram_tensor("lut4_s", [512, 16], F32, kind=EI)
    lut8_s = nc.dram_tensor("lut8_s", [512, 256], F32, kind=EI)
    w1q_s = nc.dram_tensor("w1q_s", [512, D], I32, kind=EI)
    sc_s = nc.dram_tensor("sc_s", [512, 32], F32, kind=EI)
    zr_s = nc.dram_tensor("zr_s", [512, 32], F32, kind=EI)
    mk_s = nc.dram_tensor("mk_s", [512, D], I32, kind=EI)
    s2_s = nc.dram_tensor("s2_s", [512, 1], F32, kind=EI)
    shf = nc.dram_tensor("shf", [128, 32], I32, kind=EI)
    kramp = nc.dram_tensor("kramp", [128, 256], F32, kind=EI)
    bias0 = nc.dram_tensor("bias0", [D], F32, kind=EI)
    bias1 = nc.dram_tensor("bias1", [D], F32, kind=EI)
    wqT_s = nc.dram_tensor("wqT_s", [D, 2048], F32, kind=EI)
    wkT = nc.dram_tensor("wkT", [D, HD], F32, kind=EI)
    wvT = nc.dram_tensor("wvT", [D, HD], F32, kind=EI)
    woT_s = nc.dram_tensor("woT_s", [2048, D], F32, kind=EI)
    cos1_t = nc.dram_tensor("cos1_t", [HD, S], F32, kind=EI)
    sin1_t = nc.dram_tensor("sin1_t", [HD, S], F32, kind=EI)
    am_t = nc.dram_tensor("am_t", [128, 256], F32, kind=EI)
    am1_t = nc.dram_tensor("am1_t", [128, 256], F32, kind=EI)
    rotm_t = nc.dram_tensor("rotm_t", [128, 128], BF16, kind=EI)
    x4_r = nc.dram_tensor("x4_r", [1024, D], F32, kind="ExternalOutput")
    wl_sl = nc.dram_tensor("wl_sl", [512, D], BF16, kind=IN)
    wh_sl = nc.dram_tensor("wh_sl", [512, D], BF16, kind=IN)
    w1_sl = nc.dram_tensor("w1_sl", [512, D], BF16, kind=IN)
    wl_full = nc.dram_tensor("wl_full", [D, D], BF16, kind=IN)
    wh_full = nc.dram_tensor("wh_full", [D, D], BF16, kind=IN)
    w1_full = nc.dram_tensor("w1_full", [D, D], BF16, kind=IN)
    xh0_d = nc.dram_tensor("xh0_d", [S, D], BF16, kind=IN)
    q0_d = nc.dram_tensor("q0_d", [2048, S], BF16, kind=IN)
    ao0_d = nc.dram_tensor("ao0_d", [2048, S], BF16, kind=IN)
    yp0_d = nc.dram_tensor("yp0_d", [S, D], F32, kind=IN)
    psum0_d = nc.dram_tensor("psum0_d", [1024, D], F32, kind=IN)
    x1_d = nc.dram_tensor("x1_d", [1024, D], F32, kind=IN)
    xh2_d = nc.dram_tensor("xh2_d", [1024, D], BF16, kind=IN)
    ydl_d = nc.dram_tensor("ydl_d", [1024, D], F32, kind=IN)
    ydh_d = nc.dram_tensor("ydh_d", [1024, D], F32, kind=IN)
    x2h_d = nc.dram_tensor("x2h_d", [1024, D], F32, kind=IN)
    x2f_d = nc.dram_tensor("x2f_d", [S, D], F32, kind=IN)
    xh1_d = nc.dram_tensor("xh1_d", [S, D], BF16, kind=IN)
    q1_d = nc.dram_tensor("q1_d", [2048, S], BF16, kind=IN)
    ao1_d = nc.dram_tensor("ao1_d", [2048, S], BF16, kind=IN)
    yp1_d = nc.dram_tensor("yp1_d", [S, D], F32, kind=IN)
    psum1_d = nc.dram_tensor("psum1_d", [1024, D], F32, kind=IN)
    x3_d = nc.dram_tensor("x3_d", [1024, D], F32, kind=IN)
    xh3_d = nc.dram_tensor("xh3_d", [1024, D], BF16, kind=IN)
    yd_d = nc.dram_tensor("yd_d", [1024, D], F32, kind=IN)

    p.alloc_consts()
    p.make_sems()
    G8 = [list(range(8))]
    GP = [[0, 1], [2, 3], [4, 5], [6, 7]]

    with nc.Block() as blk:
        @blk.sync
        def _(sync):
            p.load_consts(sync, p.dse)

    emit_dequant(p, qw_s, lut4_s, lut8_s, w1q_s, sc_s, zr_s, mk_s, s2_s,
                 shf, kramp, wl_sl, wh_sl, w1_sl)

    with nc.Block() as blk:
        @blk.gpsimd
        def _(g):
            g.wait_ge(p.dse.h, p.dse.n)
            g.wait_ge(p.dso.h, p.dso.n)
            p.cc.inc(g.collective_compute(
                "AllGather", ALU.bypass, replica_groups=G8,
                ins=[wl_sl.ap().opt()], outs=[wl_full.ap().opt()]))
            p.cc.inc(g.collective_compute(
                "AllGather", ALU.bypass, replica_groups=G8,
                ins=[wh_sl.ap().opt()], outs=[wh_full.ap().opt()]))
            p.cc.inc(g.collective_compute(
                "AllGather", ALU.bypass, replica_groups=G8,
                ins=[w1_sl.ap().opt()], outs=[w1_full.ap().opt()]))
    p.barrier(skip_cc=True)

    emit_l0attn(p, xb, wq_s, wk_s, wv_s, wo_s, cos_t, sin_t, tril_t, yp0_d,
                xh0_d, q0_d, ao0_d)

    ccb0 = p.cc.n
    with nc.Block() as blk:
        @blk.gpsimd
        def _(g):
            g.wait_ge(p.dse.h, p.dse.n)
            g.wait_ge(p.dso.h, p.dso.n)
            for k in range(4):
                p.cc.inc(g.collective_compute(
                    "ReduceScatter", ALU.add, replica_groups=GP,
                    ins=[bass.AP(yp0_d, k * 512 * D, [[1, 512 * D]])],
                    outs=[bass.AP(psum0_d, k * 256 * D, [[1, 256 * D]])]))
    p.barrier(skip_cc=True)

    emit_ffn0(p, x_half, psum0_d, wl_full, wh_full, bias0, x2h_d,
              x1_d, xh2_d, ydl_d, ydh_d, ccb0)

    ccbx = p.cc.n
    with nc.Block() as blk:
        @blk.gpsimd
        def _(g):
            g.wait_ge(p.dse.h, p.dse.n)
            g.wait_ge(p.dso.h, p.dso.n)
            for k in range(4):
                p.cc.inc(g.collective_compute(
                    "AllGather", ALU.bypass, replica_groups=GP,
                    ins=[bass.AP(x2h_d, k * 256 * D, [[1, 256 * D]])],
                    outs=[bass.AP(x2f_d, k * 512 * D, [[1, 512 * D]])]))
    p.barrier(skip_cc=True)

    emit_l1attn(p, x2f_d, wqT_s, wkT, wvT, woT_s, cos1_t, sin1_t, am_t, am1_t,
                rotm_t, yp1_d, xh1_d, q1_d, ao1_d, ccbx)

    ccb1 = p.cc.n
    with nc.Block() as blk:
        @blk.gpsimd
        def _(g):
            g.wait_ge(p.dse.h, p.dse.n)
            g.wait_ge(p.dso.h, p.dso.n)
            for k in range(4):
                p.cc.inc(g.collective_compute(
                    "ReduceScatter", ALU.add, replica_groups=GP,
                    ins=[bass.AP(yp1_d, k * 512 * D, [[1, 512 * D]])],
                    outs=[bass.AP(psum1_d, k * 256 * D, [[1, 256 * D]])]))
    p.barrier(skip_cc=True)

    emit_ffn1(p, x2h_d, psum1_d, w1_full, bias1, x4_r,
              x3_d, xh3_d, yd_d, ccb1)

    p.barrier()
    p.close()
    return p


def fused_inmaps(z, consts):
    x = np.asarray(z["L0_attn_x"], dtype=np.float32)
    w_qkv = np.asarray(z["L0_attn_w_qkv"], dtype=np.float32)
    w_o = np.asarray(z["L0_attn_w_o"], dtype=np.float32)
    cos0, sin0 = _rope_l0()
    tril = _tril_masks()
    cos1, sin1 = _rope_l1()
    am, am1 = _alpha_tables()
    rotm = _rotm_l1()
    WqT = np.ascontiguousarray(np.asarray(z["L1_attn_Wq"], dtype=np.float32).T)
    WkT = np.ascontiguousarray(np.asarray(z["L1_attn_Wk"], dtype=np.float32).T)
    WvT = np.ascontiguousarray(np.asarray(z["L1_attn_Wv"], dtype=np.float32).T)
    WoT = np.ascontiguousarray(np.asarray(z["L1_attn_Wo"], dtype=np.float32).T)
    shf_a = np.broadcast_to(np.arange(32, dtype=np.int32), (128, 32)).copy()
    kramp_a = np.broadcast_to(np.arange(256, dtype=np.float32), (128, 256)).copy()
    maps = []
    for c in range(8):
        b, hh = c // 2, c % 2
        r = slice(c * 512, (c + 1) * 512)
        m = dict(consts)
        m["xb"] = np.ascontiguousarray(x[b])
        m["x_half"] = np.ascontiguousarray(x[b][hh * 1024 : (hh + 1) * 1024])
        m["wq_s"] = np.ascontiguousarray(w_qkv[:, hh * 2048 : (hh + 1) * 2048])
        m["wk_s"] = np.ascontiguousarray(w_qkv[:, 4096 + hh * 512 : 4096 + (hh + 1) * 512])
        m["wv_s"] = np.ascontiguousarray(w_qkv[:, 5120 + hh * 512 : 5120 + (hh + 1) * 512])
        m["wo_s"] = np.ascontiguousarray(w_o[hh * 2048 : (hh + 1) * 2048, :])
        m["cos_t"] = cos0
        m["sin_t"] = sin0
        m["tril_t"] = tril
        m["qw_s"] = np.ascontiguousarray(np.asarray(z["L0_ffn_qweight"])[:, r, :])
        m["lut4_s"] = np.ascontiguousarray(np.asarray(z["L0_ffn_lut4"], dtype=np.float32)[r])
        m["lut8_s"] = np.ascontiguousarray(np.asarray(z["L0_ffn_lut8"], dtype=np.float32)[r])
        m["w1q_s"] = np.ascontiguousarray(np.asarray(z["L1_ffn_W_q"])[r])
        m["sc_s"] = np.ascontiguousarray(np.asarray(z["L1_ffn_scales"], dtype=np.float32)[r, :, 0])
        m["zr_s"] = np.ascontiguousarray(np.asarray(z["L1_ffn_zeros"], dtype=np.float32)[r, :, 0])
        m["mk_s"] = np.ascontiguousarray(np.asarray(z["L1_ffn_mask"])[r])
        m["s2_s"] = np.ascontiguousarray(np.asarray(z["L1_ffn_scale2"], dtype=np.float32)[r])
        m["shf"] = shf_a
        m["kramp"] = kramp_a
        m["bias0"] = np.asarray(z["L0_ffn_bias"], dtype=np.float32)
        m["bias1"] = np.asarray(z["L1_ffn_bias"], dtype=np.float32)
        m["wqT_s"] = np.ascontiguousarray(WqT[:, hh * 2048 : (hh + 1) * 2048])
        m["wkT"] = WkT
        m["wvT"] = WvT
        m["woT_s"] = np.ascontiguousarray(WoT[hh * 2048 : (hh + 1) * 2048, :])
        m["cos1_t"] = cos1
        m["sin1_t"] = sin1
        m["am_t"] = am
        m["am1_t"] = am1
        m["rotm_t"] = rotm
        maps.append(m)
    return maps


def kernel(**inputs):
    p = build_fused()
    maps = fused_inmaps(inputs, p.const_arrays())
    res = run_bass_kernel_spmd(p.nc, maps, core_ids=list(range(8)))
    out = np.stack([
        np.concatenate(
            [np.asarray(res.results[2 * b]["x4_r"]), np.asarray(res.results[2 * b + 1]["x4_r"])],
            axis=0)
        for b in range(4)])
    return np.ascontiguousarray(out.astype(np.float32))
